# version 08
# baseline (speedup 1.0000x reference)
import numpy as np
import ml_dtypes

import concourse.bass as bass
import concourse.mybir as mybir
from concourse.bass_utils import run_bass_kernel_spmd
from concourse.aot_env import allow_ici_all_to_all
from concourse.ap import AP

F32 = mybir.dt.float32
BF16 = mybir.dt.bfloat16
I32 = mybir.dt.int32
AF = mybir.ActivationFunctionType
ALU = mybir.AluOpType
AX = mybir.AxisListType

HID = 2048
S = 4096
TOK = 1024
HD = 128
SCALING = float(1.0 / np.sqrt(HD))
SHIFT = -40.0
EPS = 1e-5
IFF = 8192
MASKV = -1.0e30
INV2048 = float(np.float32(1.0) / np.float32(2048.0))
INVWELEM = float(np.float32(1.0) / np.float32(16777216.0))
RSQ8192 = float(np.float32(1.0) / np.float32(np.sqrt(np.float32(8192.0))))
EPSRAW = float(np.float32(1e-5) * np.float32(np.sqrt(np.float32(8192.0))))

_DEBUG_TAPS = []
_DEBUG_STOP = [None]


def _rope_tables():
    inv_freq = (1.0 / 10000.0 ** (np.arange(0, HD, 2, dtype=np.float32) / HD)).astype(np.float32)
    t = np.arange(S, dtype=np.float32)
    fr = t[:, None] * inv_freq[None, :]
    return (np.ascontiguousarray(np.cos(fr).astype(np.float32).T),
            np.ascontiguousarray(np.sin(fr).astype(np.float32).T))


def _deperm(n_heads):
    p = []
    for h in range(n_heads):
        base = h * HD
        p.extend(base + np.arange(0, HD, 2))
        p.extend(base + np.arange(1, HD, 2))
    return np.array(p)


def _hadamard(n):
    H = np.array([[1.0]], dtype=np.float32)
    while H.shape[0] < n:
        H = np.block([[H, H], [H, -H]])
    return H.astype(np.float32)


def _diag_masks():
    m = np.zeros((128, 4 * 512), dtype=np.float32)
    col = np.arange(512)
    rows = np.arange(128)[:, None]
    for j in range(4):
        m[:, 512 * j:512 * (j + 1)] = np.where(col[None, :] >= 128 * j + rows, 0.0, MASKV)
    return m


def bc3(ap2d, rep):
    return AP(ap2d.tensor, ap2d.offset, [ap2d.ap[0], [0, rep], ap2d.ap[1]])


class KB:
    ENGINES = ("sync", "tensor", "vector", "scalar", "gpsimd")

    def __init__(self):
        self.nc = bass.Bass("TRN2", target_bir_lowering=False, debug=False)
        self.ops = {e: [] for e in self.ENGINES}
        self.cnt = {}
        self.sems = {}
        from contextlib import ExitStack
        self.root = ExitStack()
        self.open_scopes = []

    def ins(self, eng, fn, sem=None, k=1):
        if sem is None:
            self.ops[eng].append(lambda e, _f=fn: _f(e))
            return None
        self.cnt[sem] += k
        v = self.cnt[sem]

        def run(e, _f=fn, _s=sem, _k=k):
            _f(e).then_inc(self.sems[_s], _k)
        self.ops[eng].append(run)
        return v

    def dma(self, eng, out, in_, sem="ld"):
        return self.ins(eng, lambda e, o=out, i=in_: e.dma_start(out=o, in_=i), sem, 16)

    def stream(self, name, nbuf=2):
        return Stream(self, name, nbuf)

    def wait(self, eng, sem, v=None):
        if v is None:
            v = self.cnt[sem]
        if v <= 0:
            return
        self.ops[eng].append(lambda e, _s=sem, _v=v: e.wait_ge(self.sems[_s], _v))

    def val(self, sem):
        return self.cnt[sem]

    def barrier(self):
        for eng in self.ENGINES:
            for s in sorted(self.cnt):
                self.wait(eng, s)

    def sb(self, name, shape, dt, scope=None):
        st = scope if scope is not None else self.root
        t = st.enter_context(self.nc.sbuf_tensor(name, list(shape), dt))
        return t

    def psm(self, name, shape, dt):
        return self.root.enter_context(self.nc.psum_tensor(name, list(shape), dt))

    def scope(self):
        from contextlib import ExitStack
        s = ExitStack()
        self.open_scopes.append(s)
        return s


class Stream:

    def __init__(self, K, name, nbuf):
        self.K = K
        self.names = []
        for i in range(nbuf):
            n = f"{name}{i}"
            K.sems[n] = K.root.enter_context(K.nc.semaphore(n))
            K.cnt[n] = 0
            self.names.append(n)
        self.nbuf = nbuf

    def dma(self, eng, out, in_, slot=0):
        n = self.names[slot % self.nbuf]
        self.K.ins(eng, lambda e, o=out, i=in_: e.dma_start(out=o, in_=i), n, 16)
        return (n, self.K.cnt[n])

    def wait(self, eng, token):
        self.K.wait(eng, token[0], token[1])

    def wait_all(self, eng):
        for n in self.names:
            self.K.wait(eng, n, self.K.cnt[n])


def r3(t, a, b):
    return t[:, :].rearrange("p (a b) -> p a b", a=a)


def build_program(taps=(), stop=None):
    K = KB()
    nc = K.nc
    B = K

    def dram_in(name, shape, dt=F32):
        return nc.dram_tensor(name, list(shape), dt, kind="ExternalInput").ap()

    x_h = dram_in("x", (S, HID))
    xown_h = dram_in("xown", (TOK, HID))
    wqT_h = dram_in("wqT", (HID, 512))
    wkT_h = dram_in("wkT", (HID, 256))
    wvT_h = dram_in("wvT", (HID, 256))
    woT_h = dram_in("woT", (HID, HID))
    qA_h = dram_in("qA", (8, HID))
    kA_h = dram_in("kA", (8, HID))
    vA_h = dram_in("vA", (8, HID))
    oA_h = dram_in("oA", (8, HID))
    qBT_h = dram_in("qBT", (8, 512))
    kBT_h = dram_in("kBT", (8, 256))
    vBT_h = dram_in("vBT", (8, 256))
    oBT_h = dram_in("oBT", (8, HID))
    wgT_h = dram_in("wgT", (16, 128, 16 * 512))
    wuT_h = dram_in("wuT", (16, 128, 16 * 512))
    wdT_h = dram_in("wdT", (8, 128, 64, 256))
    wgs_h = dram_in("wg_s", (128, 16384))
    wus_h = dram_in("wu_s", (128, 16384))
    wds_h = dram_in("wd_s", (128, 16384))
    identb_h = dram_in("identb", (128, 128), BF16)
    identf_h = dram_in("identf", (128, 128))
    cos_h = dram_in("cosT", (128, S // 2))
    sin_h = dram_in("sinT", (128, S // 2))
    masks_h = dram_in("masks", (128, 2048))
    had_h = dram_in("had", (128, 128), BF16)
    gates_h = dram_in("a2agate", (128, 8))

    y_h = nc.dram_tensor("y", [TOK, HID], F32, kind="ExternalOutput").ap()

    a2a_in = nc.dram_tensor("a2a_in", [8, 512, TOK], BF16)
    a2a_out = nc.dram_tensor("a2a_out", [8, 512, TOK], BF16)
    st_in = nc.dram_tensor("st_in", [8, 1], F32)
    st_out = nc.dram_tensor("st_out", [8, 1], F32)
    x2_sp = nc.dram_tensor("x2_spill", [TOK, HID], F32)
    wd_tern = nc.dram_tensor("wd_tern", [8, 128, 64, 256], BF16)
    inter_sp = nc.dram_tensor("inter_spill", [TOK, IFF], BF16)

    c_identb = K.sb("c_identb", (128, 128), BF16)
    c_identf = K.sb("c_identf", (128, 128), F32)
    c_had = K.sb("c_had", (128, 128), BF16)
    c_onesb = K.sb("c_onesb", (128, 1), BF16)
    c_onesrow = K.sb("c_onesrow", (1, 128), F32)
    c_negrow = K.sb("c_negrow", (1, 256), F32)
    c_gates = K.sb("c_gates", (128, 8), F32)
    st_row = K.sb("st_row", (128, 16), F32)
    c_eps = K.sb("c_eps", (128, 2), F32)
    statacc = K.sb("statacc", (128, 8), F32)
    stat_tp = K.sb("stat_tp", (128, 8), F32)
    strow1 = K.sb("strow1", (1, 8), F32)
    lnstat = K.sb("lnstat", (128, 16), F32)
    lnsum = K.sb("lnsum", (128, 32), F32)
    lnsq = K.sb("lnsq", (128, 32), F32)
    lnm = K.sb("lnm", (128, 32), F32)
    lnr = K.sb("lnr", (128, 32), F32)
    sq_scr = K.sb("sq_scr", (128, HID), F32)
    qscal = K.sb("qscal", (128, 64), F32)
    srow = K.sb("srow", (1, 128), F32)
    sfrows = K.sb("sfrows", (1, 512), F32)

    PA = [K.psm(f"PA{i}", (128, 1024), F32) for i in range(2)]
    PB = [K.psm(f"PB{i}", (128, 1024), BF16) for i in range(2)]
    PC = K.psm("PC", (128, 512), F32)
    PR = K.psm("PR", (1, 512), F32)

    for n in ("ld", "gd", "te", "ve", "se", "st", "cc", "pt", "fz", "qv", "dn",
              "px", "fin"):
        K.sems[n] = K.root.enter_context(nc.semaphore(n))
        K.cnt[n] = 0

    tap_list = []

    def mark_tap(name, handle, shape):
        tap_list.append((nc.dram_tensor(name, list(shape), handle.dtype if hasattr(handle, "dtype") else F32,
                                        kind="ExternalOutput").ap(), handle, shape))

    S_c = B.stream("Sc", 1)
    for dst, src in ((c_identb, identb_h), (c_identf, identf_h), (c_had, had_h),
                     (c_gates, gates_h)):
        tk_consts = S_c.dma("sync", dst[:, :], src)
    B.ins("vector", lambda e: e.memset(c_onesb[:, :], 1.0), "ve")
    B.ins("vector", lambda e: e.memset(statacc[:, :], 0.0), "ve")
    B.ins("vector", lambda e: e.memset(st_row[:, :], 0.0), "ve")
    B.ins("vector", lambda e: e.memset(qscal[:, :], 0.0), "ve")
    B.wait("vector", "ve")
    B.ins("vector", lambda e: e.memset(c_eps[:, 0:1], EPS), "ve")
    B.ins("vector", lambda e: e.memset(c_eps[:, 1:2], SHIFT), "ve")
    B.ins("vector", lambda e: e.memset(c_onesrow[:, :], 1.0), "ve")
    B.ins("vector", lambda e: e.memset(c_negrow[:, :], -1.0), "ve")

    p2scope = K.scope()
    q_fm = K.sb("q_fm", (128, 4 * S), BF16, p2scope)
    k_fm = K.sb("k_fm", (128, 2 * S), BF16, p2scope)
    v_tok = K.sb("v_tok", (128, 2 * 32 * 128), BF16, p2scope)
    p1scope = K.scope()
    weff = K.sb("weff", (128, 16 * 1024), BF16, p1scope)
    c_cos = K.sb("c_cos", (128, S // 2), F32, p1scope)
    c_sin = K.sb("c_sin", (128, S // 2), F32, p1scope)
    xin = [K.sb(f"xin{i}", (128, HID), F32, p1scope) for i in range(2)]
    xn_tok = [K.sb(f"xntok{i}", (128, HID), BF16, p1scope) for i in range(2)]
    xnT = K.sb("xnT", (128, 16 * 512), BF16, p1scope)
    rope_t = K.sb("rope_t", (64, 2 * 512), F32, p1scope)
    v_stg = K.sb("v_stg", (128, 512), BF16, p1scope)
    pre = K.scope()
    wstr = [K.sb(f"wstr{i}", (128, 1024), F32, pre) for i in range(2)]
    lora_a = K.sb("lora_a", (8, HID), F32, pre)
    lora_b = K.sb("lora_b", (8, 512), F32, pre)
    stbuf = xin


    S_st = B.stream("Sst", 2)
    for m, src in enumerate((wgs_h, wus_h, wds_h)):
        for ch in range(8):
            idx = m * 8 + ch
            buf = stbuf[idx % 2]
            if idx >= 2:
                B.wait("gpsimd", "fz", B.val("fz") - 1)
            tk = S_st.dma("gpsimd", buf[:, :], src[:, 2048 * ch:2048 * (ch + 1)], idx)
            S_st.wait("vector", tk)
            if ch == 0:
                B.ins("vector", lambda e, b=buf, mm=m: e.reduce_sum(
                    statacc[:, mm:mm + 1], b[:, :], axis=AX.X,
                    apply_absolute_value=True), "ve")
            else:
                B.ins("vector", lambda e, b=buf: e.reduce_sum(
                    stat_tp[:, 0:1], b[:, :], axis=AX.X,
                    apply_absolute_value=True), "ve")
                B.wait("vector", "ve")
                B.ins("vector", lambda e, mm=m: e.tensor_add(
                    statacc[:, mm:mm + 1], statacc[:, mm:mm + 1], stat_tp[:, 0:1]), "ve")
                B.wait("vector", "ve")
            B.ins("vector", lambda e: e.engine_nop(), "fz")
    B.wait("tensor", "ve", B.val("ve"))
    S_c.wait("tensor", tk_consts)
    B.ins("tensor", lambda e: e.transpose(
        PA[0][0:8, 0:128], statacc[:, 0:8], c_identf[:, :]), "te")
    B.wait("vector", "te", B.val("te"))
    B.ins("vector", lambda e: e.reduce_sum(
        stat_tp[0:8, 0:1], PA[0][0:8, 0:128], axis=AX.X), "ve")
    S_sm = B.stream("Ssm", 1)
    B.wait("gpsimd", "ve", B.val("ve"))
    tk = S_sm.dma("gpsimd", st_in.ap()[0:8, 0:1], stat_tp[0:8, 0:1])
    S_sm.wait("gpsimd", tk)
    B.ins("gpsimd", lambda e: e.collective_compute(
        "AllReduce", ALU.add, replica_groups=[list(range(8))],
        ins=[st_in.ap().opt()], outs=[st_out.ap().opt()]), "cc")
    B.wait("gpsimd", "cc")
    tk_row = S_sm.dma("gpsimd", strow1[0:1, 0:8], st_out.ap()[0:8, 0:1])
    S_sm.wait("tensor", tk_row)
    B.wait("tensor", "ve", B.val("ve"))
    B.ins("tensor", lambda e: e.matmul(
        PA[0][:, 0:8], c_onesrow[0:1, :], strow1[0:1, 0:8], start=True, stop=True), "te")
    B.wait("vector", "te", B.val("te"))
    B.ins("vector", lambda e: e.tensor_copy(st_row[:, 0:8], PA[0][:, 0:8]), "ve")
    B.wait("vector", "ve")
    B.ins("vector", lambda e: e.tensor_scalar(
        st_row[:, 4:7], st_row[:, 0:3], INVWELEM, EPS, ALU.mult, ALU.max), "ve")
    B.wait("vector", "ve")
    B.ins("vector", lambda e: e.tensor_scalar(
        st_row[:, 8:11], st_row[:, 4:7], 0.5, None, ALU.mult), "ve")
    B.wait("vector", "ve")
    B.ins("vector", lambda e: e.tensor_scalar(
        st_row[:, 11:14], st_row[:, 8:11], -1.0, None, ALU.mult), "ve")
    ve_stats = B.val("ve")

    B.wait("tensor", "ve", B.val("ve"))
    S_lo = B.stream("Slo", 1)
    S_wf = B.stream("Swf", 2)
    for (mi, (Ah, BTh, WTh, ncol, obase)) in enumerate((
            (qA_h, qBT_h, wqT_h, 512, 0),
            (kA_h, kBT_h, wkT_h, 256, 512),
            (vA_h, vBT_h, wvT_h, 256, 768))):
        B.wait("sync", "qv", B.val("qv"))
        S_lo.dma("sync", lora_a[:, 0:HID], Ah)
        tk_lo = S_lo.dma("sync", lora_b[:, 0:ncol], BTh)
        S_lo.wait("tensor", tk_lo)
        for ic in range(16):
            buf = wstr[ic % 2]
            if ic >= 2 or mi > 0:
                B.wait("sync", "qv", B.val("qv") - 1)
            tk_w = S_wf.dma("sync", buf[:, 0:ncol], WTh[128 * ic:128 * (ic + 1), :], ic)
            if ic >= 2 or mi > 0:
                B.wait("tensor", "qv", B.val("qv") - 1)
            B.ins("tensor", lambda e, i=ic, n=ncol: e.matmul(
                PA[i % 2][:, 0:n], lora_a[:, 128 * i:128 * (i + 1)],
                lora_b[:, 0:n], start=True, stop=True), "te")
            tev = B.val("te")
            S_wf.wait("vector", tk_w)
            B.wait("vector", "te", tev)
            B.ins("vector", lambda e, b=buf, i=ic, n=ncol, ob=obase: e.scalar_tensor_tensor(
                weff[:, 1024 * i + ob:1024 * i + ob + n], PA[i % 2][:, 0:n], 2.0,
                b[:, 0:n], ALU.mult, ALU.add), "ve")
            B.ins("vector", lambda e: e.engine_nop(), "qv")
    B.barrier()
    pre.close()


    S_cs = B.stream("Scs", 1)
    S_cs.dma("sync", c_cos[:, :], cos_h)
    tk_cs = S_cs.dma("sync", c_sin[:, :], sin_h)
    S_cs.wait("vector", tk_cs)
    B.wait("sync", "ve", ve_stats)
    S_x = B.stream("Sx", 2)

    xuse_marks = [0, 0]
    xuse_se = [0, 0]
    tpmk = [0, 0]
    qv_marks = [0, 0]

    for gt in range(32):
        xb = xin[gt % 2]
        B.wait("sync", "ve", xuse_marks[gt % 2])
        B.wait("sync", "se", xuse_se[gt % 2])
        tk_x = S_x.dma("sync", xb[:, :], x_h[128 * gt:128 * (gt + 1), :], gt)
        S_x.wait("vector", tk_x)
        S_x.wait("scalar", tk_x)
        B.ins("vector", lambda e, b=xb, g=gt: e.reduce_sum(
            lnsum[:, g:g + 1], b[:, :], axis=AX.X), "ve")
        xuse_marks[gt % 2] = B.val("ve")
        B.wait("scalar", "se", B.val("se"))
        B.ins("scalar", lambda e, b=xb, g=gt: e.activation(
            sq_scr[:, :], b[:, :], AF.Square, accum_out=lnsq[:, g:g + 1]), "se")
        xuse_se[gt % 2] = B.val("se")
    B.wait("vector", "se", B.val("se"))
    B.wait("vector", "ve", B.val("ve"))
    B.ins("vector", lambda e: e.tensor_scalar(
        lnm[:, :], lnsum[:, :], INV2048, None, ALU.mult), "ve")
    B.wait("vector", "ve")
    B.ins("vector", lambda e: e.tensor_mul(lnr[:, :], lnm[:, :], lnm[:, :]), "ve")
    B.wait("vector", "ve")
    B.ins("vector", lambda e: e.scalar_tensor_tensor(
        lnr[:, :], lnsq[:, :], INV2048, lnr[:, :], ALU.mult, ALU.subtract), "ve")
    B.wait("scalar", "ve", B.val("ve"))
    B.ins("scalar", lambda e: e.activation(
        lnsq[:, :], lnr[:, :], AF.Sqrt, bias=c_eps[:, 0:1]), "se")
    B.wait("vector", "se", B.val("se"))
    B.ins("vector", lambda e: e.reciprocal(lnr[:, :], lnsq[:, :]), "ve")
    B.wait("vector", "ve")
    ve_lnchain = B.val("ve")

    for tg in range(8):
        for tl in range(4):
            gt = tg * 4 + tl
            xb = xin[gt % 2]
            B.wait("sync", "ve", xuse_marks[gt % 2])
            B.wait("sync", "se", xuse_se[gt % 2])
            tk_x = S_x.dma("sync", xb[:, :], x_h[128 * gt:128 * (gt + 1), :], gt)
            S_x.wait("vector", tk_x)
            B.ins("vector", lambda e, b=xb, g=gt, t=gt: e.tensor_scalar(
                xn_tok[t % 2][:, :], b[:, :], lnm[:, g:g + 1],
                lnr[:, g:g + 1], ALU.subtract, ALU.mult), "ve")
            xuse_marks[gt % 2] = B.val("ve")
            vev = B.val("ve")
            B.wait("tensor", "ve", vev)
            for bb in range(2):
                B.wait("tensor", "ve", tpmk[bb])
                for j in range(8):
                    ic = bb * 8 + j
                    B.ins("tensor", lambda e, t=gt, i=ic, b=bb, j_=j: e.transpose(
                        PB[b][:, 128 * j_:128 * (j_ + 1)],
                        xn_tok[t % 2][:, 128 * i:128 * (i + 1)], c_identb[:, :]), "te")
                tev = B.val("te")
                B.wait("vector", "te", tev)
                B.ins("vector", lambda e, tl_=tl, b=bb: e.tensor_copy(
                    r3(xnT, 16, 512)[:, 8 * b:8 * (b + 1), 128 * tl_:128 * (tl_ + 1)],
                    r3(PB[b], 8, 128)[:, :, :]), "ve")
                tpmk[bb] = B.val("ve")
        ve_xnT = B.val("ve")

        B.wait("tensor", "ve", ve_xnT)
        tglob = tg * 512
        for oc in range(8):
            unit = tg * 8 + oc
            pp = PA[unit % 2]
            B.wait("tensor", "qv", qv_marks[unit % 2])
            for ic in range(16):
                B.ins("tensor", lambda e, i=ic, o=oc, pb=pp: e.matmul(
                    pb[:, 0:512],
                    weff[:, 1024 * i + 128 * o:1024 * i + 128 * (o + 1)],
                    r3(xnT, 16, 512)[:, i, :],
                    start=(i == 0), stop=(i == 15)), "te")
            tev = B.val("te")
            B.wait("vector", "te", tev)
            if oc < 6:
                dst = q_fm if oc < 4 else k_fm
                hh = oc if oc < 4 else oc - 4
                pbs = 64 * (tg // 4)
                co = 512 * (tg % 4)
                cslice = c_cos[pbs:pbs + 64, co:co + 512]
                sslice = c_sin[pbs:pbs + 64, co:co + 512]
                B.ins("vector", lambda e, pb=pp, c=cslice: e.tensor_mul(
                    rope_t[:, 0:512], pb[0:64, 0:512], c), "ve")
                B.ins("vector", lambda e, pb=pp, s_=sslice: e.tensor_mul(
                    rope_t[:, 512:1024], pb[64:128, 0:512], s_), "ve")
                B.wait("vector", "ve")
                B.ins("vector", lambda e, d=dst, h=hh, t0=tglob: e.tensor_sub(
                    d[0:64, h * S + t0:h * S + t0 + 512],
                    rope_t[:, 0:512], rope_t[:, 512:1024]), "ve")
                B.wait("vector", "ve")
                B.ins("vector", lambda e, pb=pp, s_=sslice: e.tensor_mul(
                    rope_t[:, 0:512], pb[0:64, 0:512], s_), "ve")
                B.ins("vector", lambda e, pb=pp, c=cslice: e.tensor_mul(
                    rope_t[:, 512:1024], pb[64:128, 0:512], c), "ve")
                B.wait("vector", "ve")
                B.ins("vector", lambda e, d=dst, h=hh, t0=tglob: e.tensor_add(
                    d[64:128, h * S + t0:h * S + t0 + 512],
                    rope_t[:, 0:512], rope_t[:, 512:1024]), "ve")
                B.wait("vector", "ve")
                B.ins("vector", lambda e: e.engine_nop(), "qv")
                qv_marks[unit % 2] = B.val("qv")
            else:
                kvh = oc - 6
                B.ins("vector", lambda e, pb=pp: e.tensor_copy(
                    v_stg[:, :], pb[:, 0:512]), "ve")
                B.ins("vector", lambda e: e.engine_nop(), "qv")
                qv_marks[unit % 2] = B.val("qv")
                vev = B.val("ve")
                B.wait("tensor", "ve", vev)
                B.wait("tensor", "ve", tpmk[0])
                for j in range(4):
                    B.ins("tensor", lambda e, j_=j: e.transpose(
                        PB[0][:, 128 * j_:128 * (j_ + 1)],
                        v_stg[:, 128 * j_:128 * (j_ + 1)], c_identb[:, :]), "te")
                tev = B.val("te")
                B.wait("vector", "te", tev)
                ktb = tglob // 128
                B.ins("vector", lambda e, kv=kvh, kb=ktb: e.tensor_copy(
                    v_tok[:, :].rearrange("p (kv kt d) -> p kv kt d", kv=2, kt=32)[
                        :, kv, kb:kb + 4, :],
                    r3(PB[0], 8, 128)[:, 0:4, :]), "ve")
                tpmk[0] = B.val("ve")

    B.barrier()
    p1scope.close()
    if stop == "p1":
        return _finish(K, tap_list, taps, locals())

    p2b = K.scope()
    wdin = [K.sb(f"wdin{i}", (128, HID), F32, p2b) for i in range(2)]
    wdq = [K.sb(f"wdq{i}", (128, HID), BF16, p2b) for i in range(2)]
    c_masks = K.sb("c_masks", (128, 2048), F32, p2b)
    pt_bufs = [K.sb(f"ptbuf{i}", (128, 1024), BF16, p2b) for i in range(2)]
    an_f32 = K.sb("an_f32", (128, 512), F32, p2b)
    rs_sb = K.sb("rs_sb", (1, 512), F32, p2b)

    S_mk = B.stream("Smk", 1)
    tk_mk = S_mk.dma("sync", c_masks[:, :], masks_h)

    B.wait("vector", "ve", ve_stats)
    S_wd = B.stream("Swd", 2)
    S_do = B.stream("Sdo", 2)
    wd_marks = [None, None]
    wd_tk = {}

    def emit_p2a_load(blk):
        if wd_marks[blk % 2] is not None:
            S_do.wait("gpsimd", wd_marks[blk % 2])
        ocb0 = blk // 8
        ic0 = (blk % 8) * 8
        wd_tk[blk] = S_wd.dma("gpsimd", wdin[blk % 2][:, :],
                              wdT_h[ocb0, :, ic0:ic0 + 8, :], blk)

    def emit_p2a_tern(blk):
        buf = wdin[blk % 2]
        qb = wdq[blk % 2]
        ocb0 = blk // 8
        ic0 = (blk % 8) * 8
        S_wd.wait("vector", wd_tk[blk])
        B.ins("vector", lambda e, b=buf: e.tensor_scalar(
            sq_scr[:, :], b[:, :], st_row[:, 10:11], None, ALU.is_ge), "ve")
        B.wait("vector", "ve")
        B.ins("vector", lambda e, b=buf, q=qb: e.scalar_tensor_tensor(
            q[:, :], b[:, :], st_row[:, 13:14], sq_scr[:, :],
            ALU.is_gt, ALU.add), "ve")
        vev = B.val("ve")
        B.wait("vector", "ve", vev)
        B.wait("gpsimd", "ve", vev)
        wd_marks[blk % 2] = S_do.dma(
            "gpsimd", wd_tern.ap()[ocb0, :, ic0:ic0 + 8, :], qb[:, :], blk)

    S_mk.wait("vector", tk_mk)
    S_mk.wait("scalar", tk_mk)
    S_a2 = B.stream("Sa2", 2)
    an_outs = [K.sb(f"anout{i}", (128, 512), BF16, p2b) for i in range(2)]
    an_tok = [None, None]

    se_marks = [0, 0]
    pt_marks = [0, 0]
    pa0_guard = [0]
    qv_fin = [0]
    sched = []
    for h in range(4):
        for qc in range(8):
            ngrp = 2 * (qc + 1)
            for grp in range(ngrp):
                sched.append((h, qc, grp, ngrp))

    def emit_av(idx):
        h, qc, grp, ngrp = sched[idx]
        kv = h // 2
        g = idx
        first = grp == 0
        last = grp == ngrp - 1
        if first:
            B.wait("tensor", "qv", qv_fin[0])
        for t2 in range(2):
            kt = 2 * grp + t2
            s0 = first and t2 == 0
            s1 = last and t2 == 1
            B.ins("tensor", lambda e, g_=g, k=kt, t_=t2, kv_=kv, a=s0, b=s1: e.matmul(
                PC[:, 0:512],
                v_tok[:, :].rearrange("p (kv kt d) -> p kv kt d", kv=2, kt=32)[
                    :, kv_, k, :],
                pt_bufs[g_ % 2][:, 512 * t_:512 * (t_ + 1)],
                start=a, stop=b), "te")
            B.ins("tensor", lambda e, g_=g, t_=t2, a=s0, b=s1: e.matmul(
                PR[0:1, 0:512], c_onesb[:, 0:1],
                pt_bufs[g_ % 2][:, 512 * t_:512 * (t_ + 1)],
                start=a, stop=b), "te")
        B.ins("tensor", lambda e: e.nop(), "pt")
        pt_marks[g % 2] = B.val("pt")
        if last:
            tev = B.val("te")
            B.wait("vector", "te", tev)
            B.ins("vector", lambda e: e.reciprocal(
                rs_sb[0:1, 0:512], PR[0:1, 0:512]), "ve")
            B.wait("vector", "ve")
            B.ins("vector", lambda e: e.tensor_copy(an_f32[:, :], PC[:, 0:512]), "ve")
            B.ins("vector", lambda e: e.engine_nop(), "qv")
            qv_fin[0] = B.val("qv")
            vev = B.val("ve")
            B.wait("tensor", "ve", vev)
            B.wait("tensor", "se", B.val("se"))
            B.ins("tensor", lambda e: e.matmul(
                PA[0][:, 0:512], c_onesrow[0:1, :], rs_sb[0:1, 0:512],
                start=True, stop=True), "te")
            tev = B.val("te")
            B.wait("vector", "te", tev)
            hq = h * 8 + qc
            if an_tok[hq % 2] is not None:
                S_a2.wait("vector", an_tok[hq % 2])
            B.ins("vector", lambda e, p_=hq % 2: e.tensor_mul(
                an_outs[p_][:, :], an_f32[:, :], PA[0][:, 0:512]), "ve")
            pa0_guard[0] = B.val("ve")
            vev = B.val("ve")
            B.wait("sync", "ve", vev)
            sl = qc // 2
            co = 512 * (qc % 2)
            S_a2.dma("sync", a2a_in.ap()[sl, 128 * h:128 * (h + 1), co:co + 512],
                     an_outs[hq % 2][:, :], hq)
            an_tok[hq % 2] = S_a2.dma(
                "sync", a2a_in.ap()[sl + 4, 128 * h:128 * (h + 1), co:co + 512],
                an_outs[hq % 2][:, :], hq)

    emit_p2a_load(0)
    emit_p2a_load(1)
    for gidx, (h, qc, grp, ngrp) in enumerate(sched):
        kv = h // 2
        sp = PA[gidx % 2]
        B.wait("tensor", "se", se_marks[gidx % 2])
        if gidx % 2 == 0 and pa0_guard[0]:
            B.wait("tensor", "ve", pa0_guard[0])
            pa0_guard[0] = 0
        for t2 in range(2):
            kt = 2 * grp + t2
            B.ins("tensor", lambda e, s_=sp, k=kt, t_=t2, h_=h, q_=qc, kv_=kv: e.matmul(
                s_[:, 512 * t_:512 * (t_ + 1)],
                k_fm[:, kv_ * S + 128 * k:kv_ * S + 128 * (k + 1)],
                q_fm[:, h_ * S + 512 * q_:h_ * S + 512 * (q_ + 1)],
                start=True, stop=True), "te")
        tev = B.val("te")
        if grp >= ngrp - 2:
            j2 = grp - (ngrp - 2)
            B.wait("vector", "te", tev)
            B.ins("vector", lambda e, s_=sp, j_=j2: e.tensor_add(
                s_[:, 0:1024], s_[:, 0:1024],
                c_masks[:, 1024 * j_:1024 * (j_ + 1)]), "ve")
            B.wait("scalar", "ve", B.val("ve"))
        else:
            B.wait("scalar", "te", tev)
        B.wait("scalar", "pt", pt_marks[gidx % 2])
        B.ins("scalar", lambda e, s_=sp, g=gidx: e.activation(
            pt_bufs[g % 2][:, 0:1024], s_[:, 0:1024], AF.Exp,
            bias=c_eps[:, 1:2], scale=SCALING), "se")
        se_marks[gidx % 2] = B.val("se")
        if gidx >= 1:
            B.wait("tensor", "se", B.val("se") - 1)
            emit_av(gidx - 1)
        if gidx % 4 == 0 and gidx > 0 and gidx // 4 + 1 < 64:
            emit_p2a_load(gidx // 4 + 1)
        if gidx % 4 == 2 and gidx // 4 < 64:
            emit_p2a_tern(gidx // 4)
    B.wait("tensor", "se", B.val("se"))
    emit_av(len(sched) - 1)

    B.barrier()
    p2b.close()
    p2scope.close()
    if stop == "p2":
        return _finish(K, tap_list, taps, locals())

    p45scope = K.scope()
    xq2T = K.sb("xq2T", (128, 16 * TOK), BF16, p45scope)
    p34scope = K.scope()
    x2_sb = K.sb("x2_sb", (128, 8 * HID), F32, p34scope)
    p3scope = K.scope()
    av_fm = K.sb("av_fm", (128, 16 * TOK), BF16, p3scope)
    a2a_ld = [K.sb(f"a2ald{i}", (128, 4 * TOK), BF16, p3scope) for i in range(2)]
    wo_str = [K.sb(f"wostr{i}", (128, 16 * 256), F32, p3scope) for i in range(2)]
    wo_eff = K.sb("wo_eff", (128, 16 * 256), BF16, p3scope)
    lora_o = K.sb("lora_o", (8, HID), F32, p3scope)
    lora_ob = K.sb("lora_ob", (8, 256), F32, p3scope)

    S_lo2 = B.stream("Slo2", 1)
    tk_oa = S_lo2.dma("sync", lora_o[:, 0:HID], oA_h)
    S_a2.wait_all("gpsimd")
    def _a2a(e):
        with allow_ici_all_to_all():
            return e.collective_compute(
                "AllToAll", ALU.bypass, replica_groups=[list(range(8))],
                ins=[a2a_in.ap().opt()], outs=[a2a_out.ap().opt()])
    B.ins("gpsimd", _a2a, "cc")
    B.wait("sync", "cc", B.val("cc"))
    S_al = B.stream("Sal", 1)
    for i in range(4):
        if i >= 1:
            B.wait("sync", "ve", B.val("ve"))
        for hc4 in range(4):
            S_al.dma("sync", a2a_ld[0][:, 1024 * hc4:1024 * (hc4 + 1)],
                     a2a_out.ap()[i, 128 * hc4:128 * (hc4 + 1), :])
            tk_al = S_al.dma("sync", a2a_ld[1][:, 1024 * hc4:1024 * (hc4 + 1)],
                             a2a_out.ap()[i + 4, 128 * hc4:128 * (hc4 + 1), :])
        S_al.wait("vector", tk_al)
        B.ins("vector", lambda e, i_=i: e.tensor_scalar(
            a2a_ld[1][:, :], a2a_ld[1][:, :], c_gates[:, i_ + 4:i_ + 5], None,
            ALU.mult), "ve")
        B.wait("vector", "ve")
        B.ins("vector", lambda e, i_=i: e.scalar_tensor_tensor(
            av_fm[:, 4096 * i_:4096 * (i_ + 1)], a2a_ld[0][:, :],
            c_gates[:, i_:i_ + 1], a2a_ld[1][:, :], ALU.mult, ALU.add), "ve")
    ve_av = B.val("ve")

    S_xi = B.stream("Sxi", 1)
    for tt in range(8):
        tk_x2i = S_xi.dma("sync", x2_sb[:, 2048 * tt:2048 * (tt + 1)],
                          xown_h[128 * tt:128 * (tt + 1), :])

    B.wait("tensor", "ve", ve_av)
    S_wb = B.stream("Swb", 2)
    wo_dn_marks = [0, 0]
    for ocb in range(8):
        wsb = wo_str[ocb % 2]
        if ocb >= 2:
            B.wait("sync", "qv", B.val("qv") - 1)
        B.wait("sync", "fz", B.val("fz"))
        S_wb.dma("sync", lora_ob[:, 0:256], oBT_h[0:8, 256 * ocb:256 * (ocb + 1)], ocb)
        for hc in range(16):
            tk_wo = S_wb.dma("sync", wsb[:, 256 * hc:256 * (hc + 1)],
                             woT_h[128 * hc:128 * (hc + 1), 256 * ocb:256 * (ocb + 1)], ocb)
        B.wait("tensor", "dn", B.val("dn"))
        S_wb.wait("tensor", tk_wo)
        if ocb == 0:
            S_lo2.wait("tensor", tk_oa)
        for hc in range(16):
            if not (ocb == 0 and hc < 2):
                B.wait("tensor", "fz", B.val("fz") - 1)
            B.ins("tensor", lambda e, h_=hc: e.matmul(
                PA[h_ % 2][:, 0:256], lora_o[:, 128 * h_:128 * (h_ + 1)],
                lora_ob[:, 0:256], start=True, stop=True), "te")
            tev = B.val("te")
            B.wait("vector", "te", tev)
            B.ins("vector", lambda e, h_=hc, w=wsb: e.scalar_tensor_tensor(
                wo_eff[:, 256 * h_:256 * (h_ + 1)], PA[h_ % 2][:, 0:256], 2.0,
                w[:, 256 * h_:256 * (h_ + 1)], ALU.mult, ALU.add), "ve")
            B.ins("vector", lambda e: e.engine_nop(), "fz")
        ve_woe = B.val("ve")
        B.wait("tensor", "ve", ve_woe)
        B.ins("vector", lambda e: e.engine_nop(), "qv")
        for tt in range(8):
            pp = PA[tt % 2]
            B.wait("tensor", "dn", wo_dn_marks[tt % 2])
            for hc in range(16):
                B.ins("tensor", lambda e, h_=hc, t=tt, pb=pp: e.matmul(
                    pb[:, 0:256],
                    av_fm[:, 1024 * h_ + 128 * t:1024 * h_ + 128 * (t + 1)],
                    wo_eff[:, 256 * h_:256 * (h_ + 1)],
                    start=(h_ == 0), stop=(h_ == 15)), "te")
            tev = B.val("te")
            B.wait("vector", "te", tev)
            if ocb == 0 and tt == 0:
                S_xi.wait("vector", tk_x2i)
            B.ins("vector", lambda e, t=tt, o=ocb, pb=pp: e.tensor_add(
                x2_sb[:, 2048 * t + 256 * o:2048 * t + 256 * (o + 1)],
                x2_sb[:, 2048 * t + 256 * o:2048 * t + 256 * (o + 1)],
                pb[:, 0:256]), "ve")
            B.ins("vector", lambda e: e.engine_nop(), "dn")
            wo_dn_marks[tt % 2] = B.val("dn")
    ve_x2 = B.val("ve")
    S_xs = B.stream("Sxs", 1)
    B.wait("gpsimd", "ve", ve_x2)
    for tt in range(8):
        S_xs.dma("gpsimd", x2_sp.ap()[128 * tt:128 * (tt + 1), :],
                 x2_sb[:, 2048 * tt:2048 * (tt + 1)])

    B.barrier()
    p3scope.close()
    if stop == "p3":
        return _finish(K, tap_list, taps, locals())

    p4scope = K.scope()
    xq_f32 = K.sb("xq_f32", (128, HID), F32, p4scope)
    xq_i32 = K.sb("xq_i32", (128, HID), I32, p4scope)
    xq_tok = K.sb("xq_tok", (128, HID), BF16, p4scope)

    for tt in range(8):
        xsl = x2_sb[:, 2048 * tt:2048 * (tt + 1)]
        lc = (tt % 2) * 8
        B.ins("vector", lambda e, b=xsl, c=lc: e.reduce_sum(
            lnstat[:, c + 0:c + 1], b, axis=AX.X), "ve")
        B.ins("scalar", lambda e, b=xsl, c=lc: e.activation(
            sq_scr[:, :], b, AF.Square, accum_out=lnstat[:, c + 1:c + 2]), "se")
        B.wait("vector", "se", B.val("se"))
        B.wait("vector", "ve", B.val("ve"))
        B.ins("vector", lambda e, c=lc: e.tensor_scalar(
            lnstat[:, c + 2:c + 3], lnstat[:, c + 0:c + 1], INV2048, None, ALU.mult), "ve")
        B.wait("vector", "ve")
        B.ins("vector", lambda e, c=lc: e.tensor_mul(
            lnstat[:, c + 3:c + 4], lnstat[:, c + 2:c + 3], lnstat[:, c + 2:c + 3]), "ve")
        B.wait("vector", "ve")
        B.ins("vector", lambda e, c=lc: e.scalar_tensor_tensor(
            lnstat[:, c + 4:c + 5], lnstat[:, c + 1:c + 2], INV2048,
            lnstat[:, c + 3:c + 4], ALU.mult, ALU.subtract), "ve")
        B.wait("scalar", "ve", B.val("ve"))
        B.ins("scalar", lambda e, c=lc: e.activation(
            lnstat[:, c + 5:c + 6], lnstat[:, c + 4:c + 5], AF.Sqrt, bias=c_eps[:, 0:1]), "se")
        B.wait("vector", "se", B.val("se"))
        B.ins("vector", lambda e, c=lc: e.reciprocal(
            lnstat[:, c + 6:c + 7], lnstat[:, c + 5:c + 6]), "ve")
        B.wait("vector", "ve")
        B.ins("vector", lambda e, b=xsl, c=lc: e.tensor_scalar(
            xq_f32[:, :], b, lnstat[:, c + 2:c + 3], lnstat[:, c + 6:c + 7],
            ALU.subtract, ALU.mult), "ve")
        B.wait("vector", "ve")
        B.ins("vector", lambda e, t=tt: e.reduce_max(
            qscal[:, 8 * t + 6:8 * t + 7], xq_f32[:, :], axis=AX.X,
            apply_absolute_value=True), "ve")
        B.wait("vector", "ve")
        B.ins("vector", lambda e, t=tt: e.tensor_scalar(
            qscal[:, 8 * t + 0:8 * t + 1], qscal[:, 8 * t + 6:8 * t + 7],
            EPS, None, ALU.max), "ve")
        B.wait("vector", "ve")
        B.ins("vector", lambda e, t=tt: e.reciprocal(
            qscal[:, 8 * t + 6:8 * t + 7], qscal[:, 8 * t + 0:8 * t + 1]), "ve")
        B.wait("vector", "ve")
        B.ins("vector", lambda e, t=tt: e.tensor_scalar(
            qscal[:, 8 * t + 1:8 * t + 2], qscal[:, 8 * t + 6:8 * t + 7],
            128.0, None, ALU.mult), "ve")
        B.wait("vector", "ve")
        B.ins("vector", lambda e, t=tt: e.tensor_scalar(
            xq_f32[:, :], xq_f32[:, :], qscal[:, 8 * t + 1:8 * t + 2], None,
            ALU.mult), "ve")
        B.wait("vector", "ve")
        B.ins("vector", lambda e: e.tensor_copy(xq_i32[:, :], xq_f32[:, :]), "ve")
        B.wait("vector", "ve")
        B.ins("vector", lambda e: e.tensor_copy(xq_tok[:, :], xq_i32[:, :]), "ve")
        B.wait("vector", "ve")
        B.ins("vector", lambda e, t=tt: e.tensor_scalar(
            xq_tok[:, :], xq_tok[:, :], 127.0, None, ALU.min, ALU.add,
            accum_out=qscal[:, 8 * t + 7:8 * t + 8]), "ve")
        B.wait("vector", "ve")
        B.ins("vector", lambda e, t=tt: e.tensor_scalar(
            qscal[:, 8 * t + 6:8 * t + 7], qscal[:, 8 * t + 0:8 * t + 1],
            1.0 / 128.0, None, ALU.mult), "ve")
        B.wait("vector", "ve")
        B.ins("vector", lambda e, t=tt: e.tensor_mul(
            qscal[:, 8 * t + 2:8 * t + 3], qscal[:, 8 * t + 6:8 * t + 7],
            st_row[:, 4:5]), "ve")
        B.ins("vector", lambda e, t=tt: e.tensor_mul(
            qscal[:, 8 * t + 4:8 * t + 5], qscal[:, 8 * t + 6:8 * t + 7],
            st_row[:, 5:6]), "ve")
        B.wait("vector", "ve")
        B.ins("vector", lambda e, t=tt: e.scalar_tensor_tensor(
            qscal[:, 8 * t + 3:8 * t + 4], qscal[:, 8 * t + 2:8 * t + 3], -1.0,
            qscal[:, 8 * t + 7:8 * t + 8], ALU.mult, ALU.mult), "ve")
        B.ins("vector", lambda e, t=tt: e.scalar_tensor_tensor(
            qscal[:, 8 * t + 5:8 * t + 6], qscal[:, 8 * t + 4:8 * t + 5], -1.0,
            qscal[:, 8 * t + 7:8 * t + 8], ALU.mult, ALU.mult), "ve")
        vev = B.val("ve")
        B.wait("tensor", "ve", vev)
        for bb in range(2):
            for j in range(8):
                ic = bb * 8 + j
                B.ins("tensor", lambda e, i=ic, b=bb, j_=j: e.transpose(
                    PB[b][:, 128 * j_:128 * (j_ + 1)],
                    xq_tok[:, 128 * i:128 * (i + 1)], c_identb[:, :]), "te")
            tev = B.val("te")
            B.wait("vector", "te", tev)
            B.ins("vector", lambda e, t=tt, b=bb: e.tensor_copy(
                r3(xq2T, 16, TOK)[:, 8 * b:8 * (b + 1), 128 * t:128 * (t + 1)],
                r3(PB[b], 8, 128)[:, :, :]), "ve")
            vev = B.val("ve")
            B.wait("tensor", "ve", vev)

    B.barrier()
    p4scope.close()
    p34scope.close()
    if stop == "p4":
        return _finish(K, tap_list, taps, locals())

    p5scope = K.scope()
    gu_str = [K.sb(f"gustr{i}", (128, 16 * 512), F32, p5scope) for i in range(2)]
    gu_q = [K.sb(f"guq{i}", (128, 16 * 512), BF16, p5scope) for i in range(2)]
    silu_all = K.sb("silu_all", (128, 8 * 512), F32, p5scope)
    up_sb2 = [K.sb(f"upsb{i}", (128, 512), F32, p5scope) for i in range(2)]
    int_sb = [K.sb(f"intsb{i}", (128, 512), BF16, p5scope) for i in range(2)]

    S_gu = B.stream("Sgu", 2)
    S_is = B.stream("Sis", 2)
    p5qv_marks = [None, None]
    int_marks = [None, None]
    silu_rd = [0] * 8
    tern_done = {}

    def blk_src(b):
        return (wgT_h if b % 2 == 0 else wuT_h), b // 2

    tern_rd = {}
    gu_te_marks = [0, 0]

    def emit_load(b):
        wsrc, ocb = blk_src(b)
        eng = "sync" if b % 2 == 0 else "scalar"
        B.wait(eng, "dn", tern_rd.get(b - 2, 0))
        return S_gu.dma(eng, gu_str[b % 2][:, :], wsrc[ocb, :, :], b)

    def emit_tern(b):
        tc, nc_ = (8, 11) if b % 2 == 0 else (9, 12)
        wsb, wqb = gu_str[b % 2], gu_q[b % 2]
        S_gu.wait("vector", tern_done[b])
        B.wait("vector", "te", gu_te_marks[b % 2])
        B.ins("vector", lambda e, w=wsb, t_=tc: e.tensor_scalar(
            sq_scr[:, 0:2048], w[:, 0:2048], st_row[:, t_:t_ + 1], None,
            ALU.is_ge), "ve")
        B.ins("vector", lambda e, w=wsb, t_=tc: e.tensor_scalar(
            xq_f32g[:, 0:2048], w[:, 2048:4096], st_row[:, t_:t_ + 1], None,
            ALU.is_ge), "ve")
        B.wait("vector", "ve")
        B.ins("vector", lambda e, w=wsb, q=wqb, n_=nc_: e.scalar_tensor_tensor(
            q[:, 0:2048], w[:, 0:2048], st_row[:, n_:n_ + 1],
            sq_scr[:, 0:2048], ALU.is_gt, ALU.add), "ve")
        B.ins("vector", lambda e, w=wsb, q=wqb, n_=nc_: e.scalar_tensor_tensor(
            q[:, 2048:4096], w[:, 2048:4096], st_row[:, n_:n_ + 1],
            xq_f32g[:, 0:2048], ALU.is_gt, ALU.add), "ve")
        B.wait("vector", "ve")
        B.ins("vector", lambda e, w=wsb, t_=tc: e.tensor_scalar(
            sq_scr[:, 0:2048], w[:, 4096:6144], st_row[:, t_:t_ + 1], None,
            ALU.is_ge), "ve")
        B.ins("vector", lambda e, w=wsb, t_=tc: e.tensor_scalar(
            xq_f32g[:, 0:2048], w[:, 6144:8192], st_row[:, t_:t_ + 1], None,
            ALU.is_ge), "ve")
        B.wait("vector", "ve")
        B.ins("vector", lambda e, w=wsb, q=wqb, n_=nc_: e.scalar_tensor_tensor(
            q[:, 4096:6144], w[:, 4096:6144], st_row[:, n_:n_ + 1],
            sq_scr[:, 0:2048], ALU.is_gt, ALU.add), "ve")
        B.ins("vector", lambda e, w=wsb, q=wqb, n_=nc_: e.scalar_tensor_tensor(
            q[:, 6144:8192], w[:, 6144:8192], st_row[:, n_:n_ + 1],
            xq_f32g[:, 0:2048], ALU.is_gt, ALU.add), "ve")
        B.wait("vector", "ve")
        B.ins("vector", lambda e: e.engine_nop(), "dn")
        tern_rd[b] = B.val("dn")
        return B.val("ve")

    xq_f32g = K.sb("xq_f32g", (128, 2048), F32, p5scope)

    tern_done[0] = emit_load(0)
    tern_done[1] = emit_load(1)
    tern_ve = {}
    tern_ve[0] = emit_tern(0)
    for b in range(32):
        gate = b % 2 == 0
        wqb = gu_q[b % 2]
        if b + 2 < 32:
            tern_done[b + 2] = emit_load(b + 2)
        if b + 1 < 32:
            tern_ve[b + 1] = emit_tern(b + 1)
        B.wait("tensor", "ve", tern_ve[b])
        for tt in range(8):
            po = 512 * (tt % 2)
            PAx = PA[0] if gate else PA[1]
            if p5qv_marks[tt % 2] is not None:
                B.wait("tensor", *p5qv_marks[tt % 2])
            for hc in range(16):
                B.ins("tensor", lambda e, h_=hc, t=tt, q=wqb, o=po, pb=PAx: e.matmul(
                    pb[:, o:o + 512],
                    r3(xq2T, 16, TOK)[:, h_, 128 * t:128 * (t + 1)],
                    q[:, 512 * h_:512 * (h_ + 1)],
                    start=(h_ == 0), stop=(h_ == 15)), "te")
            tev = B.val("te")
            if gate:
                B.wait("scalar", "te", tev)
                B.wait("scalar", "ve", silu_rd[tt])
                B.ins("scalar", lambda e, t=tt, o=po: e.activation(
                    silu_all[:, 512 * t:512 * (t + 1)], PA[0][:, o:o + 512], AF.Silu,
                    bias=qscal[:, 8 * t + 3:8 * t + 4],
                    scale=qscal[:, 8 * t + 2:8 * t + 3]), "se")
                p5qv_marks[tt % 2] = ("se", B.val("se"))
            else:
                B.wait("vector", "te", tev)
                B.ins("vector", lambda e, t=tt, o=po: e.tensor_scalar(
                    up_sb2[t % 2][:, :], PA[1][:, o:o + 512],
                    qscal[:, 8 * t + 4:8 * t + 5],
                    qscal[:, 8 * t + 5:8 * t + 6], ALU.mult, ALU.add), "ve")
                p5qv_marks[tt % 2] = ("ve", B.val("ve"))
                B.wait("vector", "ve")
                B.wait("vector", "se", B.val("se"))
                if int_marks[tt % 2] is not None:
                    S_is.wait("vector", int_marks[tt % 2])
                B.ins("vector", lambda e, t=tt: e.tensor_mul(
                    int_sb[t % 2][:, :], silu_all[:, 512 * t:512 * (t + 1)],
                    up_sb2[t % 2][:, :]), "ve")
                silu_rd[tt] = B.val("ve")
                vev = B.val("ve")
                B.wait("sync", "ve", vev)
                int_marks[tt % 2] = S_is.dma(
                    "sync", inter_sp.ap()[128 * tt:128 * (tt + 1),
                                          512 * (b // 2):512 * (b // 2 + 1)],
                    int_sb[tt % 2][:, :], tt)
        gu_te_marks[b % 2] = B.val("te")

    B.barrier()
    p5scope.close()
    p45scope.close()
    if stop == "p5":
        return _finish(K, tap_list, taps, locals())

    p6scope = K.scope()
    intr_ld = K.sb("intr_ld", (128, IFF), BF16, p6scope)
    itT = K.sb("itT", (128, 64 * 128), BF16, p6scope)
    fwX = [K.sb(f"fwx{i}", (128, 64 * 128), BF16, p6scope) for i in range(2)]
    fw_scr = K.sb("fw_scr", (128, 64 * 128), BF16, p6scope)
    f32scr = K.sb("f32scr", (128, 512), F32, p6scope)
    i32scr = K.sb("i32scr", (128, 512), I32, p6scope)
    m2st = K.sb("m2st", (128, 32), F32, p6scope)
    xqf2 = [K.sb(f"xqf{i}", (128, 64 * 256), BF16, p6scope) for i in range(2)]
    wd_str = [K.sb(f"wdstr{i}", (128, 16 * 256), BF16, p6scope) for i in range(3)]
    ysg = K.sb("ysg", (128, 2 * HID), F32, p6scope)

    S_is.wait_all("gpsimd")
    S_il = B.stream("Sil", 1)
    S_wl = B.stream("Swl", 3)
    S_yi = B.stream("Syi", 1)
    tp_mark = [0]
    pbm = [None, None]
    px_mark = [0]
    s1m = [None, None]
    p6dn_marks = [0, 0]
    cmark = [0, 0]
    qdone = [None, None]
    afin_mark = [0, 0]
    pr_mark = [0]
    tail_te = [0]
    ysg_mark = [0]
    B.wait("tensor", "ve", B.val("ve"))

    def emit_AH(t):
        B.wait("gpsimd", "te", tp_mark[0])
        tk_il = S_il.dma("gpsimd", intr_ld[:, :],
                         inter_sp.ap()[128 * t:128 * (t + 1), :])
        S_il.wait("tensor", tk_il)
        evse = []
        for bb in range(8):
            if pbm[bb % 2] is not None:
                B.wait("tensor", *pbm[bb % 2])
            if bb == 0:
                B.wait("tensor", "px", px_mark[0])
            for j in range(8):
                ic = bb * 8 + j
                B.ins("tensor", lambda e, i=ic, pb=bb % 2, j_=j: e.transpose(
                    PB[pb][:, 128 * j_:128 * (j_ + 1)],
                    intr_ld[:, 128 * i:128 * (i + 1)], c_identb[:, :]), "te")
            tev = B.val("te")
            if bb == 7:
                tp_mark[0] = tev
            B.wait("scalar", "te", tev)
            B.ins("scalar", lambda e, b=bb: e.copy(
                itT[:, 1024 * b:1024 * (b + 1)], PB[b % 2][:, 0:1024]), "se")
            pbm[bb % 2] = ("se", B.val("se"))
            evse.append(B.val("se"))
        fb = fwX[t % 2]
        if qdone[t % 2] is not None:
            B.wait("scalar", *qdone[t % 2])
        for m in range(16):
            po = 512 * (m % 2)
            if s1m[m % 2] is not None:
                B.wait("tensor", *s1m[m % 2])
            B.wait("tensor", "se", evse[m // 2])
            B.ins("tensor", lambda e, m_=m, o=po: e.matmul(
                PA[0][:, o:o + 512], c_had[:, :], itT[:, 512 * m_:512 * (m_ + 1)],
                start=True, stop=True), "te")
            tev = B.val("te")
            B.wait("scalar", "te", tev)
            B.ins("scalar", lambda e, m_=m, o=po, f=fb: e.copy(
                f[:, 512 * m_:512 * (m_ + 1)], PA[0][:, o:o + 512]), "se")
            s1m[m % 2] = ("se", B.val("se"))

    def emit_ATa(t):
        fb = fwX[t % 2]
        xqf = xqf2[(t // 2) % 2]
        mb = 16 * ((t // 2) % 2) + 8 * (t % 2)
        B.wait("vector", "se", B.val("se"))
        B.wait("vector", "te", tail_te[0])
        for s in range(6):
            src_ = fb if s % 2 == 0 else fw_scr
            dst_ = fw_scr if s % 2 == 0 else fb
            ps_ = 1 << s
            nh_ = 64 // (2 * ps_)
            sa = src_[:, :].rearrange("p (hi two lo u) -> p hi two lo u",
                                      hi=nh_, two=2, lo=ps_)
            da = dst_[:, :].rearrange("p (hi two lo u) -> p hi two lo u",
                                      hi=nh_, two=2, lo=ps_)
            B.ins("vector", lambda e, s_=sa, d=da: e.tensor_add(
                d[:, :, 0, :, :], s_[:, :, 0, :, :], s_[:, :, 1, :, :]), "ve")
            B.ins("vector", lambda e, s_=sa, d=da: e.tensor_sub(
                d[:, :, 1, :, :], s_[:, :, 0, :, :], s_[:, :, 1, :, :]), "ve")
            B.wait("vector", "ve")
        B.wait("scalar", "ve", B.val("ve"))
        B.ins("scalar", lambda e, f=fb: e.activation(
            fw_scr[:, :], f[:, :], AF.Abs), "se")
        B.wait("vector", "se", B.val("se"))
        ln = 32 * 128
        while ln >= 128:
            B.ins("vector", lambda e, l=ln: e.tensor_max(
                fw_scr[:, 0:l], fw_scr[:, 0:l], fw_scr[:, l:2 * l]), "ve")
            B.wait("vector", "ve")
            ln //= 2
        B.wait("tensor", "ve", B.val("ve"))
        B.ins("tensor", lambda e: e.transpose(
            PB[0][:, 0:128], fw_scr[:, 0:128], c_identb[:, :]), "te")
        tail_te[0] = B.val("te")
        B.wait("vector", "te", B.val("te"))
        B.ins("vector", lambda e, m_=mb: e.reduce_max(
            m2st[:, m_ + 0:m_ + 1], PB[0][:, 0:128], axis=AX.X), "ve")
        B.ins("vector", lambda e: e.engine_nop(), "px")
        px_mark[0] = B.val("px")
        B.wait("vector", "ve")
        B.ins("vector", lambda e, m_=mb: e.tensor_scalar(
            m2st[:, m_ + 0:m_ + 1], m2st[:, m_ + 0:m_ + 1], EPSRAW, None,
            ALU.max), "ve")
        B.wait("vector", "ve")
        B.ins("vector", lambda e, m_=mb: e.reciprocal(
            m2st[:, m_ + 1:m_ + 2], m2st[:, m_ + 0:m_ + 1]), "ve")
        B.wait("vector", "ve")
        B.ins("vector", lambda e, m_=mb: e.tensor_scalar(
            m2st[:, m_ + 1:m_ + 2], m2st[:, m_ + 1:m_ + 2], 128.0, None,
            ALU.mult), "ve")
        B.wait("vector", "ve")
        B.ins("vector", lambda e, m_=mb: e.tensor_scalar(
            m2st[:, m_ + 2:m_ + 3], m2st[:, m_ + 0:m_ + 1],
            RSQ8192 / 128.0, None, ALU.mult), "ve")
        B.wait("vector", "ve")
        B.ins("vector", lambda e, m_=mb: e.tensor_mul(
            m2st[:, m_ + 2:m_ + 3], m2st[:, m_ + 2:m_ + 3], st_row[:, 6:7]), "ve")
        vev = B.val("ve")
        B.wait("tensor", "ve", vev)
        B.ins("tensor", lambda e, m_=mb: e.transpose(
            PC[0:1, 384:512], m2st[:, m_ + 1:m_ + 2], c_identf[:, :]), "te")
        B.wait("vector", "te", B.val("te"))
        B.ins("vector", lambda e: e.tensor_copy(srow[0:1, 0:128], PC[0:1, 384:512]), "ve")
        B.wait("tensor", "ve", B.val("ve"))
        B.ins("tensor", lambda e: e.matmul(
            PC[:, 128:256], c_onesrow[0:1, :], srow[0:1, 0:128],
            start=True, stop=True), "te")
        tev = B.val("te")
        B.wait("vector", "te", tev)
        B.wait("vector", "te", cmark[(t // 2) % 2])
        for sl4 in range(16):
            fslice = fb[:, 512 * sl4:512 * (sl4 + 1)]
            B.ins("vector", lambda e, f=fslice: e.tensor_tensor(
                f32scr[:, :].rearrange("p (i u) -> p i u", i=4),
                f.rearrange("p (i u) -> p i u", i=4),
                bc3(PC[:, 128:256], 4), ALU.mult), "ve")
            B.wait("vector", "ve")
            B.ins("vector", lambda e: e.tensor_copy(i32scr[:, :], f32scr[:, :]), "ve")
            B.wait("vector", "ve")
            B.ins("vector", lambda e, tl=t % 2, s4=sl4, x=xqf: e.tensor_scalar(
                r3(x, 64, 256)[:, 4 * s4:4 * (s4 + 1), 128 * tl:128 * (tl + 1)],
                i32scr[:, :].rearrange("p (i u) -> p i u", i=4),
                127.0, None, ALU.min), "ve")
            B.wait("vector", "ve")
        qdone[t % 2] = ("ve", B.val("ve"))

    def emit_ATb(t):
        xqf = xqf2[(t // 2) % 2]
        B.wait("tensor", *qdone[t % 2])
        B.wait("tensor", "ve", pr_mark[0])
        for icm in range(64):
            B.ins("tensor", lambda e, i=icm, tl=t % 2, x=xqf: e.matmul(
                PR[0:1, 0:128], c_onesb[:, 0:1],
                r3(x, 64, 256)[:, i, 128 * tl:128 * (tl + 1)],
                start=(i == 0), stop=(i == 63)), "te")
        B.wait("vector", "te", B.val("te"))
        B.ins("vector", lambda e, c=256 * ((t // 2) % 2) + 128 * (t % 2): e.tensor_copy(
            sfrows[0:1, c:c + 128], PR[0:1, 0:128]), "ve")
        pr_mark[0] = B.val("ve")
        if t % 2 == 1:
            afin_mark[(t // 2) % 2] = B.val("ve")

    wd_mk = [0, 0, 0]

    def emit_Cstrip(sg, strip):
        xqf = xqf2[sg % 2]
        mbase = 16 * (sg % 2)
        ocb, qh = strip // 4, strip % 4
        slot = (strip // 2) % 2 if strip % 2 == 0 else 2
        wb = wd_str[slot]
        qeng = "sync" if strip % 2 == 0 else "scalar"
        if strip == 0:
            B.wait("gpsimd", "fin", ysg_mark[0])
            if sg == 0:
                S_xs.wait_all("gpsimd")
            for tl in range(2):
                tk_yi = S_yi.dma("gpsimd", ysg[:, 2048 * tl:2048 * (tl + 1)],
                                 x2_sp.ap()[256 * sg + 128 * tl:
                                            256 * sg + 128 * (tl + 1), :])
            S_yi.wait("vector", tk_yi)
            B.wait("tensor", "ve", afin_mark[sg % 2])
        B.wait(qeng, "te", wd_mk[slot])
        if sg == 0 and strip < 2:
            S_do.wait_all(qeng)
        tk_wl = S_wl.dma(
            qeng, wb[:, :],
            wd_tern.ap()[ocb, :, 16 * qh:16 * (qh + 1), :], slot)
        S_wl.wait("tensor", tk_wl)
        for tl in range(2):
            if qh == 0:
                B.wait("tensor", "dn", p6dn_marks[tl])
            for icm in range(16):
                i = 16 * qh + icm
                B.ins("tensor", lambda e, i_=i, icm_=icm, tl_=tl, w=wb,
                      o=512 * (tl % 2), x=xqf: e.matmul(
                    PA[1][:, o:o + 256],
                    r3(x, 64, 256)[:, i_, 128 * tl_:128 * (tl_ + 1)],
                    w[:, 256 * icm_:256 * (icm_ + 1)],
                    start=(i_ == 0), stop=False), "te")
            if qh == 3:
                B.ins("tensor", lambda e, o=512 * (tl % 2),
                      c=256 * (sg % 2) + 128 * tl: e.matmul(
                    PA[1][:, o:o + 256], sfrows[0:1, c:c + 128],
                    c_negrow[0:1, 0:256], start=False, stop=True), "te")
                tev = B.val("te")
                B.wait("vector", "te", tev)
                B.ins("vector", lambda e, t=tl, o=ocb, po=512 * (tl % 2),
                      m_=mbase + 8 * tl: e.scalar_tensor_tensor(
                    ysg[:, 2048 * t + 256 * o:2048 * t + 256 * (o + 1)],
                    PA[1][:, po:po + 256], m2st[:, m_ + 2:m_ + 3],
                    ysg[:, 2048 * t + 256 * o:2048 * t + 256 * (o + 1)],
                    ALU.mult, ALU.add), "ve")
                B.ins("vector", lambda e: e.engine_nop(), "dn")
                p6dn_marks[tl] = B.val("dn")
        wd_mk[slot] = B.val("te")
        if strip == 31:
            cmark[sg % 2] = B.val("te")
            ve_y = B.val("ve")
            B.wait("gpsimd", "ve", ve_y)
            for tl in range(2):
                B.dma("gpsimd", y_h[256 * sg + 128 * tl:256 * sg + 128 * (tl + 1), :],
                      ysg[:, 2048 * tl:2048 * (tl + 1)], "fin")
            ysg_mark[0] = B.val("fin")

    emit_AH(0); emit_ATa(0); emit_ATb(0)
    emit_AH(1); emit_ATa(1); emit_ATb(1)
    for sg in range(4):
        for strip in range(32):
            emit_Cstrip(sg, strip)
            if sg < 3:
                if strip == 7:
                    emit_AH(2 * sg + 2)
                elif strip == 15:
                    emit_ATa(2 * sg + 2)
                elif strip == 19:
                    emit_AH(2 * sg + 3)
                    emit_ATb(2 * sg + 2)
                elif strip == 27:
                    emit_ATa(2 * sg + 3)
                elif strip == 31:
                    emit_ATb(2 * sg + 3)
    B.wait("sync", "fin", B.val("fin"))

    return _finish(K, tap_list, taps, locals())


def _finish(K, tap_list, taps, loc):
    B = K
    for (tname, lname, shape, dt) in taps:
        h = loc.get(lname)
        if h is None:
            continue
        dst = K.nc.dram_tensor(tname, list(shape), dt, kind="ExternalOutput").ap()
        B.wait("sync", "ve")
        B.wait("sync", "te")
        B.wait("sync", "se")
        B.wait("sync", "st")
        if isinstance(h, list):
            h = h[0]
        B.dma("sync", dst, h[:, 0:shape[1]] if len(shape) == 2 else h[:, :], "fin")
    B.wait("sync", "fin")

    nc = K.nc
    with nc.Block() as block:
        @block.sync
        def _(e):
            for f in K.ops["sync"]:
                f(e)

        @block.tensor
        def _(e):
            for f in K.ops["tensor"]:
                f(e)

        @block.vector
        def _(e):
            for f in K.ops["vector"]:
                f(e)

        @block.scalar
        def _(e):
            for f in K.ops["scalar"]:
                f(e)

        @block.gpsimd
        def _(e):
            for f in K.ops["gpsimd"]:
                f(e)
    for s in reversed(K.open_scopes):
        try:
            s.close()
        except Exception:
            pass
    K.root.close()
    return nc



def _prep_inputs(inputs):
    f32 = lambda a: np.ascontiguousarray(np.asarray(a, dtype=np.float32))
    x = f32(inputs["L0_attn_x"])
    Wq, Wk, Wv, Wo = (f32(inputs["L0_attn_" + n]) for n in ("Wq", "Wk", "Wv", "Wo"))
    qA, qB, kA, kB, vA, vB, oA, oB = (f32(inputs["L0_attn_" + n]) for n in
                                      ("qA", "qB", "kA", "kB", "vA", "vB", "oA", "oB"))
    wg, wu, wd = (f32(inputs["L0_ffn_w_gate"]), f32(inputs["L0_ffn_w_up"]),
                  f32(inputs["L0_ffn_w_down"]))

    permq = _deperm(16)
    permk = _deperm(8)
    WqT = np.ascontiguousarray(Wq.T[:, permq])
    qBT = np.ascontiguousarray(qB.T[:, permq])
    WkT = np.ascontiguousarray(Wk.T[:, permk])
    kBT = np.ascontiguousarray(kB.T[:, permk])
    WvT = np.ascontiguousarray(Wv.T)
    vBT = np.ascontiguousarray(vB.T)
    WoT = np.ascontiguousarray(Wo.T)
    oBT = np.ascontiguousarray(oB.T)
    def tile_gu(wT):
        t = wT.reshape(16, 128, 16, 512)
        return np.ascontiguousarray(t.transpose(2, 1, 0, 3)).reshape(16, 128, 8192)
    wgT = tile_gu(wg.T)
    wuT = tile_gu(wu.T)
    wdTt = wd.T.reshape(64, 128, 8, 256)
    wdT = np.ascontiguousarray(wdTt.transpose(2, 1, 0, 3))

    cosT, sinT = _rope_tables()
    cosT = np.ascontiguousarray(np.concatenate([cosT[:, :S // 2], cosT[:, S // 2:]], axis=0))
    sinT = np.ascontiguousarray(np.concatenate([sinT[:, :S // 2], sinT[:, S // 2:]], axis=0))
    identb = np.eye(128, dtype=np.float32).astype(ml_dtypes.bfloat16)
    identf = np.eye(128, dtype=np.float32)
    had = _hadamard(128).astype(ml_dtypes.bfloat16)
    masks = _diag_masks()

    in_maps = []
    for c in range(8):
        b, g = c // 4, c % 4
        gates = np.zeros((128, 8), np.float32)
        gates[:, 4 * b:4 * b + 4] = 1.0
        in_maps.append({
            "x": x[b],
            "xown": np.ascontiguousarray(x[b, TOK * g:TOK * (g + 1)]),
            "wqT": np.ascontiguousarray(WqT[:, 512 * g:512 * (g + 1)]),
            "wkT": np.ascontiguousarray(WkT[:, 256 * g:256 * (g + 1)]),
            "wvT": np.ascontiguousarray(WvT[:, 256 * g:256 * (g + 1)]),
            "woT": WoT,
            "qA": qA, "kA": kA, "vA": vA, "oA": oA,
            "qBT": np.ascontiguousarray(qBT[:, 512 * g:512 * (g + 1)]),
            "kBT": np.ascontiguousarray(kBT[:, 256 * g:256 * (g + 1)]),
            "vBT": np.ascontiguousarray(vBT[:, 256 * g:256 * (g + 1)]),
            "oBT": oBT,
            "wgT": wgT, "wuT": wuT, "wdT": wdT,
            "wg_s": np.ascontiguousarray(wg.reshape(8, 128, 16384)[c]),
            "wu_s": np.ascontiguousarray(wu.reshape(8, 128, 16384)[c]),
            "wd_s": np.ascontiguousarray(wd.reshape(8, 128, 16384)[c]),
            "identb": identb, "identf": identf,
            "cosT": cosT, "sinT": sinT, "masks": masks, "had": had,
            "a2agate": gates,
        })
    return in_maps


def kernel(**inputs):
    in_maps = _prep_inputs(inputs)
    nc = build_program(taps=list(_DEBUG_TAPS), stop=_DEBUG_STOP[0])
    res = run_bass_kernel_spmd(nc, in_maps, core_ids=list(range(8)))
    kernel._last_results = res
    y = np.zeros((2, S, HID), np.float32)
    for c in range(8):
        b, g = c // 4, c % 4
        y[b, TOK * g:TOK * (g + 1)] = res.results[c]["y"]
    return y


# version 09
# speedup vs baseline: 1.0388x; 1.0388x over previous
import numpy as np
import ml_dtypes

import concourse.bass as bass
import concourse.mybir as mybir
from concourse.bass_utils import run_bass_kernel_spmd
from concourse.aot_env import allow_ici_all_to_all
from concourse.ap import AP

F32 = mybir.dt.float32
BF16 = mybir.dt.bfloat16
I32 = mybir.dt.int32
AF = mybir.ActivationFunctionType
ALU = mybir.AluOpType
AX = mybir.AxisListType

HID = 2048
S = 4096
TOK = 1024
HD = 128
SCALING = float(1.0 / np.sqrt(HD))
SHIFT = -40.0
EPS = 1e-5
IFF = 8192
MASKV = -1.0e30
INV2048 = float(np.float32(1.0) / np.float32(2048.0))
INVWELEM = float(np.float32(1.0) / np.float32(16777216.0))
RSQ8192 = float(np.float32(1.0) / np.float32(np.sqrt(np.float32(8192.0))))
EPSRAW = float(np.float32(1e-5) * np.float32(np.sqrt(np.float32(8192.0))))

_DEBUG_TAPS = []
_DEBUG_STOP = [None]


def _rope_tables():
    inv_freq = (1.0 / 10000.0 ** (np.arange(0, HD, 2, dtype=np.float32) / HD)).astype(np.float32)
    t = np.arange(S, dtype=np.float32)
    fr = t[:, None] * inv_freq[None, :]
    return (np.ascontiguousarray(np.cos(fr).astype(np.float32).T),
            np.ascontiguousarray(np.sin(fr).astype(np.float32).T))


def _deperm(n_heads):
    p = []
    for h in range(n_heads):
        base = h * HD
        p.extend(base + np.arange(0, HD, 2))
        p.extend(base + np.arange(1, HD, 2))
    return np.array(p)


def _hadamard(n):
    H = np.array([[1.0]], dtype=np.float32)
    while H.shape[0] < n:
        H = np.block([[H, H], [H, -H]])
    return H.astype(np.float32)


def _diag_masks():
    m = np.zeros((128, 4 * 512), dtype=np.float32)
    col = np.arange(512)
    rows = np.arange(128)[:, None]
    for j in range(4):
        m[:, 512 * j:512 * (j + 1)] = np.where(col[None, :] >= 128 * j + rows, 0.0, MASKV)
    return m


def bc3(ap2d, rep):
    return AP(ap2d.tensor, ap2d.offset, [ap2d.ap[0], [0, rep], ap2d.ap[1]])


class KB:
    ENGINES = ("sync", "tensor", "vector", "scalar", "gpsimd")

    def __init__(self):
        self.nc = bass.Bass("TRN2", target_bir_lowering=False, debug=False)
        self.ops = {e: [] for e in self.ENGINES}
        self.cnt = {}
        self.sems = {}
        from contextlib import ExitStack
        self.root = ExitStack()
        self.open_scopes = []

    def ins(self, eng, fn, sem=None, k=1):
        if sem is None:
            self.ops[eng].append(lambda e, _f=fn: _f(e))
            return None
        self.cnt[sem] += k
        v = self.cnt[sem]

        def run(e, _f=fn, _s=sem, _k=k):
            _f(e).then_inc(self.sems[_s], _k)
        self.ops[eng].append(run)
        return v

    def dma(self, eng, out, in_, sem="ld"):
        return self.ins(eng, lambda e, o=out, i=in_: e.dma_start(out=o, in_=i), sem, 16)

    def stream(self, name, nbuf=2):
        return Stream(self, name, nbuf)

    def wait(self, eng, sem, v=None):
        if v is None:
            v = self.cnt[sem]
        if v <= 0:
            return
        self.ops[eng].append(lambda e, _s=sem, _v=v: e.wait_ge(self.sems[_s], _v))

    def val(self, sem):
        return self.cnt[sem]

    def barrier(self):
        for eng in self.ENGINES:
            for s in sorted(self.cnt):
                self.wait(eng, s)

    def sb(self, name, shape, dt, scope=None):
        st = scope if scope is not None else self.root
        t = st.enter_context(self.nc.sbuf_tensor(name, list(shape), dt))
        return t

    def psm(self, name, shape, dt):
        return self.root.enter_context(self.nc.psum_tensor(name, list(shape), dt))

    def scope(self):
        from contextlib import ExitStack
        s = ExitStack()
        self.open_scopes.append(s)
        return s


class Stream:

    def __init__(self, K, name, nbuf):
        self.K = K
        self.names = []
        for i in range(nbuf):
            n = f"{name}{i}"
            K.sems[n] = K.root.enter_context(K.nc.semaphore(n))
            K.cnt[n] = 0
            self.names.append(n)
        self.nbuf = nbuf

    def dma(self, eng, out, in_, slot=0):
        n = self.names[slot % self.nbuf]
        self.K.ins(eng, lambda e, o=out, i=in_: e.dma_start(out=o, in_=i), n, 16)
        return (n, self.K.cnt[n])

    def wait(self, eng, token):
        self.K.wait(eng, token[0], token[1])

    def wait_all(self, eng):
        for n in self.names:
            self.K.wait(eng, n, self.K.cnt[n])


def r3(t, a, b):
    return t[:, :].rearrange("p (a b) -> p a b", a=a)


def build_program(taps=(), stop=None):
    K = KB()
    nc = K.nc
    B = K

    def dram_in(name, shape, dt=F32):
        return nc.dram_tensor(name, list(shape), dt, kind="ExternalInput").ap()

    x_h = dram_in("x", (S, HID))
    xown_h = dram_in("xown", (TOK, HID))
    wqT_h = dram_in("wqT", (HID, 512))
    wkT_h = dram_in("wkT", (HID, 256))
    wvT_h = dram_in("wvT", (HID, 256))
    woT_h = dram_in("woT", (HID, HID))
    qA_h = dram_in("qA", (8, HID))
    kA_h = dram_in("kA", (8, HID))
    vA_h = dram_in("vA", (8, HID))
    oA_h = dram_in("oA", (8, HID))
    qBT_h = dram_in("qBT", (8, 512))
    kBT_h = dram_in("kBT", (8, 256))
    vBT_h = dram_in("vBT", (8, 256))
    oBT_h = dram_in("oBT", (8, HID))
    wgT_h = dram_in("wgT", (16, 128, 16 * 512))
    wuT_h = dram_in("wuT", (16, 128, 16 * 512))
    wdT_h = dram_in("wdT", (8, 128, 64, 256))
    wgs_h = dram_in("wg_s", (128, 16384))
    wus_h = dram_in("wu_s", (128, 16384))
    wds_h = dram_in("wd_s", (128, 16384))
    identb_h = dram_in("identb", (128, 128), BF16)
    identf_h = dram_in("identf", (128, 128))
    cos_h = dram_in("cosT", (128, S // 2))
    sin_h = dram_in("sinT", (128, S // 2))
    masks_h = dram_in("masks", (128, 2048))
    had_h = dram_in("had", (128, 128), BF16)
    gates_h = dram_in("a2agate", (128, 8))

    y_h = nc.dram_tensor("y", [TOK, HID], F32, kind="ExternalOutput").ap()

    a2a_in = nc.dram_tensor("a2a_in", [8, 512, TOK], BF16)
    a2a_out = nc.dram_tensor("a2a_out", [8, 512, TOK], BF16)
    st_in = nc.dram_tensor("st_in", [8, 1], F32)
    st_out = nc.dram_tensor("st_out", [8, 1], F32)
    x2_sp = nc.dram_tensor("x2_spill", [TOK, HID], F32)
    wd_tern = nc.dram_tensor("wd_tern", [8, 128, 64, 256], BF16)
    inter_sp = nc.dram_tensor("inter_spill", [TOK, IFF], BF16)

    c_identb = K.sb("c_identb", (128, 128), BF16)
    c_identf = K.sb("c_identf", (128, 128), F32)
    c_had = K.sb("c_had", (128, 128), BF16)
    c_onesb = K.sb("c_onesb", (128, 1), BF16)
    c_onesrow = K.sb("c_onesrow", (1, 128), F32)
    c_negrow = K.sb("c_negrow", (1, 256), F32)
    c_gates = K.sb("c_gates", (128, 8), F32)
    st_row = K.sb("st_row", (128, 16), F32)
    c_eps = K.sb("c_eps", (128, 2), F32)
    statacc = K.sb("statacc", (128, 8), F32)
    stat_tp = K.sb("stat_tp", (128, 8), F32)
    strow1 = K.sb("strow1", (1, 8), F32)
    lnstat = K.sb("lnstat", (128, 16), F32)
    lnsum = K.sb("lnsum", (128, 32), F32)
    lnsq = K.sb("lnsq", (128, 32), F32)
    lnm = K.sb("lnm", (128, 32), F32)
    lnr = K.sb("lnr", (128, 32), F32)
    sq_scr = K.sb("sq_scr", (128, HID), F32)
    qscal = K.sb("qscal", (128, 64), F32)
    srow = K.sb("srow", (1, 128), F32)
    sfrows = K.sb("sfrows", (1, 512), F32)

    PA = [K.psm(f"PA{i}", (128, 1024), F32) for i in range(2)]
    PB = [K.psm(f"PB{i}", (128, 1024), BF16) for i in range(2)]
    PC = K.psm("PC", (128, 512), F32)
    PR = K.psm("PR", (1, 512), F32)

    for n in ("ld", "gd", "te", "ve", "se", "st", "cc", "pt", "fz", "qv", "dn",
              "px", "fin"):
        K.sems[n] = K.root.enter_context(nc.semaphore(n))
        K.cnt[n] = 0

    tap_list = []

    def mark_tap(name, handle, shape):
        tap_list.append((nc.dram_tensor(name, list(shape), handle.dtype if hasattr(handle, "dtype") else F32,
                                        kind="ExternalOutput").ap(), handle, shape))

    S_c = B.stream("Sc", 1)
    for dst, src in ((c_identb, identb_h), (c_identf, identf_h), (c_had, had_h),
                     (c_gates, gates_h)):
        tk_consts = S_c.dma("sync", dst[:, :], src)
    B.ins("vector", lambda e: e.memset(c_onesb[:, :], 1.0), "ve")
    B.ins("vector", lambda e: e.memset(statacc[:, :], 0.0), "ve")
    B.ins("vector", lambda e: e.memset(st_row[:, :], 0.0), "ve")
    B.ins("vector", lambda e: e.memset(qscal[:, :], 0.0), "ve")
    B.wait("vector", "ve")
    B.ins("vector", lambda e: e.memset(c_eps[:, 0:1], EPS), "ve")
    B.ins("vector", lambda e: e.memset(c_eps[:, 1:2], SHIFT), "ve")
    B.ins("vector", lambda e: e.memset(c_onesrow[:, :], 1.0), "ve")
    B.ins("vector", lambda e: e.memset(c_negrow[:, :], -1.0), "ve")

    p2scope = K.scope()
    q_fm = K.sb("q_fm", (128, 4 * S), BF16, p2scope)
    k_fm = K.sb("k_fm", (128, 2 * S), BF16, p2scope)
    v_tok = K.sb("v_tok", (128, 2 * 32 * 128), BF16, p2scope)
    p1scope = K.scope()
    weff = K.sb("weff", (128, 16 * 1024), BF16, p1scope)
    c_cos = K.sb("c_cos", (128, S // 2), F32, p1scope)
    c_sin = K.sb("c_sin", (128, S // 2), F32, p1scope)
    xin = [K.sb(f"xin{i}", (128, HID), F32, p1scope) for i in range(2)]
    xn_tok = [K.sb(f"xntok{i}", (128, HID), BF16, p1scope) for i in range(2)]
    xnT = K.sb("xnT", (128, 16 * 512), BF16, p1scope)
    rope_t = K.sb("rope_t", (64, 2 * 512), F32, p1scope)
    v_stg = K.sb("v_stg", (128, 512), BF16, p1scope)
    pre = K.scope()
    wstr = [K.sb(f"wstr{i}", (128, 1024), F32, pre) for i in range(2)]
    lora_a = K.sb("lora_a", (8, HID), F32, pre)
    lora_b = K.sb("lora_b", (8, 512), F32, pre)
    stbuf = xin


    S_st = B.stream("Sst", 2)
    for m, src in enumerate((wgs_h, wus_h, wds_h)):
        for ch in range(8):
            idx = m * 8 + ch
            buf = stbuf[idx % 2]
            if idx >= 2:
                B.wait("gpsimd", "fz", B.val("fz") - 1)
            tk = S_st.dma("gpsimd", buf[:, :], src[:, 2048 * ch:2048 * (ch + 1)], idx)
            S_st.wait("vector", tk)
            if ch == 0:
                B.ins("vector", lambda e, b=buf, mm=m: e.reduce_sum(
                    statacc[:, mm:mm + 1], b[:, :], axis=AX.X,
                    apply_absolute_value=True), "ve")
            else:
                B.ins("vector", lambda e, b=buf: e.reduce_sum(
                    stat_tp[:, 0:1], b[:, :], axis=AX.X,
                    apply_absolute_value=True), "ve")
                B.wait("vector", "ve")
                B.ins("vector", lambda e, mm=m: e.tensor_add(
                    statacc[:, mm:mm + 1], statacc[:, mm:mm + 1], stat_tp[:, 0:1]), "ve")
                B.wait("vector", "ve")
            B.ins("vector", lambda e: e.engine_nop(), "fz")
    B.wait("tensor", "ve", B.val("ve"))
    S_c.wait("tensor", tk_consts)
    B.ins("tensor", lambda e: e.transpose(
        PA[0][0:8, 0:128], statacc[:, 0:8], c_identf[:, :]), "te")
    B.wait("vector", "te", B.val("te"))
    B.ins("vector", lambda e: e.reduce_sum(
        stat_tp[0:8, 0:1], PA[0][0:8, 0:128], axis=AX.X), "ve")
    S_sm = B.stream("Ssm", 1)
    B.wait("gpsimd", "ve", B.val("ve"))
    tk = S_sm.dma("gpsimd", st_in.ap()[0:8, 0:1], stat_tp[0:8, 0:1])
    S_sm.wait("gpsimd", tk)
    B.ins("gpsimd", lambda e: e.collective_compute(
        "AllReduce", ALU.add, replica_groups=[list(range(8))],
        ins=[st_in.ap().opt()], outs=[st_out.ap().opt()]), "cc")
    B.wait("gpsimd", "cc")
    tk_row = S_sm.dma("gpsimd", strow1[0:1, 0:8], st_out.ap()[0:8, 0:1])
    S_sm.wait("tensor", tk_row)
    B.wait("tensor", "ve", B.val("ve"))
    B.ins("tensor", lambda e: e.matmul(
        PA[0][:, 0:8], c_onesrow[0:1, :], strow1[0:1, 0:8], start=True, stop=True), "te")
    B.wait("vector", "te", B.val("te"))
    B.ins("vector", lambda e: e.tensor_copy(st_row[:, 0:8], PA[0][:, 0:8]), "ve")
    B.wait("vector", "ve")
    B.ins("vector", lambda e: e.tensor_scalar(
        st_row[:, 4:7], st_row[:, 0:3], INVWELEM, EPS, ALU.mult, ALU.max), "ve")
    B.wait("vector", "ve")
    B.ins("vector", lambda e: e.tensor_scalar(
        st_row[:, 8:11], st_row[:, 4:7], 0.5, None, ALU.mult), "ve")
    B.wait("vector", "ve")
    B.ins("vector", lambda e: e.tensor_scalar(
        st_row[:, 11:14], st_row[:, 8:11], -1.0, None, ALU.mult), "ve")
    ve_stats = B.val("ve")

    B.wait("tensor", "ve", B.val("ve"))
    S_lo = B.stream("Slo", 1)
    S_wf = B.stream("Swf", 2)
    for (mi, (Ah, BTh, WTh, ncol, obase)) in enumerate((
            (qA_h, qBT_h, wqT_h, 512, 0),
            (kA_h, kBT_h, wkT_h, 256, 512),
            (vA_h, vBT_h, wvT_h, 256, 768))):
        B.wait("sync", "qv", B.val("qv"))
        S_lo.dma("sync", lora_a[:, 0:HID], Ah)
        tk_lo = S_lo.dma("sync", lora_b[:, 0:ncol], BTh)
        S_lo.wait("tensor", tk_lo)
        for ic in range(16):
            buf = wstr[ic % 2]
            if ic >= 2 or mi > 0:
                B.wait("sync", "qv", B.val("qv") - 1)
            tk_w = S_wf.dma("sync", buf[:, 0:ncol], WTh[128 * ic:128 * (ic + 1), :], ic)
            if ic >= 2 or mi > 0:
                B.wait("tensor", "qv", B.val("qv") - 1)
            B.ins("tensor", lambda e, i=ic, n=ncol: e.matmul(
                PA[i % 2][:, 0:n], lora_a[:, 128 * i:128 * (i + 1)],
                lora_b[:, 0:n], start=True, stop=True), "te")
            tev = B.val("te")
            S_wf.wait("vector", tk_w)
            B.wait("vector", "te", tev)
            B.ins("vector", lambda e, b=buf, i=ic, n=ncol, ob=obase: e.scalar_tensor_tensor(
                weff[:, 1024 * i + ob:1024 * i + ob + n], PA[i % 2][:, 0:n], 2.0,
                b[:, 0:n], ALU.mult, ALU.add), "ve")
            B.ins("vector", lambda e: e.engine_nop(), "qv")
    B.barrier()
    pre.close()


    S_cs = B.stream("Scs", 1)
    S_cs.dma("sync", c_cos[:, :], cos_h)
    tk_cs = S_cs.dma("sync", c_sin[:, :], sin_h)
    S_cs.wait("vector", tk_cs)
    B.wait("sync", "ve", ve_stats)
    S_x = B.stream("Sx", 2)

    xuse_marks = [0, 0]
    xuse_se = [0, 0]
    tpmk = [0, 0]
    qv_marks = [0, 0]

    for gt in range(32):
        xb = xin[gt % 2]
        B.wait("sync", "ve", xuse_marks[gt % 2])
        B.wait("sync", "se", xuse_se[gt % 2])
        tk_x = S_x.dma("sync", xb[:, :], x_h[128 * gt:128 * (gt + 1), :], gt)
        S_x.wait("vector", tk_x)
        S_x.wait("scalar", tk_x)
        B.ins("vector", lambda e, b=xb, g=gt: e.reduce_sum(
            lnsum[:, g:g + 1], b[:, :], axis=AX.X), "ve")
        xuse_marks[gt % 2] = B.val("ve")
        B.wait("scalar", "se", B.val("se"))
        B.ins("scalar", lambda e, b=xb, g=gt: e.activation(
            sq_scr[:, :], b[:, :], AF.Square, accum_out=lnsq[:, g:g + 1]), "se")
        xuse_se[gt % 2] = B.val("se")
    B.wait("vector", "se", B.val("se"))
    B.wait("vector", "ve", B.val("ve"))
    B.ins("vector", lambda e: e.tensor_scalar(
        lnm[:, :], lnsum[:, :], INV2048, None, ALU.mult), "ve")
    B.wait("vector", "ve")
    B.ins("vector", lambda e: e.tensor_mul(lnr[:, :], lnm[:, :], lnm[:, :]), "ve")
    B.wait("vector", "ve")
    B.ins("vector", lambda e: e.scalar_tensor_tensor(
        lnr[:, :], lnsq[:, :], INV2048, lnr[:, :], ALU.mult, ALU.subtract), "ve")
    B.wait("scalar", "ve", B.val("ve"))
    B.ins("scalar", lambda e: e.activation(
        lnsq[:, :], lnr[:, :], AF.Sqrt, bias=c_eps[:, 0:1]), "se")
    B.wait("vector", "se", B.val("se"))
    B.ins("vector", lambda e: e.reciprocal(lnr[:, :], lnsq[:, :]), "ve")
    B.wait("vector", "ve")
    ve_lnchain = B.val("ve")

    for tg in range(8):
        for tl in range(4):
            gt = tg * 4 + tl
            xb = xin[gt % 2]
            B.wait("sync", "ve", xuse_marks[gt % 2])
            B.wait("sync", "se", xuse_se[gt % 2])
            tk_x = S_x.dma("sync", xb[:, :], x_h[128 * gt:128 * (gt + 1), :], gt)
            S_x.wait("vector", tk_x)
            B.ins("vector", lambda e, b=xb, g=gt, t=gt: e.tensor_scalar(
                xn_tok[t % 2][:, :], b[:, :], lnm[:, g:g + 1],
                lnr[:, g:g + 1], ALU.subtract, ALU.mult), "ve")
            xuse_marks[gt % 2] = B.val("ve")
            vev = B.val("ve")
            B.wait("tensor", "ve", vev)
            for bb in range(2):
                B.wait("tensor", "ve", tpmk[bb])
                for j in range(8):
                    ic = bb * 8 + j
                    B.ins("tensor", lambda e, t=gt, i=ic, b=bb, j_=j: e.transpose(
                        PB[b][:, 128 * j_:128 * (j_ + 1)],
                        xn_tok[t % 2][:, 128 * i:128 * (i + 1)], c_identb[:, :]), "te")
                tev = B.val("te")
                B.wait("vector", "te", tev)
                B.ins("vector", lambda e, tl_=tl, b=bb: e.tensor_copy(
                    r3(xnT, 16, 512)[:, 8 * b:8 * (b + 1), 128 * tl_:128 * (tl_ + 1)],
                    r3(PB[b], 8, 128)[:, :, :]), "ve")
                tpmk[bb] = B.val("ve")
        ve_xnT = B.val("ve")

        B.wait("tensor", "ve", ve_xnT)
        tglob = tg * 512
        for oc in range(8):
            unit = tg * 8 + oc
            pp = PA[unit % 2]
            B.wait("tensor", "qv", qv_marks[unit % 2])
            for ic in range(16):
                B.ins("tensor", lambda e, i=ic, o=oc, pb=pp: e.matmul(
                    pb[:, 0:512],
                    weff[:, 1024 * i + 128 * o:1024 * i + 128 * (o + 1)],
                    r3(xnT, 16, 512)[:, i, :],
                    start=(i == 0), stop=(i == 15)), "te")
            tev = B.val("te")
            B.wait("vector", "te", tev)
            if oc < 6:
                dst = q_fm if oc < 4 else k_fm
                hh = oc if oc < 4 else oc - 4
                pbs = 64 * (tg // 4)
                co = 512 * (tg % 4)
                cslice = c_cos[pbs:pbs + 64, co:co + 512]
                sslice = c_sin[pbs:pbs + 64, co:co + 512]
                B.ins("vector", lambda e, pb=pp, c=cslice: e.tensor_mul(
                    rope_t[:, 0:512], pb[0:64, 0:512], c), "ve")
                B.ins("vector", lambda e, pb=pp, s_=sslice: e.tensor_mul(
                    rope_t[:, 512:1024], pb[64:128, 0:512], s_), "ve")
                B.wait("vector", "ve")
                B.ins("vector", lambda e, d=dst, h=hh, t0=tglob: e.tensor_sub(
                    d[0:64, h * S + t0:h * S + t0 + 512],
                    rope_t[:, 0:512], rope_t[:, 512:1024]), "ve")
                B.wait("vector", "ve")
                B.ins("vector", lambda e, pb=pp, s_=sslice: e.tensor_mul(
                    rope_t[:, 0:512], pb[0:64, 0:512], s_), "ve")
                B.ins("vector", lambda e, pb=pp, c=cslice: e.tensor_mul(
                    rope_t[:, 512:1024], pb[64:128, 0:512], c), "ve")
                B.wait("vector", "ve")
                B.ins("vector", lambda e, d=dst, h=hh, t0=tglob: e.tensor_add(
                    d[64:128, h * S + t0:h * S + t0 + 512],
                    rope_t[:, 0:512], rope_t[:, 512:1024]), "ve")
                B.wait("vector", "ve")
                B.ins("vector", lambda e: e.engine_nop(), "qv")
                qv_marks[unit % 2] = B.val("qv")
            else:
                kvh = oc - 6
                B.ins("vector", lambda e, pb=pp: e.tensor_copy(
                    v_stg[:, :], pb[:, 0:512]), "ve")
                B.ins("vector", lambda e: e.engine_nop(), "qv")
                qv_marks[unit % 2] = B.val("qv")
                vev = B.val("ve")
                B.wait("tensor", "ve", vev)
                B.wait("tensor", "ve", tpmk[0])
                for j in range(4):
                    B.ins("tensor", lambda e, j_=j: e.transpose(
                        PB[0][:, 128 * j_:128 * (j_ + 1)],
                        v_stg[:, 128 * j_:128 * (j_ + 1)], c_identb[:, :]), "te")
                tev = B.val("te")
                B.wait("vector", "te", tev)
                ktb = tglob // 128
                B.ins("vector", lambda e, kv=kvh, kb=ktb: e.tensor_copy(
                    v_tok[:, :].rearrange("p (kv kt d) -> p kv kt d", kv=2, kt=32)[
                        :, kv, kb:kb + 4, :],
                    r3(PB[0], 8, 128)[:, 0:4, :]), "ve")
                tpmk[0] = B.val("ve")

    B.barrier()
    p1scope.close()
    if stop == "p1":
        return _finish(K, tap_list, taps, locals())

    p2b = K.scope()
    wdin = [K.sb(f"wdin{i}", (128, HID), F32, p2b) for i in range(2)]
    wdq = [K.sb(f"wdq{i}", (128, HID), BF16, p2b) for i in range(2)]
    c_masks = K.sb("c_masks", (128, 2048), F32, p2b)
    pt_bufs = [K.sb(f"ptbuf{i}", (128, 1024), BF16, p2b) for i in range(2)]
    an_f32 = K.sb("an_f32", (128, 512), F32, p2b)
    rs_sb = K.sb("rs_sb", (1, 512), F32, p2b)

    S_mk = B.stream("Smk", 1)
    tk_mk = S_mk.dma("sync", c_masks[:, :], masks_h)

    B.wait("vector", "ve", ve_stats)
    S_wd = B.stream("Swd", 2)
    S_do = B.stream("Sdo", 2)
    wd_marks = [None, None]
    wd_tk = {}

    def emit_p2a_load(blk):
        if wd_marks[blk % 2] is not None:
            S_do.wait("gpsimd", wd_marks[blk % 2])
        ocb0 = blk // 8
        ic0 = (blk % 8) * 8
        wd_tk[blk] = S_wd.dma("gpsimd", wdin[blk % 2][:, :],
                              wdT_h[ocb0, :, ic0:ic0 + 8, :], blk)

    def emit_p2a_tern(blk):
        buf = wdin[blk % 2]
        qb = wdq[blk % 2]
        ocb0 = blk // 8
        ic0 = (blk % 8) * 8
        S_wd.wait("vector", wd_tk[blk])
        B.ins("vector", lambda e, b=buf: e.tensor_scalar(
            sq_scr[:, :], b[:, :], st_row[:, 10:11], None, ALU.is_ge), "ve")
        B.wait("vector", "ve")
        B.ins("vector", lambda e, b=buf, q=qb: e.scalar_tensor_tensor(
            q[:, :], b[:, :], st_row[:, 13:14], sq_scr[:, :],
            ALU.is_gt, ALU.add), "ve")
        vev = B.val("ve")
        B.wait("vector", "ve", vev)
        B.wait("gpsimd", "ve", vev)
        wd_marks[blk % 2] = S_do.dma(
            "gpsimd", wd_tern.ap()[ocb0, :, ic0:ic0 + 8, :], qb[:, :], blk)

    S_mk.wait("vector", tk_mk)
    S_mk.wait("scalar", tk_mk)
    S_a2 = B.stream("Sa2", 2)
    an_outs = [K.sb(f"anout{i}", (128, 512), BF16, p2b) for i in range(2)]
    an_tok = [None, None]

    se_marks = [0, 0]
    pt_marks = [0, 0]
    pa0_guard = [0]
    qv_fin = [0]
    sched = []
    for h in range(4):
        for qc in range(8):
            ngrp = 2 * (qc + 1)
            for grp in range(ngrp):
                sched.append((h, qc, grp, ngrp))

    def emit_av(idx):
        h, qc, grp, ngrp = sched[idx]
        kv = h // 2
        g = idx
        first = grp == 0
        last = grp == ngrp - 1
        if first:
            B.wait("tensor", "qv", qv_fin[0])
        for t2 in range(2):
            kt = 2 * grp + t2
            s0 = first and t2 == 0
            s1 = last and t2 == 1
            B.ins("tensor", lambda e, g_=g, k=kt, t_=t2, kv_=kv, a=s0, b=s1: e.matmul(
                PC[:, 0:512],
                v_tok[:, :].rearrange("p (kv kt d) -> p kv kt d", kv=2, kt=32)[
                    :, kv_, k, :],
                pt_bufs[g_ % 2][:, 512 * t_:512 * (t_ + 1)],
                start=a, stop=b), "te")
            B.ins("tensor", lambda e, g_=g, t_=t2, a=s0, b=s1: e.matmul(
                PR[0:1, 0:512], c_onesb[:, 0:1],
                pt_bufs[g_ % 2][:, 512 * t_:512 * (t_ + 1)],
                start=a, stop=b), "te")
        B.ins("tensor", lambda e: e.nop(), "pt")
        pt_marks[g % 2] = B.val("pt")
        if last:
            tev = B.val("te")
            B.wait("vector", "te", tev)
            B.ins("vector", lambda e: e.reciprocal(
                rs_sb[0:1, 0:512], PR[0:1, 0:512]), "ve")
            B.wait("vector", "ve")
            B.ins("vector", lambda e: e.tensor_copy(an_f32[:, :], PC[:, 0:512]), "ve")
            B.ins("vector", lambda e: e.engine_nop(), "qv")
            qv_fin[0] = B.val("qv")
            vev = B.val("ve")
            B.wait("tensor", "ve", vev)
            B.wait("tensor", "se", B.val("se"))
            B.ins("tensor", lambda e: e.matmul(
                PA[0][:, 0:512], c_onesrow[0:1, :], rs_sb[0:1, 0:512],
                start=True, stop=True), "te")
            tev = B.val("te")
            B.wait("vector", "te", tev)
            hq = h * 8 + qc
            if an_tok[hq % 2] is not None:
                S_a2.wait("vector", an_tok[hq % 2])
            B.ins("vector", lambda e, p_=hq % 2: e.tensor_mul(
                an_outs[p_][:, :], an_f32[:, :], PA[0][:, 0:512]), "ve")
            pa0_guard[0] = B.val("ve")
            vev = B.val("ve")
            B.wait("sync", "ve", vev)
            sl = qc // 2
            co = 512 * (qc % 2)
            S_a2.dma("sync", a2a_in.ap()[sl, 128 * h:128 * (h + 1), co:co + 512],
                     an_outs[hq % 2][:, :], hq)
            an_tok[hq % 2] = S_a2.dma(
                "sync", a2a_in.ap()[sl + 4, 128 * h:128 * (h + 1), co:co + 512],
                an_outs[hq % 2][:, :], hq)

    emit_p2a_load(0)
    emit_p2a_load(1)
    for gidx, (h, qc, grp, ngrp) in enumerate(sched):
        kv = h // 2
        sp = PA[gidx % 2]
        B.wait("tensor", "se", se_marks[gidx % 2])
        if gidx % 2 == 0 and pa0_guard[0]:
            B.wait("tensor", "ve", pa0_guard[0])
            pa0_guard[0] = 0
        for t2 in range(2):
            kt = 2 * grp + t2
            B.ins("tensor", lambda e, s_=sp, k=kt, t_=t2, h_=h, q_=qc, kv_=kv: e.matmul(
                s_[:, 512 * t_:512 * (t_ + 1)],
                k_fm[:, kv_ * S + 128 * k:kv_ * S + 128 * (k + 1)],
                q_fm[:, h_ * S + 512 * q_:h_ * S + 512 * (q_ + 1)],
                start=True, stop=True), "te")
        tev = B.val("te")
        if grp >= ngrp - 2:
            j2 = grp - (ngrp - 2)
            B.wait("vector", "te", tev)
            B.ins("vector", lambda e, s_=sp, j_=j2: e.tensor_add(
                s_[:, 0:1024], s_[:, 0:1024],
                c_masks[:, 1024 * j_:1024 * (j_ + 1)]), "ve")
            B.wait("scalar", "ve", B.val("ve"))
        else:
            B.wait("scalar", "te", tev)
        B.wait("scalar", "pt", pt_marks[gidx % 2])
        B.ins("scalar", lambda e, s_=sp, g=gidx: e.activation(
            pt_bufs[g % 2][:, 0:1024], s_[:, 0:1024], AF.Exp,
            bias=c_eps[:, 1:2], scale=SCALING), "se")
        se_marks[gidx % 2] = B.val("se")
        if gidx >= 1:
            B.wait("tensor", "se", B.val("se") - 1)
            emit_av(gidx - 1)
        if gidx % 4 == 0 and gidx > 0 and gidx // 4 + 1 < 64:
            emit_p2a_load(gidx // 4 + 1)
        if gidx % 4 == 2 and gidx // 4 < 64:
            emit_p2a_tern(gidx // 4)
    B.wait("tensor", "se", B.val("se"))
    emit_av(len(sched) - 1)

    B.barrier()
    p2b.close()
    p2scope.close()
    if stop == "p2":
        return _finish(K, tap_list, taps, locals())

    p45scope = K.scope()
    xq2T = K.sb("xq2T", (128, 16 * TOK), BF16, p45scope)
    p34scope = K.scope()
    x2_sb = K.sb("x2_sb", (128, 8 * HID), F32, p34scope)
    p3scope = K.scope()
    av_fm = K.sb("av_fm", (128, 16 * TOK), BF16, p3scope)
    a2a_ld = [K.sb(f"a2ald{i}", (128, 4 * TOK), BF16, p3scope) for i in range(2)]
    wo_str = [K.sb(f"wostr{i}", (128, 16 * 256), F32, p3scope) for i in range(2)]
    wo_eff = K.sb("wo_eff", (128, 16 * 256), BF16, p3scope)
    lora_o = K.sb("lora_o", (8, HID), F32, p3scope)
    lora_ob = K.sb("lora_ob", (8, 256), F32, p3scope)

    S_lo2 = B.stream("Slo2", 1)
    tk_oa = S_lo2.dma("sync", lora_o[:, 0:HID], oA_h)
    S_a2.wait_all("gpsimd")
    def _a2a(e):
        with allow_ici_all_to_all():
            return e.collective_compute(
                "AllToAll", ALU.bypass, replica_groups=[list(range(8))],
                ins=[a2a_in.ap().opt()], outs=[a2a_out.ap().opt()])
    B.ins("gpsimd", _a2a, "cc")
    B.wait("sync", "cc", B.val("cc"))
    S_al = B.stream("Sal", 1)
    for i in range(4):
        if i >= 1:
            B.wait("sync", "ve", B.val("ve"))
        for hc4 in range(4):
            S_al.dma("sync", a2a_ld[0][:, 1024 * hc4:1024 * (hc4 + 1)],
                     a2a_out.ap()[i, 128 * hc4:128 * (hc4 + 1), :])
            tk_al = S_al.dma("sync", a2a_ld[1][:, 1024 * hc4:1024 * (hc4 + 1)],
                             a2a_out.ap()[i + 4, 128 * hc4:128 * (hc4 + 1), :])
        S_al.wait("vector", tk_al)
        B.ins("vector", lambda e, i_=i: e.tensor_scalar(
            a2a_ld[1][:, :], a2a_ld[1][:, :], c_gates[:, i_ + 4:i_ + 5], None,
            ALU.mult), "ve")
        B.wait("vector", "ve")
        B.ins("vector", lambda e, i_=i: e.scalar_tensor_tensor(
            av_fm[:, 4096 * i_:4096 * (i_ + 1)], a2a_ld[0][:, :],
            c_gates[:, i_:i_ + 1], a2a_ld[1][:, :], ALU.mult, ALU.add), "ve")
    ve_av = B.val("ve")

    S_xi = B.stream("Sxi", 1)
    for tt in range(8):
        tk_x2i = S_xi.dma("sync", x2_sb[:, 2048 * tt:2048 * (tt + 1)],
                          xown_h[128 * tt:128 * (tt + 1), :])

    B.wait("tensor", "ve", ve_av)
    S_wb = B.stream("Swb", 2)
    wo_dn_marks = [0, 0]
    for ocb in range(8):
        wsb = wo_str[ocb % 2]
        if ocb >= 2:
            B.wait("sync", "qv", B.val("qv") - 1)
        B.wait("sync", "fz", B.val("fz"))
        S_wb.dma("sync", lora_ob[:, 0:256], oBT_h[0:8, 256 * ocb:256 * (ocb + 1)], ocb)
        for hc in range(16):
            tk_wo = S_wb.dma("sync", wsb[:, 256 * hc:256 * (hc + 1)],
                             woT_h[128 * hc:128 * (hc + 1), 256 * ocb:256 * (ocb + 1)], ocb)
        B.wait("tensor", "dn", B.val("dn"))
        S_wb.wait("tensor", tk_wo)
        if ocb == 0:
            S_lo2.wait("tensor", tk_oa)
        for hc in range(16):
            if not (ocb == 0 and hc < 2):
                B.wait("tensor", "fz", B.val("fz") - 1)
            B.ins("tensor", lambda e, h_=hc: e.matmul(
                PA[h_ % 2][:, 0:256], lora_o[:, 128 * h_:128 * (h_ + 1)],
                lora_ob[:, 0:256], start=True, stop=True), "te")
            tev = B.val("te")
            B.wait("vector", "te", tev)
            B.ins("vector", lambda e, h_=hc, w=wsb: e.scalar_tensor_tensor(
                wo_eff[:, 256 * h_:256 * (h_ + 1)], PA[h_ % 2][:, 0:256], 2.0,
                w[:, 256 * h_:256 * (h_ + 1)], ALU.mult, ALU.add), "ve")
            B.ins("vector", lambda e: e.engine_nop(), "fz")
        ve_woe = B.val("ve")
        B.wait("tensor", "ve", ve_woe)
        B.ins("vector", lambda e: e.engine_nop(), "qv")
        for tt in range(8):
            pp = PA[tt % 2]
            B.wait("tensor", "dn", wo_dn_marks[tt % 2])
            for hc in range(16):
                B.ins("tensor", lambda e, h_=hc, t=tt, pb=pp: e.matmul(
                    pb[:, 0:256],
                    av_fm[:, 1024 * h_ + 128 * t:1024 * h_ + 128 * (t + 1)],
                    wo_eff[:, 256 * h_:256 * (h_ + 1)],
                    start=(h_ == 0), stop=(h_ == 15)), "te")
            tev = B.val("te")
            B.wait("vector", "te", tev)
            if ocb == 0 and tt == 0:
                S_xi.wait("vector", tk_x2i)
            B.ins("vector", lambda e, t=tt, o=ocb, pb=pp: e.tensor_add(
                x2_sb[:, 2048 * t + 256 * o:2048 * t + 256 * (o + 1)],
                x2_sb[:, 2048 * t + 256 * o:2048 * t + 256 * (o + 1)],
                pb[:, 0:256]), "ve")
            B.ins("vector", lambda e: e.engine_nop(), "dn")
            wo_dn_marks[tt % 2] = B.val("dn")
    ve_x2 = B.val("ve")
    S_xs = B.stream("Sxs", 1)
    B.wait("gpsimd", "ve", ve_x2)
    for tt in range(8):
        S_xs.dma("gpsimd", x2_sp.ap()[128 * tt:128 * (tt + 1), :],
                 x2_sb[:, 2048 * tt:2048 * (tt + 1)])

    B.barrier()
    p3scope.close()
    if stop == "p3":
        return _finish(K, tap_list, taps, locals())

    p4scope = K.scope()
    xq_f32 = K.sb("xq_f32", (128, HID), F32, p4scope)
    xq_i32 = K.sb("xq_i32", (128, HID), I32, p4scope)
    xq_tok = K.sb("xq_tok", (128, HID), BF16, p4scope)

    for tt in range(8):
        xsl = x2_sb[:, 2048 * tt:2048 * (tt + 1)]
        lc = (tt % 2) * 8
        B.ins("vector", lambda e, b=xsl, c=lc: e.reduce_sum(
            lnstat[:, c + 0:c + 1], b, axis=AX.X), "ve")
        B.ins("scalar", lambda e, b=xsl, c=lc: e.activation(
            sq_scr[:, :], b, AF.Square, accum_out=lnstat[:, c + 1:c + 2]), "se")
        B.wait("vector", "se", B.val("se"))
        B.wait("vector", "ve", B.val("ve"))
        B.ins("vector", lambda e, c=lc: e.tensor_scalar(
            lnstat[:, c + 2:c + 3], lnstat[:, c + 0:c + 1], INV2048, None, ALU.mult), "ve")
        B.wait("vector", "ve")
        B.ins("vector", lambda e, c=lc: e.tensor_mul(
            lnstat[:, c + 3:c + 4], lnstat[:, c + 2:c + 3], lnstat[:, c + 2:c + 3]), "ve")
        B.wait("vector", "ve")
        B.ins("vector", lambda e, c=lc: e.scalar_tensor_tensor(
            lnstat[:, c + 4:c + 5], lnstat[:, c + 1:c + 2], INV2048,
            lnstat[:, c + 3:c + 4], ALU.mult, ALU.subtract), "ve")
        B.wait("scalar", "ve", B.val("ve"))
        B.ins("scalar", lambda e, c=lc: e.activation(
            lnstat[:, c + 5:c + 6], lnstat[:, c + 4:c + 5], AF.Sqrt, bias=c_eps[:, 0:1]), "se")
        B.wait("vector", "se", B.val("se"))
        B.ins("vector", lambda e, c=lc: e.reciprocal(
            lnstat[:, c + 6:c + 7], lnstat[:, c + 5:c + 6]), "ve")
        B.wait("vector", "ve")
        B.ins("vector", lambda e, b=xsl, c=lc: e.tensor_scalar(
            xq_f32[:, :], b, lnstat[:, c + 2:c + 3], lnstat[:, c + 6:c + 7],
            ALU.subtract, ALU.mult), "ve")
        B.wait("vector", "ve")
        B.ins("vector", lambda e, t=tt: e.reduce_max(
            qscal[:, 8 * t + 6:8 * t + 7], xq_f32[:, :], axis=AX.X,
            apply_absolute_value=True), "ve")
        B.wait("vector", "ve")
        B.ins("vector", lambda e, t=tt: e.tensor_scalar(
            qscal[:, 8 * t + 0:8 * t + 1], qscal[:, 8 * t + 6:8 * t + 7],
            EPS, None, ALU.max), "ve")
        B.wait("vector", "ve")
        B.ins("vector", lambda e, t=tt: e.reciprocal(
            qscal[:, 8 * t + 6:8 * t + 7], qscal[:, 8 * t + 0:8 * t + 1]), "ve")
        B.wait("vector", "ve")
        B.ins("vector", lambda e, t=tt: e.tensor_scalar(
            qscal[:, 8 * t + 1:8 * t + 2], qscal[:, 8 * t + 6:8 * t + 7],
            128.0, None, ALU.mult), "ve")
        B.wait("vector", "ve")
        B.ins("vector", lambda e, t=tt: e.tensor_scalar(
            xq_f32[:, :], xq_f32[:, :], qscal[:, 8 * t + 1:8 * t + 2], None,
            ALU.mult), "ve")
        B.wait("vector", "ve")
        B.ins("vector", lambda e: e.tensor_copy(xq_i32[:, :], xq_f32[:, :]), "ve")
        B.wait("vector", "ve")
        B.ins("vector", lambda e: e.tensor_copy(xq_tok[:, :], xq_i32[:, :]), "ve")
        B.wait("vector", "ve")
        B.ins("vector", lambda e, t=tt: e.tensor_scalar(
            xq_tok[:, :], xq_tok[:, :], 127.0, None, ALU.min, ALU.add,
            accum_out=qscal[:, 8 * t + 7:8 * t + 8]), "ve")
        B.wait("vector", "ve")
        B.ins("vector", lambda e, t=tt: e.tensor_scalar(
            qscal[:, 8 * t + 6:8 * t + 7], qscal[:, 8 * t + 0:8 * t + 1],
            1.0 / 128.0, None, ALU.mult), "ve")
        B.wait("vector", "ve")
        B.ins("vector", lambda e, t=tt: e.tensor_mul(
            qscal[:, 8 * t + 2:8 * t + 3], qscal[:, 8 * t + 6:8 * t + 7],
            st_row[:, 4:5]), "ve")
        B.ins("vector", lambda e, t=tt: e.tensor_mul(
            qscal[:, 8 * t + 4:8 * t + 5], qscal[:, 8 * t + 6:8 * t + 7],
            st_row[:, 5:6]), "ve")
        B.wait("vector", "ve")
        B.ins("vector", lambda e, t=tt: e.scalar_tensor_tensor(
            qscal[:, 8 * t + 3:8 * t + 4], qscal[:, 8 * t + 2:8 * t + 3], -1.0,
            qscal[:, 8 * t + 7:8 * t + 8], ALU.mult, ALU.mult), "ve")
        B.ins("vector", lambda e, t=tt: e.scalar_tensor_tensor(
            qscal[:, 8 * t + 5:8 * t + 6], qscal[:, 8 * t + 4:8 * t + 5], -1.0,
            qscal[:, 8 * t + 7:8 * t + 8], ALU.mult, ALU.mult), "ve")
        vev = B.val("ve")
        B.wait("tensor", "ve", vev)
        for bb in range(2):
            for j in range(8):
                ic = bb * 8 + j
                B.ins("tensor", lambda e, i=ic, b=bb, j_=j: e.transpose(
                    PB[b][:, 128 * j_:128 * (j_ + 1)],
                    xq_tok[:, 128 * i:128 * (i + 1)], c_identb[:, :]), "te")
            tev = B.val("te")
            B.wait("vector", "te", tev)
            B.ins("vector", lambda e, t=tt, b=bb: e.tensor_copy(
                r3(xq2T, 16, TOK)[:, 8 * b:8 * (b + 1), 128 * t:128 * (t + 1)],
                r3(PB[b], 8, 128)[:, :, :]), "ve")
            vev = B.val("ve")
            B.wait("tensor", "ve", vev)

    B.barrier()
    p4scope.close()
    p34scope.close()
    if stop == "p4":
        return _finish(K, tap_list, taps, locals())

    p5scope = K.scope()
    gu_str = [K.sb(f"gustr{i}", (128, 16 * 512), F32, p5scope) for i in range(2)]
    gu_q = [K.sb(f"guq{i}", (128, 16 * 512), BF16, p5scope) for i in range(2)]
    silu_all = K.sb("silu_all", (128, 8 * 512), F32, p5scope)
    up_sb2 = [K.sb(f"upsb{i}", (128, 512), F32, p5scope) for i in range(2)]
    int_sb = [K.sb(f"intsb{i}", (128, 512), BF16, p5scope) for i in range(2)]

    S_gu = B.stream("Sgu", 2)
    S_is = B.stream("Sis", 2)
    p5qv_marks = [None, None]
    int_marks = [None, None]
    silu_rd = [0] * 8
    tern_done = {}

    def blk_src(b):
        return (wgT_h if b % 2 == 0 else wuT_h), b // 2

    tern_rd = {}
    gu_te_marks = [0, 0]

    def emit_load(b):
        wsrc, ocb = blk_src(b)
        eng = "sync" if b % 2 == 0 else "scalar"
        B.wait(eng, "dn", tern_rd.get(b - 2, 0))
        return S_gu.dma(eng, gu_str[b % 2][:, :], wsrc[ocb, :, :], b)

    def emit_tern(b):
        tc, nc_ = (8, 11) if b % 2 == 0 else (9, 12)
        wsb, wqb = gu_str[b % 2], gu_q[b % 2]
        S_gu.wait("vector", tern_done[b])
        B.wait("vector", "te", gu_te_marks[b % 2])
        B.ins("vector", lambda e, w=wsb, t_=tc: e.tensor_scalar(
            sq_scr[:, 0:2048], w[:, 0:2048], st_row[:, t_:t_ + 1], None,
            ALU.is_ge), "ve")
        B.ins("vector", lambda e, w=wsb, t_=tc: e.tensor_scalar(
            xq_f32g[:, 0:2048], w[:, 2048:4096], st_row[:, t_:t_ + 1], None,
            ALU.is_ge), "ve")
        B.wait("vector", "ve")
        B.ins("vector", lambda e, w=wsb, q=wqb, n_=nc_: e.scalar_tensor_tensor(
            q[:, 0:2048], w[:, 0:2048], st_row[:, n_:n_ + 1],
            sq_scr[:, 0:2048], ALU.is_gt, ALU.add), "ve")
        B.ins("vector", lambda e, w=wsb, q=wqb, n_=nc_: e.scalar_tensor_tensor(
            q[:, 2048:4096], w[:, 2048:4096], st_row[:, n_:n_ + 1],
            xq_f32g[:, 0:2048], ALU.is_gt, ALU.add), "ve")
        B.wait("vector", "ve")
        B.ins("vector", lambda e, w=wsb, t_=tc: e.tensor_scalar(
            sq_scr[:, 0:2048], w[:, 4096:6144], st_row[:, t_:t_ + 1], None,
            ALU.is_ge), "ve")
        B.ins("vector", lambda e, w=wsb, t_=tc: e.tensor_scalar(
            xq_f32g[:, 0:2048], w[:, 6144:8192], st_row[:, t_:t_ + 1], None,
            ALU.is_ge), "ve")
        B.wait("vector", "ve")
        B.ins("vector", lambda e, w=wsb, q=wqb, n_=nc_: e.scalar_tensor_tensor(
            q[:, 4096:6144], w[:, 4096:6144], st_row[:, n_:n_ + 1],
            sq_scr[:, 0:2048], ALU.is_gt, ALU.add), "ve")
        B.ins("vector", lambda e, w=wsb, q=wqb, n_=nc_: e.scalar_tensor_tensor(
            q[:, 6144:8192], w[:, 6144:8192], st_row[:, n_:n_ + 1],
            xq_f32g[:, 0:2048], ALU.is_gt, ALU.add), "ve")
        B.wait("vector", "ve")
        B.ins("vector", lambda e: e.engine_nop(), "dn")
        tern_rd[b] = B.val("dn")
        return B.val("ve")

    xq_f32g = K.sb("xq_f32g", (128, 2048), F32, p5scope)

    tern_done[0] = emit_load(0)
    tern_done[1] = emit_load(1)
    tern_ve = {}
    tern_ve[0] = emit_tern(0)
    for b in range(32):
        gate = b % 2 == 0
        wqb = gu_q[b % 2]
        if b + 2 < 32:
            tern_done[b + 2] = emit_load(b + 2)
        if b + 1 < 32:
            tern_ve[b + 1] = emit_tern(b + 1)
        B.wait("tensor", "ve", tern_ve[b])
        for tt in range(8):
            po = 512 * (tt % 2)
            PAx = PA[0] if gate else PA[1]
            if p5qv_marks[tt % 2] is not None:
                B.wait("tensor", *p5qv_marks[tt % 2])
            for hc in range(16):
                B.ins("tensor", lambda e, h_=hc, t=tt, q=wqb, o=po, pb=PAx: e.matmul(
                    pb[:, o:o + 512],
                    r3(xq2T, 16, TOK)[:, h_, 128 * t:128 * (t + 1)],
                    q[:, 512 * h_:512 * (h_ + 1)],
                    start=(h_ == 0), stop=(h_ == 15)), "te")
            tev = B.val("te")
            if gate:
                B.wait("scalar", "te", tev)
                B.wait("scalar", "ve", silu_rd[tt])
                B.ins("scalar", lambda e, t=tt, o=po: e.activation(
                    silu_all[:, 512 * t:512 * (t + 1)], PA[0][:, o:o + 512], AF.Silu,
                    bias=qscal[:, 8 * t + 3:8 * t + 4],
                    scale=qscal[:, 8 * t + 2:8 * t + 3]), "se")
                p5qv_marks[tt % 2] = ("se", B.val("se"))
            else:
                B.wait("vector", "te", tev)
                B.ins("vector", lambda e, t=tt, o=po: e.tensor_scalar(
                    up_sb2[t % 2][:, :], PA[1][:, o:o + 512],
                    qscal[:, 8 * t + 4:8 * t + 5],
                    qscal[:, 8 * t + 5:8 * t + 6], ALU.mult, ALU.add), "ve")
                p5qv_marks[tt % 2] = ("ve", B.val("ve"))
                B.wait("vector", "ve")
                B.wait("vector", "se", B.val("se"))
                if int_marks[tt % 2] is not None:
                    S_is.wait("vector", int_marks[tt % 2])
                B.ins("vector", lambda e, t=tt: e.tensor_mul(
                    int_sb[t % 2][:, :], silu_all[:, 512 * t:512 * (t + 1)],
                    up_sb2[t % 2][:, :]), "ve")
                silu_rd[tt] = B.val("ve")
                vev = B.val("ve")
                B.wait("sync", "ve", vev)
                int_marks[tt % 2] = S_is.dma(
                    "sync", inter_sp.ap()[128 * tt:128 * (tt + 1),
                                          512 * (b // 2):512 * (b // 2 + 1)],
                    int_sb[tt % 2][:, :], tt)
        gu_te_marks[b % 2] = B.val("te")

    B.barrier()
    p5scope.close()
    p45scope.close()
    if stop == "p5":
        return _finish(K, tap_list, taps, locals())

    p6scope = K.scope()
    intr_ld = K.sb("intr_ld", (128, IFF), BF16, p6scope)
    itT = K.sb("itT", (128, 64 * 128), BF16, p6scope)
    fwX = [K.sb(f"fwx{i}", (128, 64 * 128), BF16, p6scope) for i in range(2)]
    fw_scr = K.sb("fw_scr", (128, 64 * 128), BF16, p6scope)
    f32scr = K.sb("f32scr", (128, 512), F32, p6scope)
    i32scr = K.sb("i32scr", (128, 512), I32, p6scope)
    m2st = K.sb("m2st", (128, 32), F32, p6scope)
    xqf2 = [K.sb(f"xqf{i}", (128, 64 * 256), BF16, p6scope) for i in range(2)]
    wd_str = [K.sb(f"wdstr{i}", (128, 16 * 256), BF16, p6scope) for i in range(3)]
    ysg = K.sb("ysg", (128, 2 * HID), F32, p6scope)

    S_is.wait_all("gpsimd")
    S_il = B.stream("Sil", 1)
    S_wl = B.stream("Swl", 3)
    S_yi = B.stream("Syi", 1)
    tp_mark = [0]
    pbm = [None, None]
    px_mark = [0]
    s1m = [None, None]
    p6dn_marks = [0, 0]
    cmark = [0, 0]
    qdone = [None, None]
    afin_mark = [0, 0]
    pr_mark = [0]
    tail_te = [0]
    ysg_mark = [0]
    B.wait("tensor", "ve", B.val("ve"))

    def emit_AH(t):
        B.wait("gpsimd", "te", tp_mark[0])
        tk_il = S_il.dma("gpsimd", intr_ld[:, :],
                         inter_sp.ap()[128 * t:128 * (t + 1), :])
        S_il.wait("tensor", tk_il)
        evse = []
        for bb in range(8):
            if pbm[bb % 2] is not None:
                B.wait("tensor", *pbm[bb % 2])
            if bb == 0:
                B.wait("tensor", "px", px_mark[0])
            for j in range(8):
                ic = bb * 8 + j
                B.ins("tensor", lambda e, i=ic, pb=bb % 2, j_=j: e.transpose(
                    PB[pb][:, 128 * j_:128 * (j_ + 1)],
                    intr_ld[:, 128 * i:128 * (i + 1)], c_identb[:, :]), "te")
            tev = B.val("te")
            if bb == 7:
                tp_mark[0] = tev
            B.wait("scalar", "te", tev)
            B.ins("scalar", lambda e, b=bb: e.copy(
                itT[:, 1024 * b:1024 * (b + 1)], PB[b % 2][:, 0:1024]), "se")
            pbm[bb % 2] = ("se", B.val("se"))
            evse.append(B.val("se"))
        fb = fwX[t % 2]
        if qdone[t % 2] is not None:
            B.wait("scalar", *qdone[t % 2])
        for m in range(16):
            po = 512 * (m % 2)
            if s1m[m % 2] is not None:
                B.wait("tensor", *s1m[m % 2])
            B.wait("tensor", "se", evse[m // 2])
            B.ins("tensor", lambda e, m_=m, o=po: e.matmul(
                PA[0][:, o:o + 512], c_had[:, :], itT[:, 512 * m_:512 * (m_ + 1)],
                start=True, stop=True), "te")
            tev = B.val("te")
            B.wait("scalar", "te", tev)
            B.ins("scalar", lambda e, m_=m, o=po, f=fb: e.copy(
                f[:, 512 * m_:512 * (m_ + 1)], PA[0][:, o:o + 512]), "se")
            s1m[m % 2] = ("se", B.val("se"))

    def bf_steps(t):
        fb = fwX[t % 2]
        steps = []

        def mk(s):
            def go():
                if s == 0:
                    B.wait("vector", "se", B.val("se"))
                    B.wait("vector", "te", tail_te[0])
                src_ = fb if s % 2 == 0 else fw_scr
                dst_ = fw_scr if s % 2 == 0 else fb
                ps_ = 1 << s
                nh_ = 64 // (2 * ps_)
                sa = src_[:, :].rearrange("p (hi two lo u) -> p hi two lo u",
                                          hi=nh_, two=2, lo=ps_)
                da = dst_[:, :].rearrange("p (hi two lo u) -> p hi two lo u",
                                          hi=nh_, two=2, lo=ps_)
                B.ins("vector", lambda e, s_=sa, d=da: e.tensor_add(
                    d[:, :, 0, :, :], s_[:, :, 0, :, :], s_[:, :, 1, :, :]), "ve")
                B.ins("vector", lambda e, s_=sa, d=da: e.tensor_sub(
                    d[:, :, 1, :, :], s_[:, :, 0, :, :], s_[:, :, 1, :, :]), "ve")
                B.wait("vector", "ve")
            return go
        for s in range(6):
            steps.append(mk(s))
        return steps

    def abs_step(t):
        def go():
            fb = fwX[t % 2]
            B.wait("scalar", "ve", B.val("ve"))
            B.ins("scalar", lambda e, f=fb: e.activation(
                fw_scr[:, :], f[:, :], AF.Abs), "se")
        return go

    def tmax_step(t):
        def go():
            B.wait("vector", "se", B.val("se"))
            ln = 32 * 128
            while ln >= 128:
                B.ins("vector", lambda e, l=ln: e.tensor_max(
                    fw_scr[:, 0:l], fw_scr[:, 0:l], fw_scr[:, l:2 * l]), "ve")
                B.wait("vector", "ve")
                ln //= 2
        return go

    def ata_tensor(t):
        def go():
            mb = 16 * ((t // 2) % 2) + 8 * (t % 2)
            B.wait("tensor", "ve", B.val("ve"))
            B.ins("tensor", lambda e: e.transpose(
                PB[0][:, 0:128], fw_scr[:, 0:128], c_identb[:, :]), "te")
            tail_te[0] = B.val("te")
            B.wait("vector", "te", B.val("te"))
            B.ins("vector", lambda e, m_=mb: e.reduce_max(
                m2st[:, m_ + 0:m_ + 1], PB[0][:, 0:128], axis=AX.X), "ve")
            B.ins("vector", lambda e: e.engine_nop(), "px")
            px_mark[0] = B.val("px")
            B.wait("vector", "ve")
            B.ins("vector", lambda e, m_=mb: e.tensor_scalar(
                m2st[:, m_ + 0:m_ + 1], m2st[:, m_ + 0:m_ + 1], EPSRAW, None,
                ALU.max), "ve")
            B.wait("vector", "ve")
            B.ins("vector", lambda e, m_=mb: e.reciprocal(
                m2st[:, m_ + 1:m_ + 2], m2st[:, m_ + 0:m_ + 1]), "ve")
            B.wait("vector", "ve")
            B.ins("vector", lambda e, m_=mb: e.tensor_scalar(
                m2st[:, m_ + 1:m_ + 2], m2st[:, m_ + 1:m_ + 2], 128.0, None,
                ALU.mult), "ve")
            B.wait("vector", "ve")
            B.ins("vector", lambda e, m_=mb: e.tensor_scalar(
                m2st[:, m_ + 2:m_ + 3], m2st[:, m_ + 0:m_ + 1],
                RSQ8192 / 128.0, None, ALU.mult), "ve")
            B.wait("vector", "ve")
            B.ins("vector", lambda e, m_=mb: e.tensor_mul(
                m2st[:, m_ + 2:m_ + 3], m2st[:, m_ + 2:m_ + 3], st_row[:, 6:7]), "ve")
            vev = B.val("ve")
            B.wait("tensor", "ve", vev)
            B.ins("tensor", lambda e, m_=mb: e.transpose(
                PC[0:1, 384:512], m2st[:, m_ + 1:m_ + 2], c_identf[:, :]), "te")
            B.wait("vector", "te", B.val("te"))
            B.ins("vector", lambda e: e.tensor_copy(srow[0:1, 0:128], PC[0:1, 384:512]), "ve")
            B.wait("tensor", "ve", B.val("ve"))
            B.ins("tensor", lambda e: e.matmul(
                PC[:, 128:256], c_onesrow[0:1, :], srow[0:1, 0:128],
                start=True, stop=True), "te")
            B.wait("vector", "te", B.val("te"))
            B.wait("vector", "te", cmark[(t // 2) % 2])
        return go

    def quant_steps(t):
        fb = fwX[t % 2]
        xqf = xqf2[(t // 2) % 2]
        steps = []

        def mk(lo, hi, last):
            def go():
                for sl4 in range(lo, hi):
                    fslice = fb[:, 512 * sl4:512 * (sl4 + 1)]
                    B.ins("vector", lambda e, f=fslice: e.tensor_tensor(
                        f32scr[:, :].rearrange("p (i u) -> p i u", i=4),
                        f.rearrange("p (i u) -> p i u", i=4),
                        bc3(PC[:, 128:256], 4), ALU.mult), "ve")
                    B.wait("vector", "ve")
                    B.ins("vector", lambda e: e.tensor_copy(i32scr[:, :], f32scr[:, :]), "ve")
                    B.wait("vector", "ve")
                    B.ins("vector", lambda e, tl=t % 2, s4=sl4, x=xqf: e.tensor_scalar(
                        r3(x, 64, 256)[:, 4 * s4:4 * (s4 + 1), 128 * tl:128 * (tl + 1)],
                        i32scr[:, :].rearrange("p (i u) -> p i u", i=4),
                        127.0, None, ALU.min), "ve")
                    B.wait("vector", "ve")
                if last:
                    qdone[t % 2] = ("ve", B.val("ve"))
            return go
        for k in range(6):
            lo = (16 * k) // 6
            hi = (16 * (k + 1)) // 6
            steps.append(mk(lo, hi, k == 5))
        return steps

    def emit_ATb(t):
        xqf = xqf2[(t // 2) % 2]
        B.wait("tensor", *qdone[t % 2])
        B.wait("tensor", "ve", pr_mark[0])
        for icm in range(64):
            B.ins("tensor", lambda e, i=icm, tl=t % 2, x=xqf: e.matmul(
                PR[0:1, 0:128], c_onesb[:, 0:1],
                r3(x, 64, 256)[:, i, 128 * tl:128 * (tl + 1)],
                start=(i == 0), stop=(i == 63)), "te")
        B.wait("vector", "te", B.val("te"))
        B.ins("vector", lambda e, c=256 * ((t // 2) % 2) + 128 * (t % 2): e.tensor_copy(
            sfrows[0:1, c:c + 128], PR[0:1, 0:128]), "ve")
        pr_mark[0] = B.val("ve")
        if t % 2 == 1:
            afin_mark[(t // 2) % 2] = B.val("ve")

    def emit_ATa(t):
        for f in bf_steps(t):
            f()
        abs_step(t)()
        tmax_step(t)()
        ata_tensor(t)()
        for f in quant_steps(t):
            f()

    wd_mk = [0, 0, 0]

    def emit_Cstrip(sg, strip):
        xqf = xqf2[sg % 2]
        mbase = 16 * (sg % 2)
        ocb, qh = strip // 4, strip % 4
        slot = (strip // 2) % 2 if strip % 2 == 0 else 2
        wb = wd_str[slot]
        qeng = "sync" if strip % 2 == 0 else "scalar"
        if strip == 0:
            B.wait("gpsimd", "fin", ysg_mark[0])
            if sg == 0:
                S_xs.wait_all("gpsimd")
            for tl in range(2):
                tk_yi = S_yi.dma("gpsimd", ysg[:, 2048 * tl:2048 * (tl + 1)],
                                 x2_sp.ap()[256 * sg + 128 * tl:
                                            256 * sg + 128 * (tl + 1), :])
            S_yi.wait("vector", tk_yi)
            B.wait("tensor", "ve", afin_mark[sg % 2])
        B.wait(qeng, "te", wd_mk[slot])
        if sg == 0 and strip < 2:
            S_do.wait_all(qeng)
        tk_wl = S_wl.dma(
            qeng, wb[:, :],
            wd_tern.ap()[ocb, :, 16 * qh:16 * (qh + 1), :], slot)
        S_wl.wait("tensor", tk_wl)
        for tl in range(2):
            if qh == 0:
                B.wait("tensor", "dn", p6dn_marks[tl])
            for icm in range(16):
                i = 16 * qh + icm
                B.ins("tensor", lambda e, i_=i, icm_=icm, tl_=tl, w=wb,
                      o=512 * (tl % 2), x=xqf: e.matmul(
                    PA[1][:, o:o + 256],
                    r3(x, 64, 256)[:, i_, 128 * tl_:128 * (tl_ + 1)],
                    w[:, 256 * icm_:256 * (icm_ + 1)],
                    start=(i_ == 0), stop=False), "te")
            if qh == 3:
                B.ins("tensor", lambda e, o=512 * (tl % 2),
                      c=256 * (sg % 2) + 128 * tl: e.matmul(
                    PA[1][:, o:o + 256], sfrows[0:1, c:c + 128],
                    c_negrow[0:1, 0:256], start=False, stop=True), "te")
                tev = B.val("te")
                B.wait("vector", "te", tev)
                B.ins("vector", lambda e, t=tl, o=ocb, po=512 * (tl % 2),
                      m_=mbase + 8 * tl: e.scalar_tensor_tensor(
                    ysg[:, 2048 * t + 256 * o:2048 * t + 256 * (o + 1)],
                    PA[1][:, po:po + 256], m2st[:, m_ + 2:m_ + 3],
                    ysg[:, 2048 * t + 256 * o:2048 * t + 256 * (o + 1)],
                    ALU.mult, ALU.add), "ve")
                B.ins("vector", lambda e: e.engine_nop(), "dn")
                p6dn_marks[tl] = B.val("dn")
        wd_mk[slot] = B.val("te")
        if strip == 31:
            cmark[sg % 2] = B.val("te")
            ve_y = B.val("ve")
            B.wait("gpsimd", "ve", ve_y)
            for tl in range(2):
                B.dma("gpsimd", y_h[256 * sg + 128 * tl:256 * sg + 128 * (tl + 1), :],
                      ysg[:, 2048 * tl:2048 * (tl + 1)], "fin")
            ysg_mark[0] = B.val("fin")

    emit_AH(0); emit_ATa(0); emit_ATb(0)
    emit_AH(1); emit_ATa(1); emit_ATb(1)
    emit_AH(2)
    for sg in range(4):
        t2, t3 = 2 * sg + 2, 2 * sg + 3
        plan = {}
        if sg < 3:
            bs2, qs2 = bf_steps(t2), quant_steps(t2)
            bs3, qs3 = bf_steps(t3), quant_steps(t3)
            for i in range(6):
                plan[i] = [bs2[i]]
                plan[9 + i] = [qs2[i]]
                plan[16 + i] = [bs3[i]]
                plan[25 + i] = [qs3[i]]
            plan[6] = [abs_step(t2)]
            plan[7] = [tmax_step(t2)]
            plan[8] = [ata_tensor(t2)]
            plan[15] = [lambda: emit_ATb(t2), lambda: emit_AH(t3)]
            plan[22] = [abs_step(t3)]
            plan[23] = [tmax_step(t3)]
            plan[24] = [ata_tensor(t3)]
            plan[31] = [lambda: emit_ATb(t3)]
            if sg < 2:
                plan[31].append(lambda: emit_AH(t2 + 2))
        for strip in range(32):
            emit_Cstrip(sg, strip)
            for f in plan.get(strip, ()):
                f()
    B.wait("sync", "fin", B.val("fin"))

    return _finish(K, tap_list, taps, locals())


def _finish(K, tap_list, taps, loc):
    B = K
    for (tname, lname, shape, dt) in taps:
        h = loc.get(lname)
        if h is None:
            continue
        dst = K.nc.dram_tensor(tname, list(shape), dt, kind="ExternalOutput").ap()
        B.wait("sync", "ve")
        B.wait("sync", "te")
        B.wait("sync", "se")
        B.wait("sync", "st")
        if isinstance(h, list):
            h = h[0]
        B.dma("sync", dst, h[:, 0:shape[1]] if len(shape) == 2 else h[:, :], "fin")
    B.wait("sync", "fin")

    nc = K.nc
    with nc.Block() as block:
        @block.sync
        def _(e):
            for f in K.ops["sync"]:
                f(e)

        @block.tensor
        def _(e):
            for f in K.ops["tensor"]:
                f(e)

        @block.vector
        def _(e):
            for f in K.ops["vector"]:
                f(e)

        @block.scalar
        def _(e):
            for f in K.ops["scalar"]:
                f(e)

        @block.gpsimd
        def _(e):
            for f in K.ops["gpsimd"]:
                f(e)
    for s in reversed(K.open_scopes):
        try:
            s.close()
        except Exception:
            pass
    K.root.close()
    return nc



def _prep_inputs(inputs):
    f32 = lambda a: np.ascontiguousarray(np.asarray(a, dtype=np.float32))
    x = f32(inputs["L0_attn_x"])
    Wq, Wk, Wv, Wo = (f32(inputs["L0_attn_" + n]) for n in ("Wq", "Wk", "Wv", "Wo"))
    qA, qB, kA, kB, vA, vB, oA, oB = (f32(inputs["L0_attn_" + n]) for n in
                                      ("qA", "qB", "kA", "kB", "vA", "vB", "oA", "oB"))
    wg, wu, wd = (f32(inputs["L0_ffn_w_gate"]), f32(inputs["L0_ffn_w_up"]),
                  f32(inputs["L0_ffn_w_down"]))

    permq = _deperm(16)
    permk = _deperm(8)
    WqT = np.ascontiguousarray(Wq.T[:, permq])
    qBT = np.ascontiguousarray(qB.T[:, permq])
    WkT = np.ascontiguousarray(Wk.T[:, permk])
    kBT = np.ascontiguousarray(kB.T[:, permk])
    WvT = np.ascontiguousarray(Wv.T)
    vBT = np.ascontiguousarray(vB.T)
    WoT = np.ascontiguousarray(Wo.T)
    oBT = np.ascontiguousarray(oB.T)
    def tile_gu(wT):
        t = wT.reshape(16, 128, 16, 512)
        return np.ascontiguousarray(t.transpose(2, 1, 0, 3)).reshape(16, 128, 8192)
    wgT = tile_gu(wg.T)
    wuT = tile_gu(wu.T)
    wdTt = wd.T.reshape(64, 128, 8, 256)
    wdT = np.ascontiguousarray(wdTt.transpose(2, 1, 0, 3))

    cosT, sinT = _rope_tables()
    cosT = np.ascontiguousarray(np.concatenate([cosT[:, :S // 2], cosT[:, S // 2:]], axis=0))
    sinT = np.ascontiguousarray(np.concatenate([sinT[:, :S // 2], sinT[:, S // 2:]], axis=0))
    identb = np.eye(128, dtype=np.float32).astype(ml_dtypes.bfloat16)
    identf = np.eye(128, dtype=np.float32)
    had = _hadamard(128).astype(ml_dtypes.bfloat16)
    masks = _diag_masks()

    in_maps = []
    for c in range(8):
        b, g = c // 4, c % 4
        gates = np.zeros((128, 8), np.float32)
        gates[:, 4 * b:4 * b + 4] = 1.0
        in_maps.append({
            "x": x[b],
            "xown": np.ascontiguousarray(x[b, TOK * g:TOK * (g + 1)]),
            "wqT": np.ascontiguousarray(WqT[:, 512 * g:512 * (g + 1)]),
            "wkT": np.ascontiguousarray(WkT[:, 256 * g:256 * (g + 1)]),
            "wvT": np.ascontiguousarray(WvT[:, 256 * g:256 * (g + 1)]),
            "woT": WoT,
            "qA": qA, "kA": kA, "vA": vA, "oA": oA,
            "qBT": np.ascontiguousarray(qBT[:, 512 * g:512 * (g + 1)]),
            "kBT": np.ascontiguousarray(kBT[:, 256 * g:256 * (g + 1)]),
            "vBT": np.ascontiguousarray(vBT[:, 256 * g:256 * (g + 1)]),
            "oBT": oBT,
            "wgT": wgT, "wuT": wuT, "wdT": wdT,
            "wg_s": np.ascontiguousarray(wg.reshape(8, 128, 16384)[c]),
            "wu_s": np.ascontiguousarray(wu.reshape(8, 128, 16384)[c]),
            "wd_s": np.ascontiguousarray(wd.reshape(8, 128, 16384)[c]),
            "identb": identb, "identf": identf,
            "cosT": cosT, "sinT": sinT, "masks": masks, "had": had,
            "a2agate": gates,
        })
    return in_maps


def kernel(**inputs):
    in_maps = _prep_inputs(inputs)
    nc = build_program(taps=list(_DEBUG_TAPS), stop=_DEBUG_STOP[0])
    res = run_bass_kernel_spmd(nc, in_maps, core_ids=list(range(8)))
    kernel._last_results = res
    y = np.zeros((2, S, HID), np.float32)
    for c in range(8):
        b, g = c // 4, c % 4
        y[b, TOK * g:TOK * (g + 1)] = res.results[c]["y"]
    return y


# version 11
# speedup vs baseline: 1.0422x; 1.0033x over previous
import numpy as np
import ml_dtypes

import concourse.bass as bass
import concourse.mybir as mybir
from concourse.bass_utils import run_bass_kernel_spmd
from concourse.aot_env import allow_ici_all_to_all
from concourse.ap import AP

F32 = mybir.dt.float32
BF16 = mybir.dt.bfloat16
I32 = mybir.dt.int32
AF = mybir.ActivationFunctionType
ALU = mybir.AluOpType
AX = mybir.AxisListType

HID = 2048
S = 4096
TOK = 1024
HD = 128
SCALING = float(1.0 / np.sqrt(HD))
SHIFT = -40.0
EPS = 1e-5
IFF = 8192
MASKV = -1.0e30
INV2048 = float(np.float32(1.0) / np.float32(2048.0))
INVWELEM = float(np.float32(1.0) / np.float32(16777216.0))
RSQ8192 = float(np.float32(1.0) / np.float32(np.sqrt(np.float32(8192.0))))
EPSRAW = float(np.float32(1e-5) * np.float32(np.sqrt(np.float32(8192.0))))

_DEBUG_TAPS = []
_DEBUG_STOP = [None]


def _rope_tables():
    inv_freq = (1.0 / 10000.0 ** (np.arange(0, HD, 2, dtype=np.float32) / HD)).astype(np.float32)
    t = np.arange(S, dtype=np.float32)
    fr = t[:, None] * inv_freq[None, :]
    return (np.ascontiguousarray(np.cos(fr).astype(np.float32).T),
            np.ascontiguousarray(np.sin(fr).astype(np.float32).T))


def _deperm(n_heads):
    p = []
    for h in range(n_heads):
        base = h * HD
        p.extend(base + np.arange(0, HD, 2))
        p.extend(base + np.arange(1, HD, 2))
    return np.array(p)


def _hadamard(n):
    H = np.array([[1.0]], dtype=np.float32)
    while H.shape[0] < n:
        H = np.block([[H, H], [H, -H]])
    return H.astype(np.float32)


def _diag_masks():
    m = np.zeros((128, 4 * 512), dtype=np.float32)
    col = np.arange(512)
    rows = np.arange(128)[:, None]
    for j in range(4):
        m[:, 512 * j:512 * (j + 1)] = np.where(col[None, :] >= 128 * j + rows, 0.0, MASKV)
    return m


def bc3(ap2d, rep):
    return AP(ap2d.tensor, ap2d.offset, [ap2d.ap[0], [0, rep], ap2d.ap[1]])


class KB:
    ENGINES = ("sync", "tensor", "vector", "scalar", "gpsimd")

    def __init__(self):
        self.nc = bass.Bass("TRN2", target_bir_lowering=False, debug=False)
        self.ops = {e: [] for e in self.ENGINES}
        self.cnt = {}
        self.sems = {}
        from contextlib import ExitStack
        self.root = ExitStack()
        self.open_scopes = []

    def ins(self, eng, fn, sem=None, k=1):
        if sem is None:
            self.ops[eng].append(lambda e, _f=fn: _f(e))
            return None
        self.cnt[sem] += k
        v = self.cnt[sem]

        def run(e, _f=fn, _s=sem, _k=k):
            _f(e).then_inc(self.sems[_s], _k)
        self.ops[eng].append(run)
        return v

    def dma(self, eng, out, in_, sem="ld"):
        return self.ins(eng, lambda e, o=out, i=in_: e.dma_start(out=o, in_=i), sem, 16)

    def stream(self, name, nbuf=2):
        return Stream(self, name, nbuf)

    def wait(self, eng, sem, v=None):
        if v is None:
            v = self.cnt[sem]
        if v <= 0:
            return
        self.ops[eng].append(lambda e, _s=sem, _v=v: e.wait_ge(self.sems[_s], _v))

    def val(self, sem):
        return self.cnt[sem]

    def barrier(self):
        for eng in self.ENGINES:
            for s in sorted(self.cnt):
                self.wait(eng, s)

    def sb(self, name, shape, dt, scope=None):
        st = scope if scope is not None else self.root
        t = st.enter_context(self.nc.sbuf_tensor(name, list(shape), dt))
        return t

    def psm(self, name, shape, dt):
        return self.root.enter_context(self.nc.psum_tensor(name, list(shape), dt))

    def scope(self):
        from contextlib import ExitStack
        s = ExitStack()
        self.open_scopes.append(s)
        return s


class Stream:

    def __init__(self, K, name, nbuf):
        self.K = K
        self.names = []
        for i in range(nbuf):
            n = f"{name}{i}"
            K.sems[n] = K.root.enter_context(K.nc.semaphore(n))
            K.cnt[n] = 0
            self.names.append(n)
        self.nbuf = nbuf

    def dma(self, eng, out, in_, slot=0):
        n = self.names[slot % self.nbuf]
        self.K.ins(eng, lambda e, o=out, i=in_: e.dma_start(out=o, in_=i), n, 16)
        return (n, self.K.cnt[n])

    def wait(self, eng, token):
        self.K.wait(eng, token[0], token[1])

    def wait_all(self, eng):
        for n in self.names:
            self.K.wait(eng, n, self.K.cnt[n])


def r3(t, a, b):
    return t[:, :].rearrange("p (a b) -> p a b", a=a)


def build_program(taps=(), stop=None):
    K = KB()
    nc = K.nc
    B = K

    def dram_in(name, shape, dt=F32):
        return nc.dram_tensor(name, list(shape), dt, kind="ExternalInput").ap()

    x_h = dram_in("x", (S, HID))
    xown_h = dram_in("xown", (TOK, HID))
    wqT_h = dram_in("wqT", (HID, 512))
    wkT_h = dram_in("wkT", (HID, 256))
    wvT_h = dram_in("wvT", (HID, 256))
    woT_h = dram_in("woT", (HID, HID))
    qA_h = dram_in("qA", (8, HID))
    kA_h = dram_in("kA", (8, HID))
    vA_h = dram_in("vA", (8, HID))
    oA_h = dram_in("oA", (8, HID))
    qBT_h = dram_in("qBT", (8, 512))
    kBT_h = dram_in("kBT", (8, 256))
    vBT_h = dram_in("vBT", (8, 256))
    oBT_h = dram_in("oBT", (8, HID))
    wgT_h = dram_in("wgT", (16, 128, 16 * 512))
    wuT_h = dram_in("wuT", (16, 128, 16 * 512))
    wdT_h = dram_in("wdT", (8, 128, 64, 256))
    wgs_h = dram_in("wg_s", (128, 16384))
    wus_h = dram_in("wu_s", (128, 16384))
    wds_h = dram_in("wd_s", (128, 16384))
    identb_h = dram_in("identb", (128, 128), BF16)
    identf_h = dram_in("identf", (128, 128))
    cos_h = dram_in("cosT", (128, S // 2))
    sin_h = dram_in("sinT", (128, S // 2))
    masks_h = dram_in("masks", (128, 2048))
    had_h = dram_in("had", (128, 128), BF16)
    gates_h = dram_in("a2agate", (128, 8))

    y_h = nc.dram_tensor("y", [TOK, HID], F32, kind="ExternalOutput").ap()

    a2a_in = nc.dram_tensor("a2a_in", [8, 512, TOK], BF16)
    a2a_out = nc.dram_tensor("a2a_out", [8, 512, TOK], BF16)
    st_in = nc.dram_tensor("st_in", [8, 1], F32)
    st_out = nc.dram_tensor("st_out", [8, 1], F32)
    x2_sp = nc.dram_tensor("x2_spill", [TOK, HID], F32)
    wd_tern = nc.dram_tensor("wd_tern", [8, 128, 64, 256], BF16)
    inter_sp = nc.dram_tensor("inter_spill", [TOK, IFF], BF16)

    c_identb = K.sb("c_identb", (128, 128), BF16)
    c_identf = K.sb("c_identf", (128, 128), F32)
    c_had = K.sb("c_had", (128, 128), BF16)
    c_onesb = K.sb("c_onesb", (128, 1), BF16)
    c_onesrow = K.sb("c_onesrow", (1, 128), F32)
    c_negrow = K.sb("c_negrow", (1, 256), F32)
    c_gates = K.sb("c_gates", (128, 8), F32)
    st_row = K.sb("st_row", (128, 16), F32)
    c_eps = K.sb("c_eps", (128, 2), F32)
    statacc = K.sb("statacc", (128, 8), F32)
    stat_tp = K.sb("stat_tp", (128, 8), F32)
    strow1 = K.sb("strow1", (1, 8), F32)
    lnstat = K.sb("lnstat", (128, 64), F32)
    lnsum = K.sb("lnsum", (128, 32), F32)
    lnsq = K.sb("lnsq", (128, 32), F32)
    lnm = K.sb("lnm", (128, 32), F32)
    lnr = K.sb("lnr", (128, 32), F32)
    sq_scr = K.sb("sq_scr", (128, HID), F32)
    qscal = K.sb("qscal", (128, 64), F32)
    srow = K.sb("srow", (1, 128), F32)
    sfrows = K.sb("sfrows", (1, 512), F32)

    PA = [K.psm(f"PA{i}", (128, 1024), F32) for i in range(2)]
    PB = [K.psm(f"PB{i}", (128, 1024), BF16) for i in range(2)]
    PC = K.psm("PC", (128, 512), F32)
    PR = K.psm("PR", (1, 512), F32)

    for n in ("ld", "gd", "te", "ve", "se", "st", "cc", "pt", "fz", "qv", "dn",
              "px", "fin"):
        K.sems[n] = K.root.enter_context(nc.semaphore(n))
        K.cnt[n] = 0

    tap_list = []

    def mark_tap(name, handle, shape):
        tap_list.append((nc.dram_tensor(name, list(shape), handle.dtype if hasattr(handle, "dtype") else F32,
                                        kind="ExternalOutput").ap(), handle, shape))

    S_c = B.stream("Sc", 1)
    for dst, src in ((c_identb, identb_h), (c_identf, identf_h), (c_had, had_h),
                     (c_gates, gates_h)):
        tk_consts = S_c.dma("sync", dst[:, :], src)
    B.ins("vector", lambda e: e.memset(c_onesb[:, :], 1.0), "ve")
    B.ins("vector", lambda e: e.memset(statacc[:, :], 0.0), "ve")
    B.ins("vector", lambda e: e.memset(st_row[:, :], 0.0), "ve")
    B.ins("vector", lambda e: e.memset(qscal[:, :], 0.0), "ve")
    B.wait("vector", "ve")
    B.ins("vector", lambda e: e.memset(c_eps[:, 0:1], EPS), "ve")
    B.ins("vector", lambda e: e.memset(c_eps[:, 1:2], SHIFT), "ve")
    B.ins("vector", lambda e: e.memset(c_onesrow[:, :], 1.0), "ve")
    B.ins("vector", lambda e: e.memset(c_negrow[:, :], -1.0), "ve")

    p2scope = K.scope()
    q_fm = K.sb("q_fm", (128, 4 * S), BF16, p2scope)
    k_fm = K.sb("k_fm", (128, 2 * S), BF16, p2scope)
    v_tok = K.sb("v_tok", (128, 2 * 32 * 128), BF16, p2scope)
    p1scope = K.scope()
    weff = K.sb("weff", (128, 16 * 1024), BF16, p1scope)
    c_cos = K.sb("c_cos", (128, S // 2), F32, p1scope)
    c_sin = K.sb("c_sin", (128, S // 2), F32, p1scope)
    xin = [K.sb(f"xin{i}", (128, HID), F32, p1scope) for i in range(2)]
    xn_tok = [K.sb(f"xntok{i}", (128, HID), BF16, p1scope) for i in range(2)]
    xnT = K.sb("xnT", (128, 16 * 512), BF16, p1scope)
    rope_t = K.sb("rope_t", (64, 2 * 512), F32, p1scope)
    v_stg = K.sb("v_stg", (128, 512), BF16, p1scope)
    pre = K.scope()
    wstr = [K.sb(f"wstr{i}", (128, 1024), F32, pre) for i in range(2)]
    lora_a = K.sb("lora_a", (8, HID), F32, pre)
    lora_b = K.sb("lora_b", (8, 512), F32, pre)
    stbuf = xin


    S_st = B.stream("Sst", 2)
    for m, src in enumerate((wgs_h, wus_h, wds_h)):
        for ch in range(8):
            idx = m * 8 + ch
            buf = stbuf[idx % 2]
            if idx >= 2:
                B.wait("gpsimd", "fz", B.val("fz") - 1)
            tk = S_st.dma("gpsimd", buf[:, :], src[:, 2048 * ch:2048 * (ch + 1)], idx)
            S_st.wait("vector", tk)
            if ch == 0:
                B.ins("vector", lambda e, b=buf, mm=m: e.reduce_sum(
                    statacc[:, mm:mm + 1], b[:, :], axis=AX.X,
                    apply_absolute_value=True), "ve")
            else:
                B.ins("vector", lambda e, b=buf: e.reduce_sum(
                    stat_tp[:, 0:1], b[:, :], axis=AX.X,
                    apply_absolute_value=True), "ve")
                B.wait("vector", "ve")
                B.ins("vector", lambda e, mm=m: e.tensor_add(
                    statacc[:, mm:mm + 1], statacc[:, mm:mm + 1], stat_tp[:, 0:1]), "ve")
                B.wait("vector", "ve")
            B.ins("vector", lambda e: e.engine_nop(), "fz")
    B.wait("tensor", "ve", B.val("ve"))
    S_c.wait("tensor", tk_consts)
    B.ins("tensor", lambda e: e.transpose(
        PA[0][0:8, 0:128], statacc[:, 0:8], c_identf[:, :]), "te")
    B.wait("vector", "te", B.val("te"))
    B.ins("vector", lambda e: e.reduce_sum(
        stat_tp[0:8, 0:1], PA[0][0:8, 0:128], axis=AX.X), "ve")
    S_sm = B.stream("Ssm", 1)
    B.wait("gpsimd", "ve", B.val("ve"))
    tk = S_sm.dma("gpsimd", st_in.ap()[0:8, 0:1], stat_tp[0:8, 0:1])
    S_sm.wait("gpsimd", tk)
    B.ins("gpsimd", lambda e: e.collective_compute(
        "AllReduce", ALU.add, replica_groups=[list(range(8))],
        ins=[st_in.ap().opt()], outs=[st_out.ap().opt()]), "cc")
    B.wait("gpsimd", "cc")
    tk_row = S_sm.dma("gpsimd", strow1[0:1, 0:8], st_out.ap()[0:8, 0:1])
    S_sm.wait("tensor", tk_row)
    B.wait("tensor", "ve", B.val("ve"))
    B.ins("tensor", lambda e: e.matmul(
        PA[0][:, 0:8], c_onesrow[0:1, :], strow1[0:1, 0:8], start=True, stop=True), "te")
    B.wait("vector", "te", B.val("te"))
    B.ins("vector", lambda e: e.tensor_copy(st_row[:, 0:8], PA[0][:, 0:8]), "ve")
    B.wait("vector", "ve")
    B.ins("vector", lambda e: e.tensor_scalar(
        st_row[:, 4:7], st_row[:, 0:3], INVWELEM, EPS, ALU.mult, ALU.max), "ve")
    B.wait("vector", "ve")
    B.ins("vector", lambda e: e.tensor_scalar(
        st_row[:, 8:11], st_row[:, 4:7], 0.5, None, ALU.mult), "ve")
    B.wait("vector", "ve")
    B.ins("vector", lambda e: e.tensor_scalar(
        st_row[:, 11:14], st_row[:, 8:11], -1.0, None, ALU.mult), "ve")
    ve_stats = B.val("ve")

    B.wait("tensor", "ve", B.val("ve"))
    S_lo = B.stream("Slo", 1)
    S_wf = B.stream("Swf", 2)
    for (mi, (Ah, BTh, WTh, ncol, obase)) in enumerate((
            (qA_h, qBT_h, wqT_h, 512, 0),
            (kA_h, kBT_h, wkT_h, 256, 512),
            (vA_h, vBT_h, wvT_h, 256, 768))):
        B.wait("sync", "qv", B.val("qv"))
        S_lo.dma("sync", lora_a[:, 0:HID], Ah)
        tk_lo = S_lo.dma("sync", lora_b[:, 0:ncol], BTh)
        S_lo.wait("tensor", tk_lo)
        for ic in range(16):
            buf = wstr[ic % 2]
            if ic >= 2 or mi > 0:
                B.wait("sync", "qv", B.val("qv") - 1)
            tk_w = S_wf.dma("sync", buf[:, 0:ncol], WTh[128 * ic:128 * (ic + 1), :], ic)
            if ic >= 2 or mi > 0:
                B.wait("tensor", "qv", B.val("qv") - 1)
            B.ins("tensor", lambda e, i=ic, n=ncol: e.matmul(
                PA[i % 2][:, 0:n], lora_a[:, 128 * i:128 * (i + 1)],
                lora_b[:, 0:n], start=True, stop=True), "te")
            tev = B.val("te")
            S_wf.wait("vector", tk_w)
            B.wait("vector", "te", tev)
            B.ins("vector", lambda e, b=buf, i=ic, n=ncol, ob=obase: e.scalar_tensor_tensor(
                weff[:, 1024 * i + ob:1024 * i + ob + n], PA[i % 2][:, 0:n], 2.0,
                b[:, 0:n], ALU.mult, ALU.add), "ve")
            B.ins("vector", lambda e: e.engine_nop(), "qv")
    B.barrier()
    pre.close()


    S_cs = B.stream("Scs", 1)
    S_cs.dma("sync", c_cos[:, :], cos_h)
    tk_cs = S_cs.dma("sync", c_sin[:, :], sin_h)
    S_cs.wait("vector", tk_cs)
    B.wait("sync", "ve", ve_stats)
    S_x = B.stream("Sx", 2)

    xuse_marks = [0, 0]
    xuse_se = [0, 0]
    tpmk = [0, 0]
    qv_marks = [0, 0]

    for gt in range(32):
        xb = xin[gt % 2]
        B.wait("sync", "ve", xuse_marks[gt % 2])
        B.wait("sync", "se", xuse_se[gt % 2])
        tk_x = S_x.dma("sync", xb[:, :], x_h[128 * gt:128 * (gt + 1), :], gt)
        S_x.wait("vector", tk_x)
        S_x.wait("scalar", tk_x)
        B.ins("vector", lambda e, b=xb, g=gt: e.reduce_sum(
            lnsum[:, g:g + 1], b[:, :], axis=AX.X), "ve")
        xuse_marks[gt % 2] = B.val("ve")
        B.wait("scalar", "se", B.val("se"))
        B.ins("scalar", lambda e, b=xb, g=gt: e.activation(
            sq_scr[:, :], b[:, :], AF.Square, accum_out=lnsq[:, g:g + 1]), "se")
        xuse_se[gt % 2] = B.val("se")
    B.wait("vector", "se", B.val("se"))
    B.wait("vector", "ve", B.val("ve"))
    B.ins("vector", lambda e: e.tensor_scalar(
        lnm[:, :], lnsum[:, :], INV2048, None, ALU.mult), "ve")
    B.wait("vector", "ve")
    B.ins("vector", lambda e: e.tensor_mul(lnr[:, :], lnm[:, :], lnm[:, :]), "ve")
    B.wait("vector", "ve")
    B.ins("vector", lambda e: e.scalar_tensor_tensor(
        lnr[:, :], lnsq[:, :], INV2048, lnr[:, :], ALU.mult, ALU.subtract), "ve")
    B.wait("scalar", "ve", B.val("ve"))
    B.ins("scalar", lambda e: e.activation(
        lnsq[:, :], lnr[:, :], AF.Sqrt, bias=c_eps[:, 0:1]), "se")
    B.wait("vector", "se", B.val("se"))
    B.ins("vector", lambda e: e.reciprocal(lnr[:, :], lnsq[:, :]), "ve")
    B.wait("vector", "ve")
    ve_lnchain = B.val("ve")

    for tg in range(8):
        for tl in range(4):
            gt = tg * 4 + tl
            xb = xin[gt % 2]
            B.wait("sync", "ve", xuse_marks[gt % 2])
            B.wait("sync", "se", xuse_se[gt % 2])
            tk_x = S_x.dma("sync", xb[:, :], x_h[128 * gt:128 * (gt + 1), :], gt)
            S_x.wait("vector", tk_x)
            B.ins("vector", lambda e, b=xb, g=gt, t=gt: e.tensor_scalar(
                xn_tok[t % 2][:, :], b[:, :], lnm[:, g:g + 1],
                lnr[:, g:g + 1], ALU.subtract, ALU.mult), "ve")
            xuse_marks[gt % 2] = B.val("ve")
            vev = B.val("ve")
            B.wait("tensor", "ve", vev)
            for bb in range(2):
                B.wait("tensor", "ve", tpmk[bb])
                for j in range(8):
                    ic = bb * 8 + j
                    B.ins("tensor", lambda e, t=gt, i=ic, b=bb, j_=j: e.transpose(
                        PB[b][:, 128 * j_:128 * (j_ + 1)],
                        xn_tok[t % 2][:, 128 * i:128 * (i + 1)], c_identb[:, :]), "te")
                tev = B.val("te")
                B.wait("vector", "te", tev)
                B.ins("vector", lambda e, tl_=tl, b=bb: e.tensor_copy(
                    r3(xnT, 16, 512)[:, 8 * b:8 * (b + 1), 128 * tl_:128 * (tl_ + 1)],
                    r3(PB[b], 8, 128)[:, :, :]), "ve")
                tpmk[bb] = B.val("ve")
        ve_xnT = B.val("ve")

        B.wait("tensor", "ve", ve_xnT)
        tglob = tg * 512
        for oc in range(8):
            unit = tg * 8 + oc
            pp = PA[unit % 2]
            B.wait("tensor", "qv", qv_marks[unit % 2])
            for ic in range(16):
                B.ins("tensor", lambda e, i=ic, o=oc, pb=pp: e.matmul(
                    pb[:, 0:512],
                    weff[:, 1024 * i + 128 * o:1024 * i + 128 * (o + 1)],
                    r3(xnT, 16, 512)[:, i, :],
                    start=(i == 0), stop=(i == 15)), "te")
            tev = B.val("te")
            B.wait("vector", "te", tev)
            if oc < 6:
                dst = q_fm if oc < 4 else k_fm
                hh = oc if oc < 4 else oc - 4
                pbs = 64 * (tg // 4)
                co = 512 * (tg % 4)
                cslice = c_cos[pbs:pbs + 64, co:co + 512]
                sslice = c_sin[pbs:pbs + 64, co:co + 512]
                B.ins("vector", lambda e, pb=pp, c=cslice: e.tensor_mul(
                    rope_t[:, 0:512], pb[0:64, 0:512], c), "ve")
                B.ins("vector", lambda e, pb=pp, s_=sslice: e.tensor_mul(
                    rope_t[:, 512:1024], pb[64:128, 0:512], s_), "ve")
                B.wait("vector", "ve")
                B.ins("vector", lambda e, d=dst, h=hh, t0=tglob: e.tensor_sub(
                    d[0:64, h * S + t0:h * S + t0 + 512],
                    rope_t[:, 0:512], rope_t[:, 512:1024]), "ve")
                B.wait("vector", "ve")
                B.ins("vector", lambda e, pb=pp, s_=sslice: e.tensor_mul(
                    rope_t[:, 0:512], pb[0:64, 0:512], s_), "ve")
                B.ins("vector", lambda e, pb=pp, c=cslice: e.tensor_mul(
                    rope_t[:, 512:1024], pb[64:128, 0:512], c), "ve")
                B.wait("vector", "ve")
                B.ins("vector", lambda e, d=dst, h=hh, t0=tglob: e.tensor_add(
                    d[64:128, h * S + t0:h * S + t0 + 512],
                    rope_t[:, 0:512], rope_t[:, 512:1024]), "ve")
                B.wait("vector", "ve")
                B.ins("vector", lambda e: e.engine_nop(), "qv")
                qv_marks[unit % 2] = B.val("qv")
            else:
                kvh = oc - 6
                B.ins("vector", lambda e, pb=pp: e.tensor_copy(
                    v_stg[:, :], pb[:, 0:512]), "ve")
                B.ins("vector", lambda e: e.engine_nop(), "qv")
                qv_marks[unit % 2] = B.val("qv")
                vev = B.val("ve")
                B.wait("tensor", "ve", vev)
                B.wait("tensor", "ve", tpmk[0])
                for j in range(4):
                    B.ins("tensor", lambda e, j_=j: e.transpose(
                        PB[0][:, 128 * j_:128 * (j_ + 1)],
                        v_stg[:, 128 * j_:128 * (j_ + 1)], c_identb[:, :]), "te")
                tev = B.val("te")
                B.wait("vector", "te", tev)
                ktb = tglob // 128
                B.ins("vector", lambda e, kv=kvh, kb=ktb: e.tensor_copy(
                    v_tok[:, :].rearrange("p (kv kt d) -> p kv kt d", kv=2, kt=32)[
                        :, kv, kb:kb + 4, :],
                    r3(PB[0], 8, 128)[:, 0:4, :]), "ve")
                tpmk[0] = B.val("ve")

    B.barrier()
    p1scope.close()
    if stop == "p1":
        return _finish(K, tap_list, taps, locals())

    p2b = K.scope()
    wdin = [K.sb(f"wdin{i}", (128, HID), F32, p2b) for i in range(2)]
    wdq = [K.sb(f"wdq{i}", (128, HID), BF16, p2b) for i in range(2)]
    c_masks = K.sb("c_masks", (128, 2048), F32, p2b)
    pt_bufs = [K.sb(f"ptbuf{i}", (128, 1024), BF16, p2b) for i in range(2)]
    an_f32 = K.sb("an_f32", (128, 512), F32, p2b)
    rs_sb = K.sb("rs_sb", (1, 512), F32, p2b)

    S_mk = B.stream("Smk", 1)
    tk_mk = S_mk.dma("sync", c_masks[:, :], masks_h)

    B.wait("vector", "ve", ve_stats)
    S_wd = B.stream("Swd", 2)
    S_do = B.stream("Sdo", 2)
    wd_marks = [None, None]
    wd_tk = {}

    def emit_p2a_load(blk):
        if wd_marks[blk % 2] is not None:
            S_do.wait("gpsimd", wd_marks[blk % 2])
        ocb0 = blk // 8
        ic0 = (blk % 8) * 8
        wd_tk[blk] = S_wd.dma("gpsimd", wdin[blk % 2][:, :],
                              wdT_h[ocb0, :, ic0:ic0 + 8, :], blk)

    def emit_p2a_tern(blk):
        buf = wdin[blk % 2]
        qb = wdq[blk % 2]
        ocb0 = blk // 8
        ic0 = (blk % 8) * 8
        S_wd.wait("vector", wd_tk[blk])
        B.ins("vector", lambda e, b=buf: e.tensor_scalar(
            sq_scr[:, :], b[:, :], st_row[:, 10:11], None, ALU.is_ge), "ve")
        B.wait("vector", "ve")
        B.ins("vector", lambda e, b=buf, q=qb: e.scalar_tensor_tensor(
            q[:, :], b[:, :], st_row[:, 13:14], sq_scr[:, :],
            ALU.is_gt, ALU.add), "ve")
        vev = B.val("ve")
        B.wait("vector", "ve", vev)
        B.wait("gpsimd", "ve", vev)
        wd_marks[blk % 2] = S_do.dma(
            "gpsimd", wd_tern.ap()[ocb0, :, ic0:ic0 + 8, :], qb[:, :], blk)

    S_mk.wait("vector", tk_mk)
    S_mk.wait("scalar", tk_mk)
    S_a2 = B.stream("Sa2", 2)
    an_outs = [K.sb(f"anout{i}", (128, 512), BF16, p2b) for i in range(2)]
    an_tok = [None, None]

    se_marks = [0, 0]
    pt_marks = [0, 0]
    pa0_guard = [0]
    qv_fin = [0]
    sched = []
    for h in range(4):
        for qc in range(8):
            ngrp = 2 * (qc + 1)
            for grp in range(ngrp):
                sched.append((h, qc, grp, ngrp))

    def emit_av(idx):
        h, qc, grp, ngrp = sched[idx]
        kv = h // 2
        g = idx
        first = grp == 0
        last = grp == ngrp - 1
        if first:
            B.wait("tensor", "qv", qv_fin[0])
        for t2 in range(2):
            kt = 2 * grp + t2
            s0 = first and t2 == 0
            s1 = last and t2 == 1
            B.ins("tensor", lambda e, g_=g, k=kt, t_=t2, kv_=kv, a=s0, b=s1: e.matmul(
                PC[:, 0:512],
                v_tok[:, :].rearrange("p (kv kt d) -> p kv kt d", kv=2, kt=32)[
                    :, kv_, k, :],
                pt_bufs[g_ % 2][:, 512 * t_:512 * (t_ + 1)],
                start=a, stop=b), "te")
            B.ins("tensor", lambda e, g_=g, t_=t2, a=s0, b=s1: e.matmul(
                PR[0:1, 0:512], c_onesb[:, 0:1],
                pt_bufs[g_ % 2][:, 512 * t_:512 * (t_ + 1)],
                start=a, stop=b), "te")
        B.ins("tensor", lambda e: e.nop(), "pt")
        pt_marks[g % 2] = B.val("pt")
        if last:
            tev = B.val("te")
            B.wait("vector", "te", tev)
            B.ins("vector", lambda e: e.reciprocal(
                rs_sb[0:1, 0:512], PR[0:1, 0:512]), "ve")
            B.wait("vector", "ve")
            B.ins("vector", lambda e: e.tensor_copy(an_f32[:, :], PC[:, 0:512]), "ve")
            B.ins("vector", lambda e: e.engine_nop(), "qv")
            qv_fin[0] = B.val("qv")
            vev = B.val("ve")
            B.wait("tensor", "ve", vev)
            B.wait("tensor", "se", B.val("se"))
            B.ins("tensor", lambda e: e.matmul(
                PA[0][:, 0:512], c_onesrow[0:1, :], rs_sb[0:1, 0:512],
                start=True, stop=True), "te")
            tev = B.val("te")
            B.wait("vector", "te", tev)
            hq = h * 8 + qc
            if an_tok[hq % 2] is not None:
                S_a2.wait("vector", an_tok[hq % 2])
            B.ins("vector", lambda e, p_=hq % 2: e.tensor_mul(
                an_outs[p_][:, :], an_f32[:, :], PA[0][:, 0:512]), "ve")
            pa0_guard[0] = B.val("ve")
            vev = B.val("ve")
            B.wait("sync", "ve", vev)
            sl = qc // 2
            co = 512 * (qc % 2)
            S_a2.dma("sync", a2a_in.ap()[sl, 128 * h:128 * (h + 1), co:co + 512],
                     an_outs[hq % 2][:, :], hq)
            an_tok[hq % 2] = S_a2.dma(
                "sync", a2a_in.ap()[sl + 4, 128 * h:128 * (h + 1), co:co + 512],
                an_outs[hq % 2][:, :], hq)

    emit_p2a_load(0)
    emit_p2a_load(1)
    for gidx, (h, qc, grp, ngrp) in enumerate(sched):
        kv = h // 2
        sp = PA[gidx % 2]
        B.wait("tensor", "se", se_marks[gidx % 2])
        if gidx % 2 == 0 and pa0_guard[0]:
            B.wait("tensor", "ve", pa0_guard[0])
            pa0_guard[0] = 0
        for t2 in range(2):
            kt = 2 * grp + t2
            B.ins("tensor", lambda e, s_=sp, k=kt, t_=t2, h_=h, q_=qc, kv_=kv: e.matmul(
                s_[:, 512 * t_:512 * (t_ + 1)],
                k_fm[:, kv_ * S + 128 * k:kv_ * S + 128 * (k + 1)],
                q_fm[:, h_ * S + 512 * q_:h_ * S + 512 * (q_ + 1)],
                start=True, stop=True), "te")
        tev = B.val("te")
        if grp >= ngrp - 2:
            j2 = grp - (ngrp - 2)
            B.wait("vector", "te", tev)
            B.ins("vector", lambda e, s_=sp, j_=j2: e.tensor_add(
                s_[:, 0:1024], s_[:, 0:1024],
                c_masks[:, 1024 * j_:1024 * (j_ + 1)]), "ve")
            B.wait("scalar", "ve", B.val("ve"))
        else:
            B.wait("scalar", "te", tev)
        B.wait("scalar", "pt", pt_marks[gidx % 2])
        B.ins("scalar", lambda e, s_=sp, g=gidx: e.activation(
            pt_bufs[g % 2][:, 0:1024], s_[:, 0:1024], AF.Exp,
            bias=c_eps[:, 1:2], scale=SCALING), "se")
        se_marks[gidx % 2] = B.val("se")
        if gidx >= 1:
            B.wait("tensor", "se", B.val("se") - 1)
            emit_av(gidx - 1)
        if gidx % 4 == 0 and gidx > 0 and gidx // 4 + 1 < 64:
            emit_p2a_load(gidx // 4 + 1)
        if gidx % 4 == 2 and gidx // 4 < 64:
            emit_p2a_tern(gidx // 4)
    B.wait("tensor", "se", B.val("se"))
    emit_av(len(sched) - 1)

    B.barrier()
    p2b.close()
    p2scope.close()
    if stop == "p2":
        return _finish(K, tap_list, taps, locals())

    p45scope = K.scope()
    xq2T = K.sb("xq2T", (128, 16 * TOK), BF16, p45scope)
    p34scope = K.scope()
    x2_sb = K.sb("x2_sb", (128, 8 * HID), F32, p34scope)
    p3scope = K.scope()
    av_fm = K.sb("av_fm", (128, 16 * TOK), BF16, p3scope)
    a2a_ld = [K.sb(f"a2ald{i}", (128, 4 * TOK), BF16, p3scope) for i in range(2)]
    wo_str = [K.sb(f"wostr{i}", (128, 16 * 256), F32, p3scope) for i in range(2)]
    wo_eff = K.sb("wo_eff", (128, 16 * 256), BF16, p3scope)
    lora_o = K.sb("lora_o", (8, HID), F32, p3scope)
    lora_ob = K.sb("lora_ob", (8, 256), F32, p3scope)

    S_lo2 = B.stream("Slo2", 1)
    tk_oa = S_lo2.dma("sync", lora_o[:, 0:HID], oA_h)
    S_a2.wait_all("gpsimd")
    def _a2a(e):
        with allow_ici_all_to_all():
            return e.collective_compute(
                "AllToAll", ALU.bypass, replica_groups=[list(range(8))],
                ins=[a2a_in.ap().opt()], outs=[a2a_out.ap().opt()])
    B.ins("gpsimd", _a2a, "cc")
    B.wait("sync", "cc", B.val("cc"))
    S_al = B.stream("Sal", 1)
    for i in range(4):
        if i >= 1:
            B.wait("sync", "ve", B.val("ve"))
        for hc4 in range(4):
            S_al.dma("sync", a2a_ld[0][:, 1024 * hc4:1024 * (hc4 + 1)],
                     a2a_out.ap()[i, 128 * hc4:128 * (hc4 + 1), :])
            tk_al = S_al.dma("sync", a2a_ld[1][:, 1024 * hc4:1024 * (hc4 + 1)],
                             a2a_out.ap()[i + 4, 128 * hc4:128 * (hc4 + 1), :])
        S_al.wait("vector", tk_al)
        B.ins("vector", lambda e, i_=i: e.tensor_scalar(
            a2a_ld[1][:, :], a2a_ld[1][:, :], c_gates[:, i_ + 4:i_ + 5], None,
            ALU.mult), "ve")
        B.wait("vector", "ve")
        B.ins("vector", lambda e, i_=i: e.scalar_tensor_tensor(
            av_fm[:, 4096 * i_:4096 * (i_ + 1)], a2a_ld[0][:, :],
            c_gates[:, i_:i_ + 1], a2a_ld[1][:, :], ALU.mult, ALU.add), "ve")
    ve_av = B.val("ve")

    S_xi = B.stream("Sxi", 1)
    for tt in range(8):
        tk_x2i = S_xi.dma("sync", x2_sb[:, 2048 * tt:2048 * (tt + 1)],
                          xown_h[128 * tt:128 * (tt + 1), :])

    B.wait("tensor", "ve", ve_av)
    S_wb = B.stream("Swb", 2)
    wo_dn_marks = [0, 0]
    for ocb in range(8):
        wsb = wo_str[ocb % 2]
        if ocb >= 2:
            B.wait("sync", "qv", B.val("qv") - 1)
        B.wait("sync", "fz", B.val("fz"))
        S_wb.dma("sync", lora_ob[:, 0:256], oBT_h[0:8, 256 * ocb:256 * (ocb + 1)], ocb)
        for hc in range(16):
            tk_wo = S_wb.dma("sync", wsb[:, 256 * hc:256 * (hc + 1)],
                             woT_h[128 * hc:128 * (hc + 1), 256 * ocb:256 * (ocb + 1)], ocb)
        B.wait("tensor", "dn", B.val("dn"))
        S_wb.wait("tensor", tk_wo)
        if ocb == 0:
            S_lo2.wait("tensor", tk_oa)
        for hc in range(16):
            if not (ocb == 0 and hc < 2):
                B.wait("tensor", "fz", B.val("fz") - 1)
            B.ins("tensor", lambda e, h_=hc: e.matmul(
                PA[h_ % 2][:, 0:256], lora_o[:, 128 * h_:128 * (h_ + 1)],
                lora_ob[:, 0:256], start=True, stop=True), "te")
            tev = B.val("te")
            B.wait("vector", "te", tev)
            B.ins("vector", lambda e, h_=hc, w=wsb: e.scalar_tensor_tensor(
                wo_eff[:, 256 * h_:256 * (h_ + 1)], PA[h_ % 2][:, 0:256], 2.0,
                w[:, 256 * h_:256 * (h_ + 1)], ALU.mult, ALU.add), "ve")
            B.ins("vector", lambda e: e.engine_nop(), "fz")
        ve_woe = B.val("ve")
        B.wait("tensor", "ve", ve_woe)
        B.ins("vector", lambda e: e.engine_nop(), "qv")
        for tt in range(8):
            pp = PA[tt % 2]
            B.wait("tensor", "dn", wo_dn_marks[tt % 2])
            for hc in range(16):
                B.ins("tensor", lambda e, h_=hc, t=tt, pb=pp: e.matmul(
                    pb[:, 0:256],
                    av_fm[:, 1024 * h_ + 128 * t:1024 * h_ + 128 * (t + 1)],
                    wo_eff[:, 256 * h_:256 * (h_ + 1)],
                    start=(h_ == 0), stop=(h_ == 15)), "te")
            tev = B.val("te")
            B.wait("vector", "te", tev)
            if ocb == 0 and tt == 0:
                S_xi.wait("vector", tk_x2i)
            B.ins("vector", lambda e, t=tt, o=ocb, pb=pp: e.tensor_add(
                x2_sb[:, 2048 * t + 256 * o:2048 * t + 256 * (o + 1)],
                x2_sb[:, 2048 * t + 256 * o:2048 * t + 256 * (o + 1)],
                pb[:, 0:256]), "ve")
            B.ins("vector", lambda e: e.engine_nop(), "dn")
            wo_dn_marks[tt % 2] = B.val("dn")
            if ocb == 7:
                xsl = x2_sb[:, 2048 * tt:2048 * (tt + 1)]
                B.wait("vector", "ve")
                B.ins("vector", lambda e, b=xsl, t=tt: e.reduce_sum(
                    lnstat[:, t:t + 1], b, axis=AX.X), "ve")
                B.wait("scalar", "ve", B.val("ve"))
                B.wait("scalar", "se")
                B.ins("scalar", lambda e, b=xsl, t=tt: e.activation(
                    sq_scr[:, :], b, AF.Square,
                    accum_out=lnstat[:, 8 + t:9 + t]), "se")
    ve_x2 = B.val("ve")
    S_xs = B.stream("Sxs", 1)
    B.wait("gpsimd", "ve", ve_x2)
    for tt in range(8):
        S_xs.dma("gpsimd", x2_sp.ap()[128 * tt:128 * (tt + 1), :],
                 x2_sb[:, 2048 * tt:2048 * (tt + 1)])

    B.barrier()
    p3scope.close()
    if stop == "p3":
        return _finish(K, tap_list, taps, locals())

    p4scope = K.scope()
    xq_f32 = K.sb("xq_f32", (128, HID), F32, p4scope)
    xq_i32 = K.sb("xq_i32", (128, HID), I32, p4scope)
    xq_tok = K.sb("xq_tok", (128, HID), BF16, p4scope)

    B.wait("vector", "se", B.val("se"))
    B.ins("vector", lambda e: e.tensor_scalar(
        lnstat[:, 16:24], lnstat[:, 0:8], INV2048, None, ALU.mult), "ve")
    B.wait("vector", "ve")
    B.ins("vector", lambda e: e.tensor_mul(
        lnstat[:, 24:32], lnstat[:, 16:24], lnstat[:, 16:24]), "ve")
    B.wait("vector", "ve")
    B.ins("vector", lambda e: e.scalar_tensor_tensor(
        lnstat[:, 32:40], lnstat[:, 8:16], INV2048,
        lnstat[:, 24:32], ALU.mult, ALU.subtract), "ve")
    B.wait("scalar", "ve", B.val("ve"))
    B.ins("scalar", lambda e: e.activation(
        lnstat[:, 40:48], lnstat[:, 32:40], AF.Sqrt, bias=c_eps[:, 0:1]), "se")
    B.wait("vector", "se", B.val("se"))
    B.ins("vector", lambda e: e.reciprocal(
        lnstat[:, 48:56], lnstat[:, 40:48]), "ve")
    B.wait("vector", "ve")

    p4tm = [0]
    q3 = qscal[:, :].rearrange("p (t k) -> p t k", k=8)
    ln3 = lnstat[:, :].rearrange("p (k t) -> p k t", k=8)
    for tt in range(8):
        xsl = x2_sb[:, 2048 * tt:2048 * (tt + 1)]
        B.ins("vector", lambda e, b=xsl, t=tt: e.tensor_scalar(
            xq_f32[:, :], b, lnstat[:, 16 + t:17 + t], lnstat[:, 48 + t:49 + t],
            ALU.subtract, ALU.mult), "ve")
        B.wait("vector", "ve")
        B.ins("vector", lambda e, t=tt: e.reduce_max(
            qscal[:, 8 * t + 6:8 * t + 7], xq_f32[:, :], axis=AX.X,
            apply_absolute_value=True), "ve")
        B.wait("vector", "ve")
    B.ins("vector", lambda e: e.tensor_scalar(
        q3[:, :, 0:1], q3[:, :, 6:7], EPS, None, ALU.max), "ve")
    B.wait("vector", "ve")
    B.ins("vector", lambda e: e.reciprocal(q3[:, :, 6:7], q3[:, :, 0:1]), "ve")
    B.wait("vector", "ve")
    B.ins("vector", lambda e: e.tensor_scalar(
        q3[:, :, 1:2], q3[:, :, 6:7], 128.0, None, ALU.mult), "ve")
    B.wait("vector", "ve")
    B.ins("vector", lambda e: e.tensor_tensor(
        ln3[:, 7:8, :], ln3[:, 6:7, :],
        qscal[:, :].rearrange("p (t k) -> p k t", t=8)[:, 1:2, :], ALU.mult), "ve")
    B.wait("vector", "ve")
    for tt in range(8):
        xsl = x2_sb[:, 2048 * tt:2048 * (tt + 1)]
        B.ins("vector", lambda e, b=xsl, t=tt: e.tensor_scalar(
            xq_f32[:, :], b, lnstat[:, 16 + t:17 + t], lnstat[:, 56 + t:57 + t],
            ALU.subtract, ALU.mult), "ve")
        B.wait("vector", "ve")
        B.ins("vector", lambda e: e.tensor_copy(xq_i32[:, :], xq_f32[:, :]), "ve")
        B.wait("vector", "ve")
        B.wait("vector", "te", p4tm[0])
        B.ins("vector", lambda e: e.tensor_copy(xq_tok[:, :], xq_i32[:, :]), "ve")
        B.wait("vector", "ve")
        B.ins("vector", lambda e, t=tt: e.tensor_scalar(
            xq_tok[:, :], xq_tok[:, :], 127.0, None, ALU.min, ALU.add,
            accum_out=qscal[:, 8 * t + 7:8 * t + 8]), "ve")
        vev = B.val("ve")
        B.wait("tensor", "ve", vev)
        for bb in range(2):
            for j in range(8):
                ic = bb * 8 + j
                B.ins("tensor", lambda e, i=ic, b=bb, j_=j: e.transpose(
                    PB[b][:, 128 * j_:128 * (j_ + 1)],
                    xq_tok[:, 128 * i:128 * (i + 1)], c_identb[:, :]), "te")
            tev = B.val("te")
            p4tm[0] = tev
            B.wait("scalar", "te", tev)
            B.ins("scalar", lambda e, t=tt, b=bb: e.copy(
                r3(xq2T, 16, TOK)[:, 8 * b:8 * (b + 1), 128 * t:128 * (t + 1)],
                r3(PB[b], 8, 128)[:, :, :]), "se")
            sev = B.val("se")
            B.wait("tensor", "se", sev)
    B.wait("vector", "ve")
    B.ins("vector", lambda e: e.tensor_scalar(
        q3[:, :, 6:7], q3[:, :, 0:1], 1.0 / 128.0, None, ALU.mult), "ve")
    B.wait("vector", "ve")
    B.ins("vector", lambda e: e.tensor_tensor(
        q3[:, :, 2:3], q3[:, :, 6:7], bc3(st_row[:, 4:5], 8), ALU.mult), "ve")
    B.ins("vector", lambda e: e.tensor_tensor(
        q3[:, :, 4:5], q3[:, :, 6:7], bc3(st_row[:, 5:6], 8), ALU.mult), "ve")
    B.wait("vector", "ve")
    B.ins("vector", lambda e: e.scalar_tensor_tensor(
        q3[:, :, 3:4], q3[:, :, 2:3], -1.0, q3[:, :, 7:8], ALU.mult, ALU.mult), "ve")
    B.ins("vector", lambda e: e.scalar_tensor_tensor(
        q3[:, :, 5:6], q3[:, :, 4:5], -1.0, q3[:, :, 7:8], ALU.mult, ALU.mult), "ve")
    B.wait("vector", "ve")

    B.barrier()
    p4scope.close()
    p34scope.close()
    if stop == "p4":
        return _finish(K, tap_list, taps, locals())

    p5scope = K.scope()
    gu_str = [K.sb(f"gustr{i}", (128, 16 * 512), F32, p5scope) for i in range(2)]
    gu_q = [K.sb(f"guq{i}", (128, 16 * 512), BF16, p5scope) for i in range(2)]
    silu_all = K.sb("silu_all", (128, 8 * 512), F32, p5scope)
    up_sb2 = [K.sb(f"upsb{i}", (128, 512), F32, p5scope) for i in range(2)]
    int_sb = [K.sb(f"intsb{i}", (128, 512), BF16, p5scope) for i in range(2)]

    S_gu = B.stream("Sgu", 2)
    S_is = B.stream("Sis", 2)
    p5qv_marks = [None, None]
    int_marks = [None, None]
    silu_rd = [0] * 8
    tern_done = {}

    def blk_src(b):
        return (wgT_h if b % 2 == 0 else wuT_h), b // 2

    tern_rd = {}
    gu_te_marks = [0, 0]

    def emit_load(b):
        wsrc, ocb = blk_src(b)
        eng = "sync" if b % 2 == 0 else "scalar"
        B.wait(eng, "dn", tern_rd.get(b - 2, 0))
        return S_gu.dma(eng, gu_str[b % 2][:, :], wsrc[ocb, :, :], b)

    def emit_tern(b):
        tc, nc_ = (8, 11) if b % 2 == 0 else (9, 12)
        wsb, wqb = gu_str[b % 2], gu_q[b % 2]
        S_gu.wait("vector", tern_done[b])
        B.wait("vector", "te", gu_te_marks[b % 2])
        B.ins("vector", lambda e, w=wsb, t_=tc: e.tensor_scalar(
            sq_scr[:, 0:2048], w[:, 0:2048], st_row[:, t_:t_ + 1], None,
            ALU.is_ge), "ve")
        B.ins("vector", lambda e, w=wsb, t_=tc: e.tensor_scalar(
            xq_f32g[:, 0:2048], w[:, 2048:4096], st_row[:, t_:t_ + 1], None,
            ALU.is_ge), "ve")
        B.wait("vector", "ve")
        B.ins("vector", lambda e, w=wsb, q=wqb, n_=nc_: e.scalar_tensor_tensor(
            q[:, 0:2048], w[:, 0:2048], st_row[:, n_:n_ + 1],
            sq_scr[:, 0:2048], ALU.is_gt, ALU.add), "ve")
        B.ins("vector", lambda e, w=wsb, q=wqb, n_=nc_: e.scalar_tensor_tensor(
            q[:, 2048:4096], w[:, 2048:4096], st_row[:, n_:n_ + 1],
            xq_f32g[:, 0:2048], ALU.is_gt, ALU.add), "ve")
        B.wait("vector", "ve")
        B.ins("vector", lambda e, w=wsb, t_=tc: e.tensor_scalar(
            sq_scr[:, 0:2048], w[:, 4096:6144], st_row[:, t_:t_ + 1], None,
            ALU.is_ge), "ve")
        B.ins("vector", lambda e, w=wsb, t_=tc: e.tensor_scalar(
            xq_f32g[:, 0:2048], w[:, 6144:8192], st_row[:, t_:t_ + 1], None,
            ALU.is_ge), "ve")
        B.wait("vector", "ve")
        B.ins("vector", lambda e, w=wsb, q=wqb, n_=nc_: e.scalar_tensor_tensor(
            q[:, 4096:6144], w[:, 4096:6144], st_row[:, n_:n_ + 1],
            sq_scr[:, 0:2048], ALU.is_gt, ALU.add), "ve")
        B.ins("vector", lambda e, w=wsb, q=wqb, n_=nc_: e.scalar_tensor_tensor(
            q[:, 6144:8192], w[:, 6144:8192], st_row[:, n_:n_ + 1],
            xq_f32g[:, 0:2048], ALU.is_gt, ALU.add), "ve")
        B.wait("vector", "ve")
        B.ins("vector", lambda e: e.engine_nop(), "dn")
        tern_rd[b] = B.val("dn")
        return B.val("ve")

    xq_f32g = K.sb("xq_f32g", (128, 2048), F32, p5scope)

    tern_done[0] = emit_load(0)
    tern_done[1] = emit_load(1)
    tern_ve = {}
    tern_ve[0] = emit_tern(0)
    for b in range(32):
        gate = b % 2 == 0
        wqb = gu_q[b % 2]
        if b + 2 < 32:
            tern_done[b + 2] = emit_load(b + 2)
        if b + 1 < 32:
            tern_ve[b + 1] = emit_tern(b + 1)
        B.wait("tensor", "ve", tern_ve[b])
        for tt in range(8):
            po = 512 * (tt % 2)
            PAx = PA[0] if gate else PA[1]
            if p5qv_marks[tt % 2] is not None:
                B.wait("tensor", *p5qv_marks[tt % 2])
            for hc in range(16):
                B.ins("tensor", lambda e, h_=hc, t=tt, q=wqb, o=po, pb=PAx: e.matmul(
                    pb[:, o:o + 512],
                    r3(xq2T, 16, TOK)[:, h_, 128 * t:128 * (t + 1)],
                    q[:, 512 * h_:512 * (h_ + 1)],
                    start=(h_ == 0), stop=(h_ == 15)), "te")
            tev = B.val("te")
            if gate:
                B.wait("scalar", "te", tev)
                B.wait("scalar", "ve", silu_rd[tt])
                B.ins("scalar", lambda e, t=tt, o=po: e.activation(
                    silu_all[:, 512 * t:512 * (t + 1)], PA[0][:, o:o + 512], AF.Silu,
                    bias=qscal[:, 8 * t + 3:8 * t + 4],
                    scale=qscal[:, 8 * t + 2:8 * t + 3]), "se")
                p5qv_marks[tt % 2] = ("se", B.val("se"))
            else:
                B.wait("vector", "te", tev)
                B.ins("vector", lambda e, t=tt, o=po: e.tensor_scalar(
                    up_sb2[t % 2][:, :], PA[1][:, o:o + 512],
                    qscal[:, 8 * t + 4:8 * t + 5],
                    qscal[:, 8 * t + 5:8 * t + 6], ALU.mult, ALU.add), "ve")
                p5qv_marks[tt % 2] = ("ve", B.val("ve"))
                B.wait("vector", "ve")
                B.wait("vector", "se", B.val("se"))
                if int_marks[tt % 2] is not None:
                    S_is.wait("vector", int_marks[tt % 2])
                B.ins("vector", lambda e, t=tt: e.tensor_mul(
                    int_sb[t % 2][:, :], silu_all[:, 512 * t:512 * (t + 1)],
                    up_sb2[t % 2][:, :]), "ve")
                silu_rd[tt] = B.val("ve")
                vev = B.val("ve")
                B.wait("sync", "ve", vev)
                int_marks[tt % 2] = S_is.dma(
                    "sync", inter_sp.ap()[128 * tt:128 * (tt + 1),
                                          512 * (b // 2):512 * (b // 2 + 1)],
                    int_sb[tt % 2][:, :], tt)
        gu_te_marks[b % 2] = B.val("te")

    B.barrier()
    p5scope.close()
    p45scope.close()
    if stop == "p5":
        return _finish(K, tap_list, taps, locals())

    p6scope = K.scope()
    intr_ld = K.sb("intr_ld", (128, IFF), BF16, p6scope)
    itT = K.sb("itT", (128, 64 * 128), BF16, p6scope)
    fwX = [K.sb(f"fwx{i}", (128, 64 * 128), BF16, p6scope) for i in range(2)]
    fw_scr = K.sb("fw_scr", (128, 64 * 128), BF16, p6scope)
    f32scr = K.sb("f32scr", (128, 512), F32, p6scope)
    i32scr = K.sb("i32scr", (128, 512), I32, p6scope)
    m2st = K.sb("m2st", (128, 32), F32, p6scope)
    xqf2 = [K.sb(f"xqf{i}", (128, 64 * 256), BF16, p6scope) for i in range(2)]
    wd_str = [K.sb(f"wdstr{i}", (128, 16 * 256), BF16, p6scope) for i in range(3)]
    ysg = K.sb("ysg", (128, 2 * HID), F32, p6scope)

    S_is.wait_all("gpsimd")
    S_il = B.stream("Sil", 1)
    S_wl = B.stream("Swl", 3)
    S_yi = B.stream("Syi", 1)
    tp_mark = [0]
    pbm = [None, None]
    px_mark = [0]
    s1m = [None, None]
    p6dn_marks = [0, 0]
    cmark = [0, 0]
    qdone = [None, None]
    afin_mark = [0, 0]
    pr_mark = [0]
    tail_te = [0]
    ysg_mark = [0]
    B.wait("tensor", "ve", B.val("ve"))

    def emit_AH(t):
        B.wait("gpsimd", "te", tp_mark[0])
        tk_il = S_il.dma("gpsimd", intr_ld[:, :],
                         inter_sp.ap()[128 * t:128 * (t + 1), :])
        S_il.wait("tensor", tk_il)
        evse = []
        for bb in range(8):
            if pbm[bb % 2] is not None:
                B.wait("tensor", *pbm[bb % 2])
            if bb == 0:
                B.wait("tensor", "px", px_mark[0])
            for j in range(8):
                ic = bb * 8 + j
                B.ins("tensor", lambda e, i=ic, pb=bb % 2, j_=j: e.transpose(
                    PB[pb][:, 128 * j_:128 * (j_ + 1)],
                    intr_ld[:, 128 * i:128 * (i + 1)], c_identb[:, :]), "te")
            tev = B.val("te")
            if bb == 7:
                tp_mark[0] = tev
            B.wait("scalar", "te", tev)
            B.ins("scalar", lambda e, b=bb: e.copy(
                itT[:, 1024 * b:1024 * (b + 1)], PB[b % 2][:, 0:1024]), "se")
            pbm[bb % 2] = ("se", B.val("se"))
            evse.append(B.val("se"))
        fb = fwX[t % 2]
        if qdone[t % 2] is not None:
            B.wait("scalar", *qdone[t % 2])
        for m in range(16):
            po = 512 * (m % 2)
            if s1m[m % 2] is not None:
                B.wait("tensor", *s1m[m % 2])
            B.wait("tensor", "se", evse[m // 2])
            B.ins("tensor", lambda e, m_=m, o=po: e.matmul(
                PA[0][:, o:o + 512], c_had[:, :], itT[:, 512 * m_:512 * (m_ + 1)],
                start=True, stop=True), "te")
            tev = B.val("te")
            B.wait("scalar", "te", tev)
            B.ins("scalar", lambda e, m_=m, o=po, f=fb: e.copy(
                f[:, 512 * m_:512 * (m_ + 1)], PA[0][:, o:o + 512]), "se")
            s1m[m % 2] = ("se", B.val("se"))

    def bf_steps(t):
        fb = fwX[t % 2]
        steps = []

        def mk(s):
            def go():
                if s == 0:
                    B.wait("vector", "se", B.val("se"))
                    B.wait("vector", "te", tail_te[0])
                src_ = fb if s % 2 == 0 else fw_scr
                dst_ = fw_scr if s % 2 == 0 else fb
                ps_ = 1 << s
                nh_ = 64 // (2 * ps_)
                sa = src_[:, :].rearrange("p (hi two lo u) -> p hi two lo u",
                                          hi=nh_, two=2, lo=ps_)
                da = dst_[:, :].rearrange("p (hi two lo u) -> p hi two lo u",
                                          hi=nh_, two=2, lo=ps_)
                B.ins("vector", lambda e, s_=sa, d=da: e.tensor_add(
                    d[:, :, 0, :, :], s_[:, :, 0, :, :], s_[:, :, 1, :, :]), "ve")
                B.ins("vector", lambda e, s_=sa, d=da: e.tensor_sub(
                    d[:, :, 1, :, :], s_[:, :, 0, :, :], s_[:, :, 1, :, :]), "ve")
                B.wait("vector", "ve")
            return go
        for s in range(6):
            steps.append(mk(s))
        return steps

    def abs_step(t):
        def go():
            fb = fwX[t % 2]
            B.wait("scalar", "ve", B.val("ve"))
            B.ins("scalar", lambda e, f=fb: e.activation(
                fw_scr[:, :], f[:, :], AF.Abs), "se")
        return go

    def tmax_step(t):
        def go():
            B.wait("vector", "se", B.val("se"))
            ln = 32 * 128
            while ln >= 128:
                B.ins("vector", lambda e, l=ln: e.tensor_max(
                    fw_scr[:, 0:l], fw_scr[:, 0:l], fw_scr[:, l:2 * l]), "ve")
                B.wait("vector", "ve")
                ln //= 2
        return go

    def ata_tensor(t):
        def go():
            mb = 16 * ((t // 2) % 2) + 8 * (t % 2)
            B.wait("tensor", "ve", B.val("ve"))
            B.ins("tensor", lambda e: e.transpose(
                PB[0][:, 0:128], fw_scr[:, 0:128], c_identb[:, :]), "te")
            tail_te[0] = B.val("te")
            B.wait("vector", "te", B.val("te"))
            B.ins("vector", lambda e, m_=mb: e.reduce_max(
                m2st[:, m_ + 0:m_ + 1], PB[0][:, 0:128], axis=AX.X), "ve")
            B.ins("vector", lambda e: e.engine_nop(), "px")
            px_mark[0] = B.val("px")
            B.wait("vector", "ve")
            B.ins("vector", lambda e, m_=mb: e.tensor_scalar(
                m2st[:, m_ + 0:m_ + 1], m2st[:, m_ + 0:m_ + 1], EPSRAW, None,
                ALU.max), "ve")
            B.wait("vector", "ve")
            B.ins("vector", lambda e, m_=mb: e.reciprocal(
                m2st[:, m_ + 1:m_ + 2], m2st[:, m_ + 0:m_ + 1]), "ve")
            B.wait("vector", "ve")
            B.ins("vector", lambda e, m_=mb: e.tensor_scalar(
                m2st[:, m_ + 1:m_ + 2], m2st[:, m_ + 1:m_ + 2], 128.0, None,
                ALU.mult), "ve")
            B.wait("vector", "ve")
            B.ins("vector", lambda e, m_=mb: e.tensor_scalar(
                m2st[:, m_ + 2:m_ + 3], m2st[:, m_ + 0:m_ + 1],
                RSQ8192 / 128.0, None, ALU.mult), "ve")
            B.wait("vector", "ve")
            B.ins("vector", lambda e, m_=mb: e.tensor_mul(
                m2st[:, m_ + 2:m_ + 3], m2st[:, m_ + 2:m_ + 3], st_row[:, 6:7]), "ve")
            vev = B.val("ve")
            B.wait("tensor", "ve", vev)
            B.ins("tensor", lambda e, m_=mb: e.transpose(
                PC[0:1, 384:512], m2st[:, m_ + 1:m_ + 2], c_identf[:, :]), "te")
            B.wait("vector", "te", B.val("te"))
            B.ins("vector", lambda e: e.tensor_copy(srow[0:1, 0:128], PC[0:1, 384:512]), "ve")
            B.wait("tensor", "ve", B.val("ve"))
            B.ins("tensor", lambda e: e.matmul(
                PC[:, 128:256], c_onesrow[0:1, :], srow[0:1, 0:128],
                start=True, stop=True), "te")
            B.wait("vector", "te", B.val("te"))
            B.wait("vector", "te", cmark[(t // 2) % 2])
        return go

    def quant_steps(t):
        fb = fwX[t % 2]
        xqf = xqf2[(t // 2) % 2]
        steps = []

        def mk(lo, hi, last):
            def go():
                for sl4 in range(lo, hi):
                    fslice = fb[:, 512 * sl4:512 * (sl4 + 1)]
                    B.ins("vector", lambda e, f=fslice: e.tensor_tensor(
                        f32scr[:, :].rearrange("p (i u) -> p i u", i=4),
                        f.rearrange("p (i u) -> p i u", i=4),
                        bc3(PC[:, 128:256], 4), ALU.mult), "ve")
                    B.wait("vector", "ve")
                    B.ins("vector", lambda e: e.tensor_copy(i32scr[:, :], f32scr[:, :]), "ve")
                    B.wait("vector", "ve")
                    B.ins("vector", lambda e, tl=t % 2, s4=sl4, x=xqf: e.tensor_scalar(
                        r3(x, 64, 256)[:, 4 * s4:4 * (s4 + 1), 128 * tl:128 * (tl + 1)],
                        i32scr[:, :].rearrange("p (i u) -> p i u", i=4),
                        127.0, None, ALU.min), "ve")
                    B.wait("vector", "ve")
                if last:
                    qdone[t % 2] = ("ve", B.val("ve"))
            return go
        for k in range(6):
            lo = (16 * k) // 6
            hi = (16 * (k + 1)) // 6
            steps.append(mk(lo, hi, k == 5))
        return steps

    def emit_ATb(t):
        xqf = xqf2[(t // 2) % 2]
        B.wait("tensor", *qdone[t % 2])
        B.wait("tensor", "ve", pr_mark[0])
        for icm in range(64):
            B.ins("tensor", lambda e, i=icm, tl=t % 2, x=xqf: e.matmul(
                PR[0:1, 0:128], c_onesb[:, 0:1],
                r3(x, 64, 256)[:, i, 128 * tl:128 * (tl + 1)],
                start=(i == 0), stop=(i == 63)), "te")
        B.wait("vector", "te", B.val("te"))
        B.ins("vector", lambda e, c=256 * ((t // 2) % 2) + 128 * (t % 2): e.tensor_copy(
            sfrows[0:1, c:c + 128], PR[0:1, 0:128]), "ve")
        pr_mark[0] = B.val("ve")
        if t % 2 == 1:
            afin_mark[(t // 2) % 2] = B.val("ve")

    def emit_ATa(t):
        for f in bf_steps(t):
            f()
        abs_step(t)()
        tmax_step(t)()
        ata_tensor(t)()
        for f in quant_steps(t):
            f()

    wd_mk = [0, 0, 0]

    def emit_Cstrip(sg, strip):
        xqf = xqf2[sg % 2]
        mbase = 16 * (sg % 2)
        ocb, qh = strip // 4, strip % 4
        slot = (strip // 2) % 2 if strip % 2 == 0 else 2
        wb = wd_str[slot]
        qeng = "sync" if strip % 2 == 0 else "scalar"
        if strip == 0:
            B.wait("gpsimd", "fin", ysg_mark[0])
            if sg == 0:
                S_xs.wait_all("gpsimd")
            for tl in range(2):
                tk_yi = S_yi.dma("gpsimd", ysg[:, 2048 * tl:2048 * (tl + 1)],
                                 x2_sp.ap()[256 * sg + 128 * tl:
                                            256 * sg + 128 * (tl + 1), :])
            S_yi.wait("vector", tk_yi)
            B.wait("tensor", "ve", afin_mark[sg % 2])
        B.wait(qeng, "te", wd_mk[slot])
        if sg == 0 and strip < 2:
            S_do.wait_all(qeng)
        tk_wl = S_wl.dma(
            qeng, wb[:, :],
            wd_tern.ap()[ocb, :, 16 * qh:16 * (qh + 1), :], slot)
        S_wl.wait("tensor", tk_wl)
        for tl in range(2):
            if qh == 0:
                B.wait("tensor", "dn", p6dn_marks[tl])
            for icm in range(16):
                i = 16 * qh + icm
                B.ins("tensor", lambda e, i_=i, icm_=icm, tl_=tl, w=wb,
                      o=512 * (tl % 2), x=xqf: e.matmul(
                    PA[1][:, o:o + 256],
                    r3(x, 64, 256)[:, i_, 128 * tl_:128 * (tl_ + 1)],
                    w[:, 256 * icm_:256 * (icm_ + 1)],
                    start=(i_ == 0), stop=False), "te")
            if qh == 3:
                B.ins("tensor", lambda e, o=512 * (tl % 2),
                      c=256 * (sg % 2) + 128 * tl: e.matmul(
                    PA[1][:, o:o + 256], sfrows[0:1, c:c + 128],
                    c_negrow[0:1, 0:256], start=False, stop=True), "te")
                tev = B.val("te")
                B.wait("vector", "te", tev)
                B.ins("vector", lambda e, t=tl, o=ocb, po=512 * (tl % 2),
                      m_=mbase + 8 * tl: e.scalar_tensor_tensor(
                    ysg[:, 2048 * t + 256 * o:2048 * t + 256 * (o + 1)],
                    PA[1][:, po:po + 256], m2st[:, m_ + 2:m_ + 3],
                    ysg[:, 2048 * t + 256 * o:2048 * t + 256 * (o + 1)],
                    ALU.mult, ALU.add), "ve")
                B.ins("vector", lambda e: e.engine_nop(), "dn")
                p6dn_marks[tl] = B.val("dn")
        wd_mk[slot] = B.val("te")
        if strip == 31:
            cmark[sg % 2] = B.val("te")
            ve_y = B.val("ve")
            B.wait("gpsimd", "ve", ve_y)
            for tl in range(2):
                B.dma("gpsimd", y_h[256 * sg + 128 * tl:256 * sg + 128 * (tl + 1), :],
                      ysg[:, 2048 * tl:2048 * (tl + 1)], "fin")
            ysg_mark[0] = B.val("fin")

    emit_AH(0); emit_ATa(0); emit_ATb(0)
    emit_AH(1); emit_ATa(1); emit_ATb(1)
    emit_AH(2)
    for sg in range(4):
        t2, t3 = 2 * sg + 2, 2 * sg + 3
        plan = {}
        if sg < 3:
            bs2, qs2 = bf_steps(t2), quant_steps(t2)
            bs3, qs3 = bf_steps(t3), quant_steps(t3)
            for i in range(6):
                plan[i] = [bs2[i]]
                plan[9 + i] = [qs2[i]]
                plan[16 + i] = [bs3[i]]
                plan[25 + i] = [qs3[i]]
            plan[6] = [abs_step(t2)]
            plan[7] = [tmax_step(t2)]
            plan[8] = [ata_tensor(t2)]
            plan[15] = [lambda: emit_ATb(t2), lambda: emit_AH(t3)]
            plan[22] = [abs_step(t3)]
            plan[23] = [tmax_step(t3)]
            plan[24] = [ata_tensor(t3)]
            plan[31] = [lambda: emit_ATb(t3)]
            if sg < 2:
                plan[31].append(lambda: emit_AH(t2 + 2))
        for strip in range(32):
            emit_Cstrip(sg, strip)
            for f in plan.get(strip, ()):
                f()
    B.wait("sync", "fin", B.val("fin"))

    return _finish(K, tap_list, taps, locals())


def _finish(K, tap_list, taps, loc):
    B = K
    for (tname, lname, shape, dt) in taps:
        h = loc.get(lname)
        if h is None:
            continue
        dst = K.nc.dram_tensor(tname, list(shape), dt, kind="ExternalOutput").ap()
        B.wait("sync", "ve")
        B.wait("sync", "te")
        B.wait("sync", "se")
        B.wait("sync", "st")
        if isinstance(h, list):
            h = h[0]
        B.dma("sync", dst, h[:, 0:shape[1]] if len(shape) == 2 else h[:, :], "fin")
    B.wait("sync", "fin")

    nc = K.nc
    with nc.Block() as block:
        @block.sync
        def _(e):
            for f in K.ops["sync"]:
                f(e)

        @block.tensor
        def _(e):
            for f in K.ops["tensor"]:
                f(e)

        @block.vector
        def _(e):
            for f in K.ops["vector"]:
                f(e)

        @block.scalar
        def _(e):
            for f in K.ops["scalar"]:
                f(e)

        @block.gpsimd
        def _(e):
            for f in K.ops["gpsimd"]:
                f(e)
    for s in reversed(K.open_scopes):
        try:
            s.close()
        except Exception:
            pass
    K.root.close()
    return nc



def _prep_inputs(inputs):
    f32 = lambda a: np.ascontiguousarray(np.asarray(a, dtype=np.float32))
    x = f32(inputs["L0_attn_x"])
    Wq, Wk, Wv, Wo = (f32(inputs["L0_attn_" + n]) for n in ("Wq", "Wk", "Wv", "Wo"))
    qA, qB, kA, kB, vA, vB, oA, oB = (f32(inputs["L0_attn_" + n]) for n in
                                      ("qA", "qB", "kA", "kB", "vA", "vB", "oA", "oB"))
    wg, wu, wd = (f32(inputs["L0_ffn_w_gate"]), f32(inputs["L0_ffn_w_up"]),
                  f32(inputs["L0_ffn_w_down"]))

    permq = _deperm(16)
    permk = _deperm(8)
    WqT = np.ascontiguousarray(Wq.T[:, permq])
    qBT = np.ascontiguousarray(qB.T[:, permq])
    WkT = np.ascontiguousarray(Wk.T[:, permk])
    kBT = np.ascontiguousarray(kB.T[:, permk])
    WvT = np.ascontiguousarray(Wv.T)
    vBT = np.ascontiguousarray(vB.T)
    WoT = np.ascontiguousarray(Wo.T)
    oBT = np.ascontiguousarray(oB.T)
    def tile_gu(wT):
        t = wT.reshape(16, 128, 16, 512)
        return np.ascontiguousarray(t.transpose(2, 1, 0, 3)).reshape(16, 128, 8192)
    wgT = tile_gu(wg.T)
    wuT = tile_gu(wu.T)
    wdTt = wd.T.reshape(64, 128, 8, 256)
    wdT = np.ascontiguousarray(wdTt.transpose(2, 1, 0, 3))

    cosT, sinT = _rope_tables()
    cosT = np.ascontiguousarray(np.concatenate([cosT[:, :S // 2], cosT[:, S // 2:]], axis=0))
    sinT = np.ascontiguousarray(np.concatenate([sinT[:, :S // 2], sinT[:, S // 2:]], axis=0))
    identb = np.eye(128, dtype=np.float32).astype(ml_dtypes.bfloat16)
    identf = np.eye(128, dtype=np.float32)
    had = _hadamard(128).astype(ml_dtypes.bfloat16)
    masks = _diag_masks()

    in_maps = []
    for c in range(8):
        b, g = c // 4, c % 4
        gates = np.zeros((128, 8), np.float32)
        gates[:, 4 * b:4 * b + 4] = 1.0
        in_maps.append({
            "x": x[b],
            "xown": np.ascontiguousarray(x[b, TOK * g:TOK * (g + 1)]),
            "wqT": np.ascontiguousarray(WqT[:, 512 * g:512 * (g + 1)]),
            "wkT": np.ascontiguousarray(WkT[:, 256 * g:256 * (g + 1)]),
            "wvT": np.ascontiguousarray(WvT[:, 256 * g:256 * (g + 1)]),
            "woT": WoT,
            "qA": qA, "kA": kA, "vA": vA, "oA": oA,
            "qBT": np.ascontiguousarray(qBT[:, 512 * g:512 * (g + 1)]),
            "kBT": np.ascontiguousarray(kBT[:, 256 * g:256 * (g + 1)]),
            "vBT": np.ascontiguousarray(vBT[:, 256 * g:256 * (g + 1)]),
            "oBT": oBT,
            "wgT": wgT, "wuT": wuT, "wdT": wdT,
            "wg_s": np.ascontiguousarray(wg.reshape(8, 128, 16384)[c]),
            "wu_s": np.ascontiguousarray(wu.reshape(8, 128, 16384)[c]),
            "wd_s": np.ascontiguousarray(wd.reshape(8, 128, 16384)[c]),
            "identb": identb, "identf": identf,
            "cosT": cosT, "sinT": sinT, "masks": masks, "had": had,
            "a2agate": gates,
        })
    return in_maps


def kernel(**inputs):
    in_maps = _prep_inputs(inputs)
    nc = build_program(taps=list(_DEBUG_TAPS), stop=_DEBUG_STOP[0])
    res = run_bass_kernel_spmd(nc, in_maps, core_ids=list(range(8)))
    kernel._last_results = res
    y = np.zeros((2, S, HID), np.float32)
    for c in range(8):
        b, g = c // 4, c % 4
        y[b, TOK * g:TOK * (g + 1)] = res.results[c]["y"]
    return y


# version 12 (final)
# speedup vs baseline: 1.1109x; 1.0659x over previous
import numpy as np
import ml_dtypes

import concourse.bass as bass
import concourse.mybir as mybir
from concourse.bass_utils import run_bass_kernel_spmd
from concourse.aot_env import allow_ici_all_to_all
from concourse.ap import AP

F32 = mybir.dt.float32
BF16 = mybir.dt.bfloat16
I32 = mybir.dt.int32
AF = mybir.ActivationFunctionType
ALU = mybir.AluOpType
AX = mybir.AxisListType

HID = 2048
S = 4096
TOK = 1024
HD = 128
SCALING = float(1.0 / np.sqrt(HD))
SHIFT = -40.0
EPS = 1e-5
IFF = 8192
MASKV = -1.0e30
INV2048 = float(np.float32(1.0) / np.float32(2048.0))
INVWELEM = float(np.float32(1.0) / np.float32(16777216.0))
RSQ8192 = float(np.float32(1.0) / np.float32(np.sqrt(np.float32(8192.0))))
EPSRAW = float(np.float32(1e-5) * np.float32(np.sqrt(np.float32(8192.0))))

_DEBUG_TAPS = []
_DEBUG_STOP = [None]


def _rope_tables():
    inv_freq = (1.0 / 10000.0 ** (np.arange(0, HD, 2, dtype=np.float32) / HD)).astype(np.float32)
    t = np.arange(S, dtype=np.float32)
    fr = t[:, None] * inv_freq[None, :]
    return (np.ascontiguousarray(np.cos(fr).astype(np.float32).T),
            np.ascontiguousarray(np.sin(fr).astype(np.float32).T))


def _deperm(n_heads):
    p = []
    for h in range(n_heads):
        base = h * HD
        p.extend(base + np.arange(0, HD, 2))
        p.extend(base + np.arange(1, HD, 2))
    return np.array(p)


def _hadamard(n):
    H = np.array([[1.0]], dtype=np.float32)
    while H.shape[0] < n:
        H = np.block([[H, H], [H, -H]])
    return H.astype(np.float32)


def _diag_masks():
    m = np.zeros((128, 4 * 512), dtype=np.float32)
    col = np.arange(512)
    rows = np.arange(128)[:, None]
    for j in range(4):
        m[:, 512 * j:512 * (j + 1)] = np.where(col[None, :] >= 128 * j + rows, 0.0, MASKV)
    return m


def bc3(ap2d, rep):
    return AP(ap2d.tensor, ap2d.offset, [ap2d.ap[0], [0, rep], ap2d.ap[1]])


class KB:
    ENGINES = ("sync", "tensor", "vector", "scalar", "gpsimd")

    def __init__(self):
        self.nc = bass.Bass("TRN2", target_bir_lowering=False, debug=False)
        self.ops = {e: [] for e in self.ENGINES}
        self.cnt = {}
        self.sems = {}
        from contextlib import ExitStack
        self.root = ExitStack()
        self.open_scopes = []

    def ins(self, eng, fn, sem=None, k=1):
        if sem is None:
            self.ops[eng].append(lambda e, _f=fn: _f(e))
            return None
        self.cnt[sem] += k
        v = self.cnt[sem]

        def run(e, _f=fn, _s=sem, _k=k):
            _f(e).then_inc(self.sems[_s], _k)
        self.ops[eng].append(run)
        return v

    def dma(self, eng, out, in_, sem="ld"):
        return self.ins(eng, lambda e, o=out, i=in_: e.dma_start(out=o, in_=i), sem, 16)

    def stream(self, name, nbuf=2):
        return Stream(self, name, nbuf)

    def wait(self, eng, sem, v=None):
        if v is None:
            v = self.cnt[sem]
        if v <= 0:
            return
        self.ops[eng].append(lambda e, _s=sem, _v=v: e.wait_ge(self.sems[_s], _v))

    def val(self, sem):
        return self.cnt[sem]

    def barrier(self):
        for eng in self.ENGINES:
            for s in sorted(self.cnt):
                self.wait(eng, s)

    def sb(self, name, shape, dt, scope=None):
        st = scope if scope is not None else self.root
        t = st.enter_context(self.nc.sbuf_tensor(name, list(shape), dt))
        return t

    def psm(self, name, shape, dt):
        return self.root.enter_context(self.nc.psum_tensor(name, list(shape), dt))

    def scope(self):
        from contextlib import ExitStack
        s = ExitStack()
        self.open_scopes.append(s)
        return s


class Stream:

    def __init__(self, K, name, nbuf):
        self.K = K
        self.names = []
        for i in range(nbuf):
            n = f"{name}{i}"
            K.sems[n] = K.root.enter_context(K.nc.semaphore(n))
            K.cnt[n] = 0
            self.names.append(n)
        self.nbuf = nbuf

    def dma(self, eng, out, in_, slot=0):
        n = self.names[slot % self.nbuf]
        self.K.ins(eng, lambda e, o=out, i=in_: e.dma_start(out=o, in_=i), n, 16)
        return (n, self.K.cnt[n])

    def wait(self, eng, token):
        self.K.wait(eng, token[0], token[1])

    def wait_all(self, eng):
        for n in self.names:
            self.K.wait(eng, n, self.K.cnt[n])


def r3(t, a, b):
    return t[:, :].rearrange("p (a b) -> p a b", a=a)


def build_program(taps=(), stop=None):
    K = KB()
    nc = K.nc
    B = K

    def dram_in(name, shape, dt=F32):
        return nc.dram_tensor(name, list(shape), dt, kind="ExternalInput").ap()

    x_h = dram_in("x", (S, HID))
    xown_h = dram_in("xown", (TOK, HID))
    wqT_h = dram_in("wqT", (HID, 512))
    wkT_h = dram_in("wkT", (HID, 256))
    wvT_h = dram_in("wvT", (HID, 256))
    woT_h = dram_in("woT", (HID, HID))
    qA_h = dram_in("qA", (8, HID))
    kA_h = dram_in("kA", (8, HID))
    vA_h = dram_in("vA", (8, HID))
    oA_h = dram_in("oA", (8, HID))
    qBT_h = dram_in("qBT", (8, 512))
    kBT_h = dram_in("kBT", (8, 256))
    vBT_h = dram_in("vBT", (8, 256))
    oBT_h = dram_in("oBT", (8, HID))
    wgT_h = dram_in("wgT", (16, 128, 16 * 512))
    wuT_h = dram_in("wuT", (16, 128, 16 * 512))
    wdT_h = dram_in("wdT", (8, 128, 64, 256))
    wgs_h = dram_in("wg_s", (128, 16384))
    wus_h = dram_in("wu_s", (128, 16384))
    wds_h = dram_in("wd_s", (128, 16384))
    identb_h = dram_in("identb", (128, 128), BF16)
    identf_h = dram_in("identf", (128, 128))
    cos_h = dram_in("cosT", (128, S // 2))
    sin_h = dram_in("sinT", (128, S // 2))
    masks_h = dram_in("masks", (128, 2048))
    had_h = dram_in("had", (128, 128), BF16)
    gates_h = dram_in("a2agate", (128, 8))

    y_h = nc.dram_tensor("y", [TOK, HID], F32, kind="ExternalOutput").ap()

    a2a_in = nc.dram_tensor("a2a_in", [8, 512, TOK], BF16)
    a2a_out = nc.dram_tensor("a2a_out", [8, 512, TOK], BF16)
    st_in = nc.dram_tensor("st_in", [8, 1], F32)
    st_out = nc.dram_tensor("st_out", [8, 1], F32)
    x2_sp = nc.dram_tensor("x2_spill", [TOK, HID], F32)
    wd_tern = nc.dram_tensor("wd_tern", [8, 128, 64, 256], BF16)
    inter_sp = nc.dram_tensor("inter_spill", [TOK, IFF], BF16)

    c_identb = K.sb("c_identb", (128, 128), BF16)
    c_identf = K.sb("c_identf", (128, 128), F32)
    c_had = K.sb("c_had", (128, 128), BF16)
    c_onesb = K.sb("c_onesb", (128, 1), BF16)
    c_onesrow = K.sb("c_onesrow", (1, 128), F32)
    c_negrow = K.sb("c_negrow", (1, 256), F32)
    c_gates = K.sb("c_gates", (128, 8), F32)
    st_row = K.sb("st_row", (128, 16), F32)
    c_eps = K.sb("c_eps", (128, 2), F32)
    statacc = K.sb("statacc", (128, 8), F32)
    stat_tp = K.sb("stat_tp", (128, 8), F32)
    strow1 = K.sb("strow1", (1, 8), F32)
    lnstat = K.sb("lnstat", (128, 64), F32)
    lnsum = K.sb("lnsum", (128, 32), F32)
    lnsq = K.sb("lnsq", (128, 32), F32)
    lnm = K.sb("lnm", (128, 32), F32)
    lnr = K.sb("lnr", (128, 32), F32)
    sq_scr = K.sb("sq_scr", (128, HID), F32)
    qscal = K.sb("qscal", (128, 64), F32)
    srow = K.sb("srow", (1, 128), F32)
    sfrows = K.sb("sfrows", (1, 512), F32)

    PA = [K.psm(f"PA{i}", (128, 1024), F32) for i in range(2)]
    PB = [K.psm(f"PB{i}", (128, 1024), BF16) for i in range(2)]
    PC = K.psm("PC", (128, 512), F32)
    PR = K.psm("PR", (1, 512), F32)

    for n in ("ld", "gd", "te", "ve", "se", "st", "cc", "pt", "fz", "qv", "dn",
              "px", "fin"):
        K.sems[n] = K.root.enter_context(nc.semaphore(n))
        K.cnt[n] = 0

    tap_list = []

    def mark_tap(name, handle, shape):
        tap_list.append((nc.dram_tensor(name, list(shape), handle.dtype if hasattr(handle, "dtype") else F32,
                                        kind="ExternalOutput").ap(), handle, shape))

    S_c = B.stream("Sc", 1)
    for dst, src in ((c_identb, identb_h), (c_identf, identf_h), (c_had, had_h),
                     (c_gates, gates_h)):
        tk_consts = S_c.dma("sync", dst[:, :], src)
    B.ins("vector", lambda e: e.memset(c_onesb[:, :], 1.0), "ve")
    B.ins("vector", lambda e: e.memset(statacc[:, :], 0.0), "ve")
    B.ins("vector", lambda e: e.memset(st_row[:, :], 0.0), "ve")
    B.ins("vector", lambda e: e.memset(qscal[:, :], 0.0), "ve")
    B.wait("vector", "ve")
    B.ins("vector", lambda e: e.memset(c_eps[:, 0:1], EPS), "ve")
    B.ins("vector", lambda e: e.memset(c_eps[:, 1:2], SHIFT), "ve")
    B.ins("vector", lambda e: e.memset(c_onesrow[:, :], 1.0), "ve")
    B.ins("vector", lambda e: e.memset(c_negrow[:, :], -1.0), "ve")

    p2scope = K.scope()
    q_fm = K.sb("q_fm", (128, 4 * S), BF16, p2scope)
    k_fm = K.sb("k_fm", (128, 2 * S), BF16, p2scope)
    v_tok = K.sb("v_tok", (128, 2 * 32 * 128), BF16, p2scope)
    p1scope = K.scope()
    weff = K.sb("weff", (128, 16 * 1024), BF16, p1scope)
    c_cos = K.sb("c_cos", (128, S // 2), F32, p1scope)
    c_sin = K.sb("c_sin", (128, S // 2), F32, p1scope)
    xin = [K.sb(f"xin{i}", (128, HID), F32, p1scope) for i in range(2)]
    xn_tok = [K.sb(f"xntok{i}", (128, HID), BF16, p1scope) for i in range(2)]
    xnT = K.sb("xnT", (128, 16 * 512), BF16, p1scope)
    rope_t = K.sb("rope_t", (64, 2 * 512), F32, p1scope)
    v_stg = K.sb("v_stg", (128, 512), BF16, p1scope)
    stbuf2 = [K.sb(f"stbuf{i}", (128, HID), F32, p1scope) for i in range(2)]
    pre = K.scope()
    wstr = [K.sb(f"wstr{i}", (128, 1024), F32, pre) for i in range(2)]
    lora_a = K.sb("lora_a", (8, HID), F32, pre)
    lora_b = K.sb("lora_b", (8, 512), F32, pre)
    stbuf = xin


    B.wait("tensor", "ve", B.val("ve"))
    S_lo = B.stream("Slo", 1)
    S_wf = B.stream("Swf", 2)
    for (mi, (Ah, BTh, WTh, ncol, obase)) in enumerate((
            (qA_h, qBT_h, wqT_h, 512, 0),
            (kA_h, kBT_h, wkT_h, 256, 512),
            (vA_h, vBT_h, wvT_h, 256, 768))):
        B.wait("sync", "qv", B.val("qv"))
        S_lo.dma("sync", lora_a[:, 0:HID], Ah)
        tk_lo = S_lo.dma("sync", lora_b[:, 0:ncol], BTh)
        S_lo.wait("tensor", tk_lo)
        for ic in range(16):
            buf = wstr[ic % 2]
            if ic >= 2 or mi > 0:
                B.wait("sync", "qv", B.val("qv") - 1)
            tk_w = S_wf.dma("sync", buf[:, 0:ncol], WTh[128 * ic:128 * (ic + 1), :], ic)
            if ic >= 2 or mi > 0:
                B.wait("tensor", "qv", B.val("qv") - 1)
            B.ins("tensor", lambda e, i=ic, n=ncol: e.matmul(
                PA[i % 2][:, 0:n], lora_a[:, 128 * i:128 * (i + 1)],
                lora_b[:, 0:n], start=True, stop=True), "te")
            tev = B.val("te")
            S_wf.wait("vector", tk_w)
            B.wait("vector", "te", tev)
            B.ins("vector", lambda e, b=buf, i=ic, n=ncol, ob=obase: e.scalar_tensor_tensor(
                weff[:, 1024 * i + ob:1024 * i + ob + n], PA[i % 2][:, 0:n], 2.0,
                b[:, 0:n], ALU.mult, ALU.add), "ve")
            B.ins("vector", lambda e: e.engine_nop(), "qv")
    B.barrier()
    pre.close()


    S_cs = B.stream("Scs", 1)
    S_cs.dma("sync", c_cos[:, :], cos_h)
    tk_cs = S_cs.dma("sync", c_sin[:, :], sin_h)
    S_cs.wait("vector", tk_cs)
    S_x = B.stream("Sx", 2)

    xuse_marks = [0, 0]
    xuse_se = [0, 0]
    tpmk = [0, 0]
    qv_marks = [0, 0]

    S_st = B.stream("Sst", 2)
    st_srcs = (wgs_h, wus_h, wds_h)
    st_tk = {}

    def emit_stat_load(idx):
        m, ch = idx // 8, idx % 8
        buf = stbuf2[idx % 2]
        if idx >= 2:
            B.wait("gpsimd", "fz", B.val("fz"))
        st_tk[idx] = S_st.dma("gpsimd", buf[:, :],
                              st_srcs[m][:, 2048 * ch:2048 * (ch + 1)], idx)

    def emit_stat_red(idx):
        m, ch = idx // 8, idx % 8
        buf = stbuf2[idx % 2]
        S_st.wait("vector", st_tk[idx])
        if ch == 0:
            B.ins("vector", lambda e, b=buf, mm=m: e.reduce_sum(
                statacc[:, mm:mm + 1], b[:, :], axis=AX.X,
                apply_absolute_value=True), "ve")
        else:
            B.wait("vector", "ve")
            B.ins("vector", lambda e, b=buf: e.reduce_sum(
                stat_tp[:, 0:1], b[:, :], axis=AX.X,
                apply_absolute_value=True), "ve")
            B.wait("vector", "ve")
            B.ins("vector", lambda e, mm=m: e.tensor_add(
                statacc[:, mm:mm + 1], statacc[:, mm:mm + 1], stat_tp[:, 0:1]), "ve")
            B.wait("vector", "ve")
        B.ins("vector", lambda e: e.engine_nop(), "fz")

    emit_stat_load(0)
    emit_stat_load(1)

    for gt in range(32):
        xb = xin[gt % 2]
        B.wait("sync", "ve", xuse_marks[gt % 2])
        B.wait("sync", "se", xuse_se[gt % 2])
        tk_x = S_x.dma("sync", xb[:, :], x_h[128 * gt:128 * (gt + 1), :], gt)
        S_x.wait("vector", tk_x)
        S_x.wait("scalar", tk_x)
        B.ins("vector", lambda e, b=xb, g=gt: e.reduce_sum(
            lnsum[:, g:g + 1], b[:, :], axis=AX.X), "ve")
        xuse_marks[gt % 2] = B.val("ve")
        B.wait("scalar", "se", B.val("se"))
        B.ins("scalar", lambda e, b=xb, g=gt: e.activation(
            sq_scr[:, :], b[:, :], AF.Square, accum_out=lnsq[:, g:g + 1]), "se")
        xuse_se[gt % 2] = B.val("se")
        if gt < 24:
            emit_stat_red(gt)
            if gt + 2 < 24:
                emit_stat_load(gt + 2)
    B.wait("tensor", "ve", B.val("ve"))
    S_c.wait("tensor", tk_consts)
    B.ins("tensor", lambda e: e.transpose(
        PC[0:8, 0:128], statacc[:, 0:8], c_identf[:, :]), "te")
    B.wait("vector", "te", B.val("te"))
    B.ins("vector", lambda e: e.reduce_sum(
        stat_tp[0:8, 0:1], PC[0:8, 0:128], axis=AX.X), "ve")
    S_sm = B.stream("Ssm", 1)
    B.wait("gpsimd", "ve", B.val("ve"))
    tk = S_sm.dma("gpsimd", st_in.ap()[0:8, 0:1], stat_tp[0:8, 0:1])
    S_sm.wait("gpsimd", tk)
    B.ins("gpsimd", lambda e: e.collective_compute(
        "AllReduce", ALU.add, replica_groups=[list(range(8))],
        ins=[st_in.ap().opt()], outs=[st_out.ap().opt()]), "cc")
    B.wait("gpsimd", "cc")
    tk_row = S_sm.dma("gpsimd", strow1[0:1, 0:8], st_out.ap()[0:8, 0:1])
    B.wait("vector", "se", B.val("se"))
    B.wait("vector", "ve", B.val("ve"))
    B.ins("vector", lambda e: e.tensor_scalar(
        lnm[:, :], lnsum[:, :], INV2048, None, ALU.mult), "ve")
    B.wait("vector", "ve")
    B.ins("vector", lambda e: e.tensor_mul(lnr[:, :], lnm[:, :], lnm[:, :]), "ve")
    B.wait("vector", "ve")
    B.ins("vector", lambda e: e.scalar_tensor_tensor(
        lnr[:, :], lnsq[:, :], INV2048, lnr[:, :], ALU.mult, ALU.subtract), "ve")
    B.wait("scalar", "ve", B.val("ve"))
    B.ins("scalar", lambda e: e.activation(
        lnsq[:, :], lnr[:, :], AF.Sqrt, bias=c_eps[:, 0:1]), "se")
    B.wait("vector", "se", B.val("se"))
    B.ins("vector", lambda e: e.reciprocal(lnr[:, :], lnsq[:, :]), "ve")
    B.wait("vector", "ve")
    ve_lnchain = B.val("ve")

    for tg in range(8):
        for tl in range(4):
            gt = tg * 4 + tl
            xb = xin[gt % 2]
            B.wait("sync", "ve", xuse_marks[gt % 2])
            B.wait("sync", "se", xuse_se[gt % 2])
            tk_x = S_x.dma("sync", xb[:, :], x_h[128 * gt:128 * (gt + 1), :], gt)
            S_x.wait("vector", tk_x)
            B.ins("vector", lambda e, b=xb, g=gt, t=gt: e.tensor_scalar(
                xn_tok[t % 2][:, :], b[:, :], lnm[:, g:g + 1],
                lnr[:, g:g + 1], ALU.subtract, ALU.mult), "ve")
            xuse_marks[gt % 2] = B.val("ve")
            vev = B.val("ve")
            B.wait("tensor", "ve", vev)
            for bb in range(2):
                B.wait("tensor", "ve", tpmk[bb])
                for j in range(8):
                    ic = bb * 8 + j
                    B.ins("tensor", lambda e, t=gt, i=ic, b=bb, j_=j: e.transpose(
                        PB[b][:, 128 * j_:128 * (j_ + 1)],
                        xn_tok[t % 2][:, 128 * i:128 * (i + 1)], c_identb[:, :]), "te")
                tev = B.val("te")
                B.wait("vector", "te", tev)
                B.ins("vector", lambda e, tl_=tl, b=bb: e.tensor_copy(
                    r3(xnT, 16, 512)[:, 8 * b:8 * (b + 1), 128 * tl_:128 * (tl_ + 1)],
                    r3(PB[b], 8, 128)[:, :, :]), "ve")
                tpmk[bb] = B.val("ve")
        ve_xnT = B.val("ve")

        B.wait("tensor", "ve", ve_xnT)
        tglob = tg * 512
        for oc in range(8):
            unit = tg * 8 + oc
            pp = PA[unit % 2]
            B.wait("tensor", "qv", qv_marks[unit % 2])
            for ic in range(16):
                B.ins("tensor", lambda e, i=ic, o=oc, pb=pp: e.matmul(
                    pb[:, 0:512],
                    weff[:, 1024 * i + 128 * o:1024 * i + 128 * (o + 1)],
                    r3(xnT, 16, 512)[:, i, :],
                    start=(i == 0), stop=(i == 15)), "te")
            tev = B.val("te")
            B.wait("vector", "te", tev)
            if oc < 6:
                dst = q_fm if oc < 4 else k_fm
                hh = oc if oc < 4 else oc - 4
                pbs = 64 * (tg // 4)
                co = 512 * (tg % 4)
                cslice = c_cos[pbs:pbs + 64, co:co + 512]
                sslice = c_sin[pbs:pbs + 64, co:co + 512]
                B.ins("vector", lambda e, pb=pp, c=cslice: e.tensor_mul(
                    rope_t[:, 0:512], pb[0:64, 0:512], c), "ve")
                B.ins("vector", lambda e, pb=pp, s_=sslice: e.tensor_mul(
                    rope_t[:, 512:1024], pb[64:128, 0:512], s_), "ve")
                B.wait("vector", "ve")
                B.ins("vector", lambda e, d=dst, h=hh, t0=tglob: e.tensor_sub(
                    d[0:64, h * S + t0:h * S + t0 + 512],
                    rope_t[:, 0:512], rope_t[:, 512:1024]), "ve")
                B.wait("vector", "ve")
                B.ins("vector", lambda e, pb=pp, s_=sslice: e.tensor_mul(
                    rope_t[:, 0:512], pb[0:64, 0:512], s_), "ve")
                B.ins("vector", lambda e, pb=pp, c=cslice: e.tensor_mul(
                    rope_t[:, 512:1024], pb[64:128, 0:512], c), "ve")
                B.wait("vector", "ve")
                B.ins("vector", lambda e, d=dst, h=hh, t0=tglob: e.tensor_add(
                    d[64:128, h * S + t0:h * S + t0 + 512],
                    rope_t[:, 0:512], rope_t[:, 512:1024]), "ve")
                B.wait("vector", "ve")
                B.ins("vector", lambda e: e.engine_nop(), "qv")
                qv_marks[unit % 2] = B.val("qv")
            else:
                kvh = oc - 6
                B.ins("vector", lambda e, pb=pp: e.tensor_copy(
                    v_stg[:, :], pb[:, 0:512]), "ve")
                B.ins("vector", lambda e: e.engine_nop(), "qv")
                qv_marks[unit % 2] = B.val("qv")
                vev = B.val("ve")
                B.wait("tensor", "ve", vev)
                B.wait("tensor", "ve", tpmk[0])
                for j in range(4):
                    B.ins("tensor", lambda e, j_=j: e.transpose(
                        PB[0][:, 128 * j_:128 * (j_ + 1)],
                        v_stg[:, 128 * j_:128 * (j_ + 1)], c_identb[:, :]), "te")
                tev = B.val("te")
                B.wait("vector", "te", tev)
                ktb = tglob // 128
                B.ins("vector", lambda e, kv=kvh, kb=ktb: e.tensor_copy(
                    v_tok[:, :].rearrange("p (kv kt d) -> p kv kt d", kv=2, kt=32)[
                        :, kv, kb:kb + 4, :],
                    r3(PB[0], 8, 128)[:, 0:4, :]), "ve")
                tpmk[0] = B.val("ve")

    B.barrier()
    p1scope.close()
    if stop == "p1":
        return _finish(K, tap_list, taps, locals())

    p2b = K.scope()
    wdin = [K.sb(f"wdin{i}", (128, HID), F32, p2b) for i in range(2)]
    wdq = [K.sb(f"wdq{i}", (128, HID), BF16, p2b) for i in range(2)]
    c_masks = K.sb("c_masks", (128, 2048), F32, p2b)
    pt_bufs = [K.sb(f"ptbuf{i}", (128, 1024), BF16, p2b) for i in range(2)]
    an_f32 = K.sb("an_f32", (128, 512), F32, p2b)
    rs_sb = K.sb("rs_sb", (1, 512), F32, p2b)

    S_mk = B.stream("Smk", 1)
    tk_mk = S_mk.dma("sync", c_masks[:, :], masks_h)

    S_sm.wait("tensor", tk_row)
    B.wait("tensor", "ve", B.val("ve"))
    B.ins("tensor", lambda e: e.matmul(
        PC[:, 0:8], c_onesrow[0:1, :], strow1[0:1, 0:8], start=True, stop=True), "te")
    B.wait("vector", "te", B.val("te"))
    B.ins("vector", lambda e: e.tensor_copy(st_row[:, 0:8], PC[:, 0:8]), "ve")
    B.ins("vector", lambda e: e.engine_nop(), "qv")
    B.wait("vector", "ve")
    B.ins("vector", lambda e: e.tensor_scalar(
        st_row[:, 4:7], st_row[:, 0:3], INVWELEM, EPS, ALU.mult, ALU.max), "ve")
    B.wait("vector", "ve")
    B.ins("vector", lambda e: e.tensor_scalar(
        st_row[:, 8:11], st_row[:, 4:7], 0.5, None, ALU.mult), "ve")
    B.wait("vector", "ve")
    B.ins("vector", lambda e: e.tensor_scalar(
        st_row[:, 11:14], st_row[:, 8:11], -1.0, None, ALU.mult), "ve")
    ve_stats = B.val("ve")

    B.wait("vector", "ve", ve_stats)
    S_wd = B.stream("Swd", 2)
    S_do = B.stream("Sdo", 2)
    wd_marks = [None, None]
    wd_tk = {}

    def emit_p2a_load(blk):
        if wd_marks[blk % 2] is not None:
            S_do.wait("gpsimd", wd_marks[blk % 2])
        ocb0 = blk // 8
        ic0 = (blk % 8) * 8
        wd_tk[blk] = S_wd.dma("gpsimd", wdin[blk % 2][:, :],
                              wdT_h[ocb0, :, ic0:ic0 + 8, :], blk)

    def emit_p2a_tern(blk):
        buf = wdin[blk % 2]
        qb = wdq[blk % 2]
        ocb0 = blk // 8
        ic0 = (blk % 8) * 8
        S_wd.wait("vector", wd_tk[blk])
        B.ins("vector", lambda e, b=buf: e.tensor_scalar(
            sq_scr[:, :], b[:, :], st_row[:, 10:11], None, ALU.is_ge), "ve")
        B.wait("vector", "ve")
        B.ins("vector", lambda e, b=buf, q=qb: e.scalar_tensor_tensor(
            q[:, :], b[:, :], st_row[:, 13:14], sq_scr[:, :],
            ALU.is_gt, ALU.add), "ve")
        vev = B.val("ve")
        B.wait("vector", "ve", vev)
        B.wait("gpsimd", "ve", vev)
        wd_marks[blk % 2] = S_do.dma(
            "gpsimd", wd_tern.ap()[ocb0, :, ic0:ic0 + 8, :], qb[:, :], blk)

    S_mk.wait("vector", tk_mk)
    S_mk.wait("scalar", tk_mk)
    S_a2 = B.stream("Sa2", 2)
    an_outs = [K.sb(f"anout{i}", (128, 512), BF16, p2b) for i in range(2)]
    an_tok = [None, None]

    se_marks = [0, 0]
    pt_marks = [0, 0]
    pa0_guard = [0]
    qv_fin = [B.val("qv")]
    sched = []
    for h in range(4):
        for qc in range(8):
            ngrp = 2 * (qc + 1)
            for grp in range(ngrp):
                sched.append((h, qc, grp, ngrp))

    def emit_av(idx):
        h, qc, grp, ngrp = sched[idx]
        kv = h // 2
        g = idx
        first = grp == 0
        last = grp == ngrp - 1
        if first:
            B.wait("tensor", "qv", qv_fin[0])
        for t2 in range(2):
            kt = 2 * grp + t2
            s0 = first and t2 == 0
            s1 = last and t2 == 1
            B.ins("tensor", lambda e, g_=g, k=kt, t_=t2, kv_=kv, a=s0, b=s1: e.matmul(
                PC[:, 0:512],
                v_tok[:, :].rearrange("p (kv kt d) -> p kv kt d", kv=2, kt=32)[
                    :, kv_, k, :],
                pt_bufs[g_ % 2][:, 512 * t_:512 * (t_ + 1)],
                start=a, stop=b), "te")
            B.ins("tensor", lambda e, g_=g, t_=t2, a=s0, b=s1: e.matmul(
                PR[0:1, 0:512], c_onesb[:, 0:1],
                pt_bufs[g_ % 2][:, 512 * t_:512 * (t_ + 1)],
                start=a, stop=b), "te")
        B.ins("tensor", lambda e: e.nop(), "pt")
        pt_marks[g % 2] = B.val("pt")
        if last:
            tev = B.val("te")
            B.wait("vector", "te", tev)
            B.ins("vector", lambda e: e.reciprocal(
                rs_sb[0:1, 0:512], PR[0:1, 0:512]), "ve")
            B.wait("vector", "ve")
            B.ins("vector", lambda e: e.tensor_copy(an_f32[:, :], PC[:, 0:512]), "ve")
            B.ins("vector", lambda e: e.engine_nop(), "qv")
            qv_fin[0] = B.val("qv")
            vev = B.val("ve")
            B.wait("tensor", "ve", vev)
            B.wait("tensor", "se", B.val("se"))
            B.ins("tensor", lambda e: e.matmul(
                PA[0][:, 0:512], c_onesrow[0:1, :], rs_sb[0:1, 0:512],
                start=True, stop=True), "te")
            tev = B.val("te")
            B.wait("vector", "te", tev)
            hq = h * 8 + qc
            if an_tok[hq % 2] is not None:
                S_a2.wait("vector", an_tok[hq % 2])
            B.ins("vector", lambda e, p_=hq % 2: e.tensor_mul(
                an_outs[p_][:, :], an_f32[:, :], PA[0][:, 0:512]), "ve")
            pa0_guard[0] = B.val("ve")
            vev = B.val("ve")
            B.wait("sync", "ve", vev)
            sl = qc // 2
            co = 512 * (qc % 2)
            S_a2.dma("sync", a2a_in.ap()[sl, 128 * h:128 * (h + 1), co:co + 512],
                     an_outs[hq % 2][:, :], hq)
            an_tok[hq % 2] = S_a2.dma(
                "sync", a2a_in.ap()[sl + 4, 128 * h:128 * (h + 1), co:co + 512],
                an_outs[hq % 2][:, :], hq)

    emit_p2a_load(0)
    emit_p2a_load(1)
    for gidx, (h, qc, grp, ngrp) in enumerate(sched):
        kv = h // 2
        sp = PA[gidx % 2]
        B.wait("tensor", "se", se_marks[gidx % 2])
        if gidx % 2 == 0 and pa0_guard[0]:
            B.wait("tensor", "ve", pa0_guard[0])
            pa0_guard[0] = 0
        for t2 in range(2):
            kt = 2 * grp + t2
            B.ins("tensor", lambda e, s_=sp, k=kt, t_=t2, h_=h, q_=qc, kv_=kv: e.matmul(
                s_[:, 512 * t_:512 * (t_ + 1)],
                k_fm[:, kv_ * S + 128 * k:kv_ * S + 128 * (k + 1)],
                q_fm[:, h_ * S + 512 * q_:h_ * S + 512 * (q_ + 1)],
                start=True, stop=True), "te")
        tev = B.val("te")
        if grp >= ngrp - 2:
            j2 = grp - (ngrp - 2)
            B.wait("vector", "te", tev)
            B.ins("vector", lambda e, s_=sp, j_=j2: e.tensor_add(
                s_[:, 0:1024], s_[:, 0:1024],
                c_masks[:, 1024 * j_:1024 * (j_ + 1)]), "ve")
            B.wait("scalar", "ve", B.val("ve"))
        else:
            B.wait("scalar", "te", tev)
        B.wait("scalar", "pt", pt_marks[gidx % 2])
        B.ins("scalar", lambda e, s_=sp, g=gidx: e.activation(
            pt_bufs[g % 2][:, 0:1024], s_[:, 0:1024], AF.Exp,
            bias=c_eps[:, 1:2], scale=SCALING), "se")
        se_marks[gidx % 2] = B.val("se")
        if gidx >= 1:
            B.wait("tensor", "se", B.val("se") - 1)
            emit_av(gidx - 1)
        if gidx % 4 == 0 and gidx > 0 and gidx // 4 + 1 < 64:
            emit_p2a_load(gidx // 4 + 1)
        if gidx % 4 == 2 and gidx // 4 < 64:
            emit_p2a_tern(gidx // 4)
    B.wait("tensor", "se", B.val("se"))
    emit_av(len(sched) - 1)

    B.barrier()
    p2b.close()
    p2scope.close()
    if stop == "p2":
        return _finish(K, tap_list, taps, locals())

    p45scope = K.scope()
    xq2T = K.sb("xq2T", (128, 16 * TOK), BF16, p45scope)
    p34scope = K.scope()
    x2_sb = K.sb("x2_sb", (128, 8 * HID), F32, p34scope)
    p3scope = K.scope()
    av_fm = K.sb("av_fm", (128, 16 * TOK), BF16, p3scope)
    a2a_ld = [K.sb(f"a2ald{i}", (128, 4 * TOK), BF16, p3scope) for i in range(2)]
    wo_str = [K.sb(f"wostr{i}", (128, 16 * 256), F32, p3scope) for i in range(2)]
    wo_eff = K.sb("wo_eff", (128, 16 * 256), BF16, p3scope)
    lora_o = K.sb("lora_o", (8, HID), F32, p3scope)
    lora_ob = K.sb("lora_ob", (8, 256), F32, p3scope)

    S_lo2 = B.stream("Slo2", 1)
    tk_oa = S_lo2.dma("sync", lora_o[:, 0:HID], oA_h)
    S_a2.wait_all("gpsimd")
    def _a2a(e):
        with allow_ici_all_to_all():
            return e.collective_compute(
                "AllToAll", ALU.bypass, replica_groups=[list(range(8))],
                ins=[a2a_in.ap().opt()], outs=[a2a_out.ap().opt()])
    B.ins("gpsimd", _a2a, "cc")
    B.wait("sync", "cc", B.val("cc"))
    S_al = B.stream("Sal", 1)
    for i in range(4):
        if i >= 1:
            B.wait("sync", "ve", B.val("ve"))
        for hc4 in range(4):
            S_al.dma("sync", a2a_ld[0][:, 1024 * hc4:1024 * (hc4 + 1)],
                     a2a_out.ap()[i, 128 * hc4:128 * (hc4 + 1), :])
            tk_al = S_al.dma("sync", a2a_ld[1][:, 1024 * hc4:1024 * (hc4 + 1)],
                             a2a_out.ap()[i + 4, 128 * hc4:128 * (hc4 + 1), :])
        S_al.wait("vector", tk_al)
        B.ins("vector", lambda e, i_=i: e.tensor_scalar(
            a2a_ld[1][:, :], a2a_ld[1][:, :], c_gates[:, i_ + 4:i_ + 5], None,
            ALU.mult), "ve")
        B.wait("vector", "ve")
        B.ins("vector", lambda e, i_=i: e.scalar_tensor_tensor(
            av_fm[:, 4096 * i_:4096 * (i_ + 1)], a2a_ld[0][:, :],
            c_gates[:, i_:i_ + 1], a2a_ld[1][:, :], ALU.mult, ALU.add), "ve")
    ve_av = B.val("ve")

    S_xi = B.stream("Sxi", 1)
    for tt in range(8):
        tk_x2i = S_xi.dma("sync", x2_sb[:, 2048 * tt:2048 * (tt + 1)],
                          xown_h[128 * tt:128 * (tt + 1), :])

    B.wait("tensor", "ve", ve_av)
    S_wb = B.stream("Swb", 2)
    wo_dn_marks = [0, 0]
    for ocb in range(8):
        wsb = wo_str[ocb % 2]
        if ocb >= 2:
            B.wait("sync", "qv", B.val("qv") - 1)
        B.wait("sync", "fz", B.val("fz"))
        S_wb.dma("sync", lora_ob[:, 0:256], oBT_h[0:8, 256 * ocb:256 * (ocb + 1)], ocb)
        for hc in range(16):
            tk_wo = S_wb.dma("sync", wsb[:, 256 * hc:256 * (hc + 1)],
                             woT_h[128 * hc:128 * (hc + 1), 256 * ocb:256 * (ocb + 1)], ocb)
        B.wait("tensor", "dn", B.val("dn"))
        S_wb.wait("tensor", tk_wo)
        if ocb == 0:
            S_lo2.wait("tensor", tk_oa)
        for hc in range(16):
            if not (ocb == 0 and hc < 2):
                B.wait("tensor", "fz", B.val("fz") - 1)
            B.ins("tensor", lambda e, h_=hc: e.matmul(
                PA[h_ % 2][:, 0:256], lora_o[:, 128 * h_:128 * (h_ + 1)],
                lora_ob[:, 0:256], start=True, stop=True), "te")
            tev = B.val("te")
            B.wait("vector", "te", tev)
            B.ins("vector", lambda e, h_=hc, w=wsb: e.scalar_tensor_tensor(
                wo_eff[:, 256 * h_:256 * (h_ + 1)], PA[h_ % 2][:, 0:256], 2.0,
                w[:, 256 * h_:256 * (h_ + 1)], ALU.mult, ALU.add), "ve")
            B.ins("vector", lambda e: e.engine_nop(), "fz")
        ve_woe = B.val("ve")
        B.wait("tensor", "ve", ve_woe)
        B.ins("vector", lambda e: e.engine_nop(), "qv")
        for tt in range(8):
            pp = PA[tt % 2]
            B.wait("tensor", "dn", wo_dn_marks[tt % 2])
            for hc in range(16):
                B.ins("tensor", lambda e, h_=hc, t=tt, pb=pp: e.matmul(
                    pb[:, 0:256],
                    av_fm[:, 1024 * h_ + 128 * t:1024 * h_ + 128 * (t + 1)],
                    wo_eff[:, 256 * h_:256 * (h_ + 1)],
                    start=(h_ == 0), stop=(h_ == 15)), "te")
            tev = B.val("te")
            B.wait("vector", "te", tev)
            if ocb == 0 and tt == 0:
                S_xi.wait("vector", tk_x2i)
            B.ins("vector", lambda e, t=tt, o=ocb, pb=pp: e.tensor_add(
                x2_sb[:, 2048 * t + 256 * o:2048 * t + 256 * (o + 1)],
                x2_sb[:, 2048 * t + 256 * o:2048 * t + 256 * (o + 1)],
                pb[:, 0:256]), "ve")
            B.ins("vector", lambda e: e.engine_nop(), "dn")
            wo_dn_marks[tt % 2] = B.val("dn")
            if ocb == 7:
                xsl = x2_sb[:, 2048 * tt:2048 * (tt + 1)]
                B.wait("vector", "ve")
                B.ins("vector", lambda e, b=xsl, t=tt: e.reduce_sum(
                    lnstat[:, t:t + 1], b, axis=AX.X), "ve")
                B.wait("scalar", "ve", B.val("ve"))
                B.wait("scalar", "se")
                B.ins("scalar", lambda e, b=xsl, t=tt: e.activation(
                    sq_scr[:, :], b, AF.Square,
                    accum_out=lnstat[:, 8 + t:9 + t]), "se")
    ve_x2 = B.val("ve")
    S_xs = B.stream("Sxs", 1)
    B.wait("gpsimd", "ve", ve_x2)
    for tt in range(8):
        S_xs.dma("gpsimd", x2_sp.ap()[128 * tt:128 * (tt + 1), :],
                 x2_sb[:, 2048 * tt:2048 * (tt + 1)])

    B.barrier()
    p3scope.close()
    if stop == "p3":
        return _finish(K, tap_list, taps, locals())

    p4scope = K.scope()
    xq_f32 = K.sb("xq_f32", (128, HID), F32, p4scope)
    xq_i32 = K.sb("xq_i32", (128, HID), I32, p4scope)
    xq_tok = K.sb("xq_tok", (128, HID), BF16, p4scope)

    B.wait("vector", "se", B.val("se"))
    B.ins("vector", lambda e: e.tensor_scalar(
        lnstat[:, 16:24], lnstat[:, 0:8], INV2048, None, ALU.mult), "ve")
    B.wait("vector", "ve")
    B.ins("vector", lambda e: e.tensor_mul(
        lnstat[:, 24:32], lnstat[:, 16:24], lnstat[:, 16:24]), "ve")
    B.wait("vector", "ve")
    B.ins("vector", lambda e: e.scalar_tensor_tensor(
        lnstat[:, 32:40], lnstat[:, 8:16], INV2048,
        lnstat[:, 24:32], ALU.mult, ALU.subtract), "ve")
    B.wait("scalar", "ve", B.val("ve"))
    B.ins("scalar", lambda e: e.activation(
        lnstat[:, 40:48], lnstat[:, 32:40], AF.Sqrt, bias=c_eps[:, 0:1]), "se")
    B.wait("vector", "se", B.val("se"))
    B.ins("vector", lambda e: e.reciprocal(
        lnstat[:, 48:56], lnstat[:, 40:48]), "ve")
    B.wait("vector", "ve")

    p4tm = [0]
    q3 = qscal[:, :].rearrange("p (t k) -> p t k", k=8)
    ln3 = lnstat[:, :].rearrange("p (k t) -> p k t", k=8)
    for tt in range(8):
        xsl = x2_sb[:, 2048 * tt:2048 * (tt + 1)]
        B.ins("vector", lambda e, b=xsl, t=tt: e.tensor_scalar(
            xq_f32[:, :], b, lnstat[:, 16 + t:17 + t], lnstat[:, 48 + t:49 + t],
            ALU.subtract, ALU.mult), "ve")
        B.wait("vector", "ve")
        B.ins("vector", lambda e, t=tt: e.reduce_max(
            qscal[:, 8 * t + 6:8 * t + 7], xq_f32[:, :], axis=AX.X,
            apply_absolute_value=True), "ve")
        B.wait("vector", "ve")
    B.ins("vector", lambda e: e.tensor_scalar(
        q3[:, :, 0:1], q3[:, :, 6:7], EPS, None, ALU.max), "ve")
    B.wait("vector", "ve")
    B.ins("vector", lambda e: e.reciprocal(q3[:, :, 6:7], q3[:, :, 0:1]), "ve")
    B.wait("vector", "ve")
    B.ins("vector", lambda e: e.tensor_scalar(
        q3[:, :, 1:2], q3[:, :, 6:7], 128.0, None, ALU.mult), "ve")
    B.wait("vector", "ve")
    B.ins("vector", lambda e: e.tensor_tensor(
        ln3[:, 7:8, :], ln3[:, 6:7, :],
        qscal[:, :].rearrange("p (t k) -> p k t", t=8)[:, 1:2, :], ALU.mult), "ve")
    B.wait("vector", "ve")
    for tt in range(8):
        xsl = x2_sb[:, 2048 * tt:2048 * (tt + 1)]
        B.ins("vector", lambda e, b=xsl, t=tt: e.tensor_scalar(
            xq_f32[:, :], b, lnstat[:, 16 + t:17 + t], lnstat[:, 56 + t:57 + t],
            ALU.subtract, ALU.mult), "ve")
        B.wait("vector", "ve")
        B.ins("vector", lambda e: e.tensor_copy(xq_i32[:, :], xq_f32[:, :]), "ve")
        B.wait("vector", "ve")
        B.wait("vector", "te", p4tm[0])
        B.ins("vector", lambda e: e.tensor_copy(xq_tok[:, :], xq_i32[:, :]), "ve")
        B.wait("vector", "ve")
        B.ins("vector", lambda e, t=tt: e.tensor_scalar(
            xq_tok[:, :], xq_tok[:, :], 127.0, None, ALU.min, ALU.add,
            accum_out=qscal[:, 8 * t + 7:8 * t + 8]), "ve")
        vev = B.val("ve")
        B.wait("tensor", "ve", vev)
        for bb in range(2):
            for j in range(8):
                ic = bb * 8 + j
                B.ins("tensor", lambda e, i=ic, b=bb, j_=j: e.transpose(
                    PB[b][:, 128 * j_:128 * (j_ + 1)],
                    xq_tok[:, 128 * i:128 * (i + 1)], c_identb[:, :]), "te")
            tev = B.val("te")
            p4tm[0] = tev
            B.wait("scalar", "te", tev)
            B.ins("scalar", lambda e, t=tt, b=bb: e.copy(
                r3(xq2T, 16, TOK)[:, 8 * b:8 * (b + 1), 128 * t:128 * (t + 1)],
                r3(PB[b], 8, 128)[:, :, :]), "se")
            sev = B.val("se")
            B.wait("tensor", "se", sev)
    B.wait("vector", "ve")
    B.ins("vector", lambda e: e.tensor_scalar(
        q3[:, :, 6:7], q3[:, :, 0:1], 1.0 / 128.0, None, ALU.mult), "ve")
    B.wait("vector", "ve")
    B.ins("vector", lambda e: e.tensor_tensor(
        q3[:, :, 2:3], q3[:, :, 6:7], bc3(st_row[:, 4:5], 8), ALU.mult), "ve")
    B.ins("vector", lambda e: e.tensor_tensor(
        q3[:, :, 4:5], q3[:, :, 6:7], bc3(st_row[:, 5:6], 8), ALU.mult), "ve")
    B.wait("vector", "ve")
    B.ins("vector", lambda e: e.scalar_tensor_tensor(
        q3[:, :, 3:4], q3[:, :, 2:3], -1.0, q3[:, :, 7:8], ALU.mult, ALU.mult), "ve")
    B.ins("vector", lambda e: e.scalar_tensor_tensor(
        q3[:, :, 5:6], q3[:, :, 4:5], -1.0, q3[:, :, 7:8], ALU.mult, ALU.mult), "ve")
    B.wait("vector", "ve")

    B.barrier()
    p4scope.close()
    p34scope.close()
    if stop == "p4":
        return _finish(K, tap_list, taps, locals())

    p5scope = K.scope()
    gu_str = [K.sb(f"gustr{i}", (128, 16 * 512), F32, p5scope) for i in range(2)]
    gu_q = [K.sb(f"guq{i}", (128, 16 * 512), BF16, p5scope) for i in range(2)]
    silu_all = K.sb("silu_all", (128, 8 * 512), F32, p5scope)
    up_sb2 = [K.sb(f"upsb{i}", (128, 512), F32, p5scope) for i in range(2)]
    int_sb = [K.sb(f"intsb{i}", (128, 512), BF16, p5scope) for i in range(2)]

    S_gu = B.stream("Sgu", 2)
    S_is = B.stream("Sis", 2)
    p5qv_marks = [None, None]
    int_marks = [None, None]
    silu_rd = [0] * 8
    tern_done = {}

    def blk_src(b):
        return (wgT_h if b % 2 == 0 else wuT_h), b // 2

    tern_rd = {}
    gu_te_marks = [0, 0]

    def emit_load(b):
        wsrc, ocb = blk_src(b)
        eng = "sync" if b % 2 == 0 else "scalar"
        B.wait(eng, "dn", tern_rd.get(b - 2, 0))
        return S_gu.dma(eng, gu_str[b % 2][:, :], wsrc[ocb, :, :], b)

    def emit_tern(b):
        tc, nc_ = (8, 11) if b % 2 == 0 else (9, 12)
        wsb, wqb = gu_str[b % 2], gu_q[b % 2]
        S_gu.wait("vector", tern_done[b])
        B.wait("vector", "te", gu_te_marks[b % 2])
        B.ins("vector", lambda e, w=wsb, t_=tc: e.tensor_scalar(
            sq_scr[:, 0:2048], w[:, 0:2048], st_row[:, t_:t_ + 1], None,
            ALU.is_ge), "ve")
        B.ins("vector", lambda e, w=wsb, t_=tc: e.tensor_scalar(
            xq_f32g[:, 0:2048], w[:, 2048:4096], st_row[:, t_:t_ + 1], None,
            ALU.is_ge), "ve")
        B.wait("vector", "ve")
        B.ins("vector", lambda e, w=wsb, q=wqb, n_=nc_: e.scalar_tensor_tensor(
            q[:, 0:2048], w[:, 0:2048], st_row[:, n_:n_ + 1],
            sq_scr[:, 0:2048], ALU.is_gt, ALU.add), "ve")
        B.ins("vector", lambda e, w=wsb, q=wqb, n_=nc_: e.scalar_tensor_tensor(
            q[:, 2048:4096], w[:, 2048:4096], st_row[:, n_:n_ + 1],
            xq_f32g[:, 0:2048], ALU.is_gt, ALU.add), "ve")
        B.wait("vector", "ve")
        B.ins("vector", lambda e, w=wsb, t_=tc: e.tensor_scalar(
            sq_scr[:, 0:2048], w[:, 4096:6144], st_row[:, t_:t_ + 1], None,
            ALU.is_ge), "ve")
        B.ins("vector", lambda e, w=wsb, t_=tc: e.tensor_scalar(
            xq_f32g[:, 0:2048], w[:, 6144:8192], st_row[:, t_:t_ + 1], None,
            ALU.is_ge), "ve")
        B.wait("vector", "ve")
        B.ins("vector", lambda e, w=wsb, q=wqb, n_=nc_: e.scalar_tensor_tensor(
            q[:, 4096:6144], w[:, 4096:6144], st_row[:, n_:n_ + 1],
            sq_scr[:, 0:2048], ALU.is_gt, ALU.add), "ve")
        B.ins("vector", lambda e, w=wsb, q=wqb, n_=nc_: e.scalar_tensor_tensor(
            q[:, 6144:8192], w[:, 6144:8192], st_row[:, n_:n_ + 1],
            xq_f32g[:, 0:2048], ALU.is_gt, ALU.add), "ve")
        B.wait("vector", "ve")
        B.ins("vector", lambda e: e.engine_nop(), "dn")
        tern_rd[b] = B.val("dn")
        return B.val("ve")

    xq_f32g = K.sb("xq_f32g", (128, 2048), F32, p5scope)

    tern_done[0] = emit_load(0)
    tern_done[1] = emit_load(1)
    tern_ve = {}
    tern_ve[0] = emit_tern(0)
    for b in range(32):
        gate = b % 2 == 0
        wqb = gu_q[b % 2]
        if b + 2 < 32:
            tern_done[b + 2] = emit_load(b + 2)
        if b + 1 < 32:
            tern_ve[b + 1] = emit_tern(b + 1)
        B.wait("tensor", "ve", tern_ve[b])
        for tt in range(8):
            po = 512 * (tt % 2)
            PAx = PA[0] if gate else PA[1]
            if p5qv_marks[tt % 2] is not None:
                B.wait("tensor", *p5qv_marks[tt % 2])
            for hc in range(16):
                B.ins("tensor", lambda e, h_=hc, t=tt, q=wqb, o=po, pb=PAx: e.matmul(
                    pb[:, o:o + 512],
                    r3(xq2T, 16, TOK)[:, h_, 128 * t:128 * (t + 1)],
                    q[:, 512 * h_:512 * (h_ + 1)],
                    start=(h_ == 0), stop=(h_ == 15)), "te")
            tev = B.val("te")
            if gate:
                B.wait("scalar", "te", tev)
                B.wait("scalar", "ve", silu_rd[tt])
                B.ins("scalar", lambda e, t=tt, o=po: e.activation(
                    silu_all[:, 512 * t:512 * (t + 1)], PA[0][:, o:o + 512], AF.Silu,
                    bias=qscal[:, 8 * t + 3:8 * t + 4],
                    scale=qscal[:, 8 * t + 2:8 * t + 3]), "se")
                p5qv_marks[tt % 2] = ("se", B.val("se"))
            else:
                B.wait("vector", "te", tev)
                B.ins("vector", lambda e, t=tt, o=po: e.tensor_scalar(
                    up_sb2[t % 2][:, :], PA[1][:, o:o + 512],
                    qscal[:, 8 * t + 4:8 * t + 5],
                    qscal[:, 8 * t + 5:8 * t + 6], ALU.mult, ALU.add), "ve")
                p5qv_marks[tt % 2] = ("ve", B.val("ve"))
                B.wait("vector", "ve")
                B.wait("vector", "se", B.val("se"))
                if int_marks[tt % 2] is not None:
                    S_is.wait("vector", int_marks[tt % 2])
                B.ins("vector", lambda e, t=tt: e.tensor_mul(
                    int_sb[t % 2][:, :], silu_all[:, 512 * t:512 * (t + 1)],
                    up_sb2[t % 2][:, :]), "ve")
                silu_rd[tt] = B.val("ve")
                vev = B.val("ve")
                B.wait("sync", "ve", vev)
                int_marks[tt % 2] = S_is.dma(
                    "sync", inter_sp.ap()[128 * tt:128 * (tt + 1),
                                          512 * (b // 2):512 * (b // 2 + 1)],
                    int_sb[tt % 2][:, :], tt)
        gu_te_marks[b % 2] = B.val("te")

    B.barrier()
    p5scope.close()
    p45scope.close()
    if stop == "p5":
        return _finish(K, tap_list, taps, locals())

    p6scope = K.scope()
    intr_ld = K.sb("intr_ld", (128, IFF), BF16, p6scope)
    itT = K.sb("itT", (128, 64 * 128), BF16, p6scope)
    fwX = [K.sb(f"fwx{i}", (128, 64 * 128), BF16, p6scope) for i in range(2)]
    fw_scr = K.sb("fw_scr", (128, 64 * 128), BF16, p6scope)
    f32scr = K.sb("f32scr", (128, 512), F32, p6scope)
    i32scr = K.sb("i32scr", (128, 512), I32, p6scope)
    m2st = K.sb("m2st", (128, 32), F32, p6scope)
    xqf2 = [K.sb(f"xqf{i}", (128, 64 * 256), BF16, p6scope) for i in range(2)]
    wd_str = [K.sb(f"wdstr{i}", (128, 16 * 256), BF16, p6scope) for i in range(3)]
    ysg = K.sb("ysg", (128, 2 * HID), F32, p6scope)

    S_is.wait_all("gpsimd")
    S_il = B.stream("Sil", 1)
    S_wl = B.stream("Swl", 3)
    S_yi = B.stream("Syi", 1)
    tp_mark = [0]
    pbm = [None, None]
    px_mark = [0]
    s1m = [None, None]
    p6dn_marks = [0, 0]
    cmark = [0, 0]
    qdone = [None, None]
    afin_mark = [0, 0]
    pr_mark = [0]
    tail_te = [0]
    ysg_mark = [0]
    B.wait("tensor", "ve", B.val("ve"))

    def emit_AH(t):
        B.wait("gpsimd", "te", tp_mark[0])
        tk_il = S_il.dma("gpsimd", intr_ld[:, :],
                         inter_sp.ap()[128 * t:128 * (t + 1), :])
        S_il.wait("tensor", tk_il)
        evse = []
        for bb in range(8):
            if pbm[bb % 2] is not None:
                B.wait("tensor", *pbm[bb % 2])
            if bb == 0:
                B.wait("tensor", "px", px_mark[0])
            for j in range(8):
                ic = bb * 8 + j
                B.ins("tensor", lambda e, i=ic, pb=bb % 2, j_=j: e.transpose(
                    PB[pb][:, 128 * j_:128 * (j_ + 1)],
                    intr_ld[:, 128 * i:128 * (i + 1)], c_identb[:, :]), "te")
            tev = B.val("te")
            if bb == 7:
                tp_mark[0] = tev
            B.wait("scalar", "te", tev)
            B.ins("scalar", lambda e, b=bb: e.copy(
                itT[:, 1024 * b:1024 * (b + 1)], PB[b % 2][:, 0:1024]), "se")
            pbm[bb % 2] = ("se", B.val("se"))
            evse.append(B.val("se"))
        fb = fwX[t % 2]
        if qdone[t % 2] is not None:
            B.wait("scalar", *qdone[t % 2])
        for m in range(16):
            po = 512 * (m % 2)
            if s1m[m % 2] is not None:
                B.wait("tensor", *s1m[m % 2])
            B.wait("tensor", "se", evse[m // 2])
            B.ins("tensor", lambda e, m_=m, o=po: e.matmul(
                PA[0][:, o:o + 512], c_had[:, :], itT[:, 512 * m_:512 * (m_ + 1)],
                start=True, stop=True), "te")
            tev = B.val("te")
            B.wait("scalar", "te", tev)
            B.ins("scalar", lambda e, m_=m, o=po, f=fb: e.copy(
                f[:, 512 * m_:512 * (m_ + 1)], PA[0][:, o:o + 512]), "se")
            s1m[m % 2] = ("se", B.val("se"))

    def bf_steps(t):
        fb = fwX[t % 2]
        steps = []

        def mk(s):
            def go():
                if s == 0:
                    B.wait("vector", "se", B.val("se"))
                    B.wait("vector", "te", tail_te[0])
                src_ = fb if s % 2 == 0 else fw_scr
                dst_ = fw_scr if s % 2 == 0 else fb
                ps_ = 1 << s
                nh_ = 64 // (2 * ps_)
                sa = src_[:, :].rearrange("p (hi two lo u) -> p hi two lo u",
                                          hi=nh_, two=2, lo=ps_)
                da = dst_[:, :].rearrange("p (hi two lo u) -> p hi two lo u",
                                          hi=nh_, two=2, lo=ps_)
                B.ins("vector", lambda e, s_=sa, d=da: e.tensor_add(
                    d[:, :, 0, :, :], s_[:, :, 0, :, :], s_[:, :, 1, :, :]), "ve")
                B.ins("vector", lambda e, s_=sa, d=da: e.tensor_sub(
                    d[:, :, 1, :, :], s_[:, :, 0, :, :], s_[:, :, 1, :, :]), "ve")
                B.wait("vector", "ve")
            return go
        for s in range(6):
            steps.append(mk(s))
        return steps

    def abs_step(t):
        def go():
            fb = fwX[t % 2]
            B.wait("scalar", "ve", B.val("ve"))
            B.ins("scalar", lambda e, f=fb: e.activation(
                fw_scr[:, :], f[:, :], AF.Abs), "se")
        return go

    def tmax_step(t):
        def go():
            B.wait("vector", "se", B.val("se"))
            ln = 32 * 128
            while ln >= 128:
                B.ins("vector", lambda e, l=ln: e.tensor_max(
                    fw_scr[:, 0:l], fw_scr[:, 0:l], fw_scr[:, l:2 * l]), "ve")
                B.wait("vector", "ve")
                ln //= 2
        return go

    def ata_tensor(t):
        def go():
            mb = 16 * ((t // 2) % 2) + 8 * (t % 2)
            B.wait("tensor", "ve", B.val("ve"))
            B.ins("tensor", lambda e: e.transpose(
                PB[0][:, 0:128], fw_scr[:, 0:128], c_identb[:, :]), "te")
            tail_te[0] = B.val("te")
            B.wait("vector", "te", B.val("te"))
            B.ins("vector", lambda e, m_=mb: e.reduce_max(
                m2st[:, m_ + 0:m_ + 1], PB[0][:, 0:128], axis=AX.X), "ve")
            B.ins("vector", lambda e: e.engine_nop(), "px")
            px_mark[0] = B.val("px")
            B.wait("vector", "ve")
            B.ins("vector", lambda e, m_=mb: e.tensor_scalar(
                m2st[:, m_ + 0:m_ + 1], m2st[:, m_ + 0:m_ + 1], EPSRAW, None,
                ALU.max), "ve")
            B.wait("vector", "ve")
            B.ins("vector", lambda e, m_=mb: e.reciprocal(
                m2st[:, m_ + 1:m_ + 2], m2st[:, m_ + 0:m_ + 1]), "ve")
            B.wait("vector", "ve")
            B.ins("vector", lambda e, m_=mb: e.tensor_scalar(
                m2st[:, m_ + 1:m_ + 2], m2st[:, m_ + 1:m_ + 2], 128.0, None,
                ALU.mult), "ve")
            B.wait("vector", "ve")
            B.ins("vector", lambda e, m_=mb: e.tensor_scalar(
                m2st[:, m_ + 2:m_ + 3], m2st[:, m_ + 0:m_ + 1],
                RSQ8192 / 128.0, None, ALU.mult), "ve")
            B.wait("vector", "ve")
            B.ins("vector", lambda e, m_=mb: e.tensor_mul(
                m2st[:, m_ + 2:m_ + 3], m2st[:, m_ + 2:m_ + 3], st_row[:, 6:7]), "ve")
            vev = B.val("ve")
            B.wait("tensor", "ve", vev)
            B.ins("tensor", lambda e, m_=mb: e.transpose(
                PC[0:1, 384:512], m2st[:, m_ + 1:m_ + 2], c_identf[:, :]), "te")
            B.wait("vector", "te", B.val("te"))
            B.ins("vector", lambda e: e.tensor_copy(srow[0:1, 0:128], PC[0:1, 384:512]), "ve")
            B.wait("tensor", "ve", B.val("ve"))
            B.ins("tensor", lambda e: e.matmul(
                PC[:, 128:256], c_onesrow[0:1, :], srow[0:1, 0:128],
                start=True, stop=True), "te")
            B.wait("vector", "te", B.val("te"))
            B.wait("vector", "te", cmark[(t // 2) % 2])
        return go

    def quant_steps(t):
        fb = fwX[t % 2]
        xqf = xqf2[(t // 2) % 2]
        steps = []

        def mk(lo, hi, last):
            def go():
                for sl4 in range(lo, hi):
                    fslice = fb[:, 512 * sl4:512 * (sl4 + 1)]
                    B.ins("vector", lambda e, f=fslice: e.tensor_tensor(
                        f32scr[:, :].rearrange("p (i u) -> p i u", i=4),
                        f.rearrange("p (i u) -> p i u", i=4),
                        bc3(PC[:, 128:256], 4), ALU.mult), "ve")
                    B.wait("vector", "ve")
                    B.ins("vector", lambda e: e.tensor_copy(i32scr[:, :], f32scr[:, :]), "ve")
                    B.wait("vector", "ve")
                    B.ins("vector", lambda e, tl=t % 2, s4=sl4, x=xqf: e.tensor_scalar(
                        r3(x, 64, 256)[:, 4 * s4:4 * (s4 + 1), 128 * tl:128 * (tl + 1)],
                        i32scr[:, :].rearrange("p (i u) -> p i u", i=4),
                        127.0, None, ALU.min), "ve")
                    B.wait("vector", "ve")
                if last:
                    qdone[t % 2] = ("ve", B.val("ve"))
            return go
        for k in range(6):
            lo = (16 * k) // 6
            hi = (16 * (k + 1)) // 6
            steps.append(mk(lo, hi, k == 5))
        return steps

    def emit_ATb(t):
        xqf = xqf2[(t // 2) % 2]
        B.wait("tensor", *qdone[t % 2])
        B.wait("tensor", "ve", pr_mark[0])
        for icm in range(64):
            B.ins("tensor", lambda e, i=icm, tl=t % 2, x=xqf: e.matmul(
                PR[0:1, 0:128], c_onesb[:, 0:1],
                r3(x, 64, 256)[:, i, 128 * tl:128 * (tl + 1)],
                start=(i == 0), stop=(i == 63)), "te")
        B.wait("vector", "te", B.val("te"))
        B.ins("vector", lambda e, c=256 * ((t // 2) % 2) + 128 * (t % 2): e.tensor_copy(
            sfrows[0:1, c:c + 128], PR[0:1, 0:128]), "ve")
        pr_mark[0] = B.val("ve")
        if t % 2 == 1:
            afin_mark[(t // 2) % 2] = B.val("ve")

    def emit_ATa(t):
        for f in bf_steps(t):
            f()
        abs_step(t)()
        tmax_step(t)()
        ata_tensor(t)()
        for f in quant_steps(t):
            f()

    wd_mk = [0, 0, 0]

    def emit_Cstrip(sg, strip):
        xqf = xqf2[sg % 2]
        mbase = 16 * (sg % 2)
        ocb, qh = strip // 4, strip % 4
        slot = (strip // 2) % 2 if strip % 2 == 0 else 2
        wb = wd_str[slot]
        qeng = "sync" if strip % 2 == 0 else "scalar"
        if strip == 0:
            B.wait("gpsimd", "fin", ysg_mark[0])
            if sg == 0:
                S_xs.wait_all("gpsimd")
            for tl in range(2):
                tk_yi = S_yi.dma("gpsimd", ysg[:, 2048 * tl:2048 * (tl + 1)],
                                 x2_sp.ap()[256 * sg + 128 * tl:
                                            256 * sg + 128 * (tl + 1), :])
            S_yi.wait("vector", tk_yi)
            B.wait("tensor", "ve", afin_mark[sg % 2])
        B.wait(qeng, "te", wd_mk[slot])
        if sg == 0 and strip < 2:
            S_do.wait_all(qeng)
        tk_wl = S_wl.dma(
            qeng, wb[:, :],
            wd_tern.ap()[ocb, :, 16 * qh:16 * (qh + 1), :], slot)
        S_wl.wait("tensor", tk_wl)
        for tl in range(2):
            if qh == 0:
                B.wait("tensor", "dn", p6dn_marks[tl])
            for icm in range(16):
                i = 16 * qh + icm
                B.ins("tensor", lambda e, i_=i, icm_=icm, tl_=tl, w=wb,
                      o=512 * (tl % 2), x=xqf: e.matmul(
                    PA[1][:, o:o + 256],
                    r3(x, 64, 256)[:, i_, 128 * tl_:128 * (tl_ + 1)],
                    w[:, 256 * icm_:256 * (icm_ + 1)],
                    start=(i_ == 0), stop=False), "te")
            if qh == 3:
                B.ins("tensor", lambda e, o=512 * (tl % 2),
                      c=256 * (sg % 2) + 128 * tl: e.matmul(
                    PA[1][:, o:o + 256], sfrows[0:1, c:c + 128],
                    c_negrow[0:1, 0:256], start=False, stop=True), "te")
                tev = B.val("te")
                B.wait("vector", "te", tev)
                B.ins("vector", lambda e, t=tl, o=ocb, po=512 * (tl % 2),
                      m_=mbase + 8 * tl: e.scalar_tensor_tensor(
                    ysg[:, 2048 * t + 256 * o:2048 * t + 256 * (o + 1)],
                    PA[1][:, po:po + 256], m2st[:, m_ + 2:m_ + 3],
                    ysg[:, 2048 * t + 256 * o:2048 * t + 256 * (o + 1)],
                    ALU.mult, ALU.add), "ve")
                B.ins("vector", lambda e: e.engine_nop(), "dn")
                p6dn_marks[tl] = B.val("dn")
        wd_mk[slot] = B.val("te")
        if strip == 31:
            cmark[sg % 2] = B.val("te")
            ve_y = B.val("ve")
            B.wait("gpsimd", "ve", ve_y)
            for tl in range(2):
                B.dma("gpsimd", y_h[256 * sg + 128 * tl:256 * sg + 128 * (tl + 1), :],
                      ysg[:, 2048 * tl:2048 * (tl + 1)], "fin")
            ysg_mark[0] = B.val("fin")

    emit_AH(0); emit_ATa(0); emit_ATb(0)
    emit_AH(1); emit_ATa(1); emit_ATb(1)
    emit_AH(2)
    for sg in range(4):
        t2, t3 = 2 * sg + 2, 2 * sg + 3
        plan = {}
        if sg < 3:
            bs2, qs2 = bf_steps(t2), quant_steps(t2)
            bs3, qs3 = bf_steps(t3), quant_steps(t3)
            for i in range(6):
                plan[i] = [bs2[i]]
                plan[9 + i] = [qs2[i]]
                plan[16 + i] = [bs3[i]]
                plan[25 + i] = [qs3[i]]
            plan[6] = [abs_step(t2)]
            plan[7] = [tmax_step(t2)]
            plan[8] = [ata_tensor(t2)]
            plan[15] = [lambda: emit_ATb(t2), lambda: emit_AH(t3)]
            plan[22] = [abs_step(t3)]
            plan[23] = [tmax_step(t3)]
            plan[24] = [ata_tensor(t3)]
            plan[31] = [lambda: emit_ATb(t3)]
            if sg < 2:
                plan[31].append(lambda: emit_AH(t2 + 2))
        for strip in range(32):
            emit_Cstrip(sg, strip)
            for f in plan.get(strip, ()):
                f()
    B.wait("sync", "fin", B.val("fin"))

    return _finish(K, tap_list, taps, locals())


def _finish(K, tap_list, taps, loc):
    B = K
    for (tname, lname, shape, dt) in taps:
        h = loc.get(lname)
        if h is None:
            continue
        dst = K.nc.dram_tensor(tname, list(shape), dt, kind="ExternalOutput").ap()
        B.wait("sync", "ve")
        B.wait("sync", "te")
        B.wait("sync", "se")
        B.wait("sync", "st")
        if isinstance(h, list):
            h = h[0]
        B.dma("sync", dst, h[:, 0:shape[1]] if len(shape) == 2 else h[:, :], "fin")
    B.wait("sync", "fin")

    nc = K.nc
    with nc.Block() as block:
        @block.sync
        def _(e):
            for f in K.ops["sync"]:
                f(e)

        @block.tensor
        def _(e):
            for f in K.ops["tensor"]:
                f(e)

        @block.vector
        def _(e):
            for f in K.ops["vector"]:
                f(e)

        @block.scalar
        def _(e):
            for f in K.ops["scalar"]:
                f(e)

        @block.gpsimd
        def _(e):
            for f in K.ops["gpsimd"]:
                f(e)
    for s in reversed(K.open_scopes):
        try:
            s.close()
        except Exception:
            pass
    K.root.close()
    return nc



def _prep_inputs(inputs):
    f32 = lambda a: np.ascontiguousarray(np.asarray(a, dtype=np.float32))
    x = f32(inputs["L0_attn_x"])
    Wq, Wk, Wv, Wo = (f32(inputs["L0_attn_" + n]) for n in ("Wq", "Wk", "Wv", "Wo"))
    qA, qB, kA, kB, vA, vB, oA, oB = (f32(inputs["L0_attn_" + n]) for n in
                                      ("qA", "qB", "kA", "kB", "vA", "vB", "oA", "oB"))
    wg, wu, wd = (f32(inputs["L0_ffn_w_gate"]), f32(inputs["L0_ffn_w_up"]),
                  f32(inputs["L0_ffn_w_down"]))

    permq = _deperm(16)
    permk = _deperm(8)
    WqT = np.ascontiguousarray(Wq.T[:, permq])
    qBT = np.ascontiguousarray(qB.T[:, permq])
    WkT = np.ascontiguousarray(Wk.T[:, permk])
    kBT = np.ascontiguousarray(kB.T[:, permk])
    WvT = np.ascontiguousarray(Wv.T)
    vBT = np.ascontiguousarray(vB.T)
    WoT = np.ascontiguousarray(Wo.T)
    oBT = np.ascontiguousarray(oB.T)
    def tile_gu(wT):
        t = wT.reshape(16, 128, 16, 512)
        return np.ascontiguousarray(t.transpose(2, 1, 0, 3)).reshape(16, 128, 8192)
    wgT = tile_gu(wg.T)
    wuT = tile_gu(wu.T)
    wdTt = wd.T.reshape(64, 128, 8, 256)
    wdT = np.ascontiguousarray(wdTt.transpose(2, 1, 0, 3))

    cosT, sinT = _rope_tables()
    cosT = np.ascontiguousarray(np.concatenate([cosT[:, :S // 2], cosT[:, S // 2:]], axis=0))
    sinT = np.ascontiguousarray(np.concatenate([sinT[:, :S // 2], sinT[:, S // 2:]], axis=0))
    identb = np.eye(128, dtype=np.float32).astype(ml_dtypes.bfloat16)
    identf = np.eye(128, dtype=np.float32)
    had = _hadamard(128).astype(ml_dtypes.bfloat16)
    masks = _diag_masks()

    in_maps = []
    for c in range(8):
        b, g = c // 4, c % 4
        gates = np.zeros((128, 8), np.float32)
        gates[:, 4 * b:4 * b + 4] = 1.0
        in_maps.append({
            "x": x[b],
            "xown": np.ascontiguousarray(x[b, TOK * g:TOK * (g + 1)]),
            "wqT": np.ascontiguousarray(WqT[:, 512 * g:512 * (g + 1)]),
            "wkT": np.ascontiguousarray(WkT[:, 256 * g:256 * (g + 1)]),
            "wvT": np.ascontiguousarray(WvT[:, 256 * g:256 * (g + 1)]),
            "woT": WoT,
            "qA": qA, "kA": kA, "vA": vA, "oA": oA,
            "qBT": np.ascontiguousarray(qBT[:, 512 * g:512 * (g + 1)]),
            "kBT": np.ascontiguousarray(kBT[:, 256 * g:256 * (g + 1)]),
            "vBT": np.ascontiguousarray(vBT[:, 256 * g:256 * (g + 1)]),
            "oBT": oBT,
            "wgT": wgT, "wuT": wuT, "wdT": wdT,
            "wg_s": np.ascontiguousarray(wg.reshape(8, 128, 16384)[c]),
            "wu_s": np.ascontiguousarray(wu.reshape(8, 128, 16384)[c]),
            "wd_s": np.ascontiguousarray(wd.reshape(8, 128, 16384)[c]),
            "identb": identb, "identf": identf,
            "cosT": cosT, "sinT": sinT, "masks": masks, "had": had,
            "a2agate": gates,
        })
    return in_maps


def kernel(**inputs):
    in_maps = _prep_inputs(inputs)
    nc = build_program(taps=list(_DEBUG_TAPS), stop=_DEBUG_STOP[0])
    res = run_bass_kernel_spmd(nc, in_maps, core_ids=list(range(8)))
    kernel._last_results = res
    y = np.zeros((2, S, HID), np.float32)
    for c in range(8):
        b, g = c // 4, c % 4
        y[b, TOK * g:TOK * (g + 1)] = res.results[c]["y"]
    return y
